# version 08
# baseline (speedup 1.0000x reference)
import numpy as np
import ml_dtypes
from contextlib import ExitStack
import concourse.bass as bass
import concourse.mybir as mybir
from concourse.bass_utils import run_bass_kernel_spmd

F32 = mybir.dt.float32
BF16 = mybir.dt.bfloat16
AF = mybir.ActivationFunctionType
ALU = mybir.AluOpType
AX = mybir.AxisListType

T = 2048
D = 1024
H = 16
HE = 64
DFF = 2048
NE = 8
NEXP = 9
NCH = D // 128
NTT = T // 512
NMT = T // 128

ENGS = ("tensor", "vector", "scalar", "sync", "gpsimd")

DBG = {"stop": "out", "nexp": NEXP, "dump": "xres", "nheads": H, "qkv_parts": "all", "evac": "gelu"}
_DBGREF = {}


class _PhaseDone(Exception):
    pass


def _ckpt(ph):
    if DBG["stop"] == ph:
        raise _PhaseDone()


class Plan:
    def __init__(self, sems):
        self.ops = {e: [] for e in ENGS}
        self.cnt = {k: 0 for k in sems}
        self.sems = sems
        self.tail_inc = {e: True for e in ENGS}

    def emit(self, eng, fn, waits=(), inc=None, inc_n=1, neutral=False):
        waits = tuple((s, self.cnt[s] if s in ("SQ", "GQ") else v)
                      for (s, v) in waits)
        waits = tuple((s, v) for (s, v) in waits if v > 0)
        for s, v in waits:
            assert v <= self.cnt[s], f"wait on future count: {s} {v} > {self.cnt[s]}"
        if inc is not None:
            self.cnt[inc] += inc_n
        sems = self.sems

        def run(e, _w=waits, _f=fn, _i=inc, _n=inc_n):
            for s, v in _w:
                e.wait_ge(sems[s], v)
            inst = _f(e)
            if _i is not None:
                inst.then_inc(sems[_i], _n)
        self.ops[eng].append(run)
        if not neutral:
            self.tail_inc[eng] = inc is not None
        return self.cnt[inc] if inc is not None else None


def build_program():
    nc = bass.Bass("TRN2", target_bir_lowering=False)
    dt = nc.dram_tensor
    xt = dt("xt", [D, T], F32, kind="ExternalInput").ap()
    wqkv2 = dt("wqkv2", [128, 192], F32, kind="ExternalInput").ap()
    bqkv_row = dt("bqkv_row", [1, 192], F32, kind="ExternalInput").ap()
    bqkv_col = dt("bqkv_col", [64, 2], F32, kind="ExternalInput").ap()
    wr_rl = dt("wr_rl", [128, NCH * D], F32, kind="ExternalInput").ap()
    br_row = dt("br_row", [1, D], F32, kind="ExternalInput").ap()
    rww_rl = dt("rww_rl", [128, NCH * NE], F32, kind="ExternalInput").ap()
    rwb_row = dt("rwb_row", [1, NE], F32, kind="ExternalInput").ap()
    w1cat = dt("w1cat", [NEXP, 128, NCH * DFF], F32, kind="ExternalInput").ap()
    w2cat = dt("w2cat", [NEXP, 128, (DFF // 128) * D], F32, kind="ExternalInput").ap()
    b1col = dt("b1col", [NEXP, 128, 16], F32, kind="ExternalInput").ap()
    b2row = dt("b2row", [NEXP, 1, D], F32, kind="ExternalInput").ap()
    b2col_t = dt("b2col_t", [NEXP, 128, 8], F32, kind="ExternalInput").ap()
    c_ones = dt("c_ones", [128, 512], BF16, kind="ExternalInput").ap()
    c_idb = dt("c_idb", [128, 128], BF16, kind="ExternalInput").ap()
    c_idf = dt("c_idf", [128, 128], F32, kind="ExternalInput").ap()
    c_masks = dt("c_masks", [128, 4 * 512], F32, kind="ExternalInput").ap()
    c_eps = dt("c_eps", [1, 1], F32, kind="ExternalInput").ap()
    c_rcp1024 = dt("c_rcp1024", [1, 1], F32, kind="ExternalInput").ap()
    ot = dt("ot", [D, T], F32, kind="ExternalOutput").ap()

    with ExitStack() as ctx:
        E = ctx.enter_context
        sems = {
            "ST": E(nc.semaphore("ST")),
            "SV": E(nc.semaphore("SV")),
            "SS": E(nc.semaphore("SS")),
            "SQ": E(nc.semaphore("SQ")),
            "GQ": E(nc.semaphore("GQ")),
            "W1S": E(nc.semaphore("W1S")),
            "W2S": E(nc.semaphore("W2S")),
            "WRS": E(nc.semaphore("WRS")),
            "B1S": E(nc.semaphore("B1S")),
            "B2S_": E(nc.semaphore("B2S_")),
        }
        P = Plan(sems)
        C = P.cnt

        XRES = E(nc.sbuf_tensor("XRES", [128, NCH * T], F32))
        XLN = E(nc.sbuf_tensor("XLN", [128, NCH * T], BF16))
        ONES = E(nc.sbuf_tensor("ONES", [128, 512], BF16))
        WSB = E(nc.sbuf_tensor("WSB", [8, T], BF16))
        WQ = E(nc.sbuf_tensor("WQ", [128, 192], BF16))
        EPS = E(nc.sbuf_tensor("EPS", [1, 1], F32))
        RCP1024 = E(nc.sbuf_tensor("RCP1024", [1, 1], F32))
        DUMMY = E(nc.sbuf_tensor("DUMMY", [1, 8], F32))

        nc.const_aps.register(F32, 1.0 / 1024.0, RCP1024[0:1, 0:1])

        def xres(c):
            return XRES[:, c * T:(c + 1) * T]

        def xln(c):
            return XLN[:, c * T:(c + 1) * T]

        def barrier():
            for eng in ("tensor", "vector", "scalar"):
                assert P.tail_inc[eng], f"barrier: {eng} tail lacks inc"
            w = [(k, C[k]) for k in C]
            for eng in ENGS:
                P.emit(eng, lambda e: e.nop(), waits=w, neutral=True)

        sq_x = []
        for c in range(NCH):
            sq_x.append(P.emit(
                "sync",
                lambda e, c=c: e.dma_start(out=xres(c), in_=xt[c * 128:(c + 1) * 128, :]),
                inc="SQ", inc_n=16))
        sq_ones = P.emit("sync", lambda e: e.dma_start(out=ONES[:], in_=c_ones),
                         inc="SQ", inc_n=16)
        sq_eps = P.emit("sync", lambda e: e.dma_start(out=EPS[:], in_=c_eps),
                        inc="SQ", inc_n=16)
        sq_rcp = P.emit("sync", lambda e: e.dma_start(out=RCP1024[:], in_=c_rcp1024),
                        inc="SQ", inc_n=16)
        gq_wq = P.emit("gpsimd", lambda e: e.dma_start(out=WQ[:], in_=wqkv2),
                       inc="GQ", inc_n=16)
        P.emit("scalar", lambda e: e.memzero(DUMMY[:]), inc="SS")

        _ln_n = [0]

        def emit_ln(x_waits):
            _ln_n[0] += 1
            sfx = str(_ln_n[0])
            with ExitStack() as lctx:
                LE = lctx.enter_context
                XSQ = LE(nc.sbuf_tensor("XSQ" + sfx, [128, T], BF16))
                MROW = LE(nc.sbuf_tensor("MROW" + sfx, [1, T], BF16))
                VROW = LE(nc.sbuf_tensor("VROW" + sfx, [1, T], BF16))
                LROW = LE(nc.sbuf_tensor("LROW" + sfx, [1, T], BF16))
                RSTD = LE(nc.sbuf_tensor("RSTD" + sfx, [1, T], BF16))
                MRSTD = LE(nc.sbuf_tensor("MRSTD" + sfx, [1, T], BF16))
                TMP = LE(nc.sbuf_tensor("TMPLN" + sfx, [128, 1024], F32))
                with ExitStack() as pctx:
                    PSM = pctx.enter_context(nc.psum_tensor("PSM" + sfx, [1, T], F32))
                    PSQ = pctx.enter_context(nc.psum_tensor("PSQ" + sfx, [1, T], F32))
                    sq_done_prev = 0
                    for c in range(NCH):
                        vc = P.emit("vector",
                                    lambda e, c=c: e.tensor_copy(xln(c), xres(c)),
                                    waits=list(x_waits[c]), inc="SV")
                        vs = P.emit("vector",
                                    lambda e, c=c: e.tensor_mul(XSQ[:], xln(c), xln(c)),
                                    waits=[("SV", vc), ("ST", sq_done_prev)],
                                    inc="SV")
                        for tt in range(NTT):
                            P.emit("tensor",
                                   lambda e, c=c, tt=tt: e.matmul(
                                       PSM[0:1, tt * 512:(tt + 1) * 512],
                                       ONES[:, 0:1],
                                       xln(c)[:, tt * 512:(tt + 1) * 512],
                                       start=(c == 0), stop=(c == NCH - 1)),
                                   waits=([("SV", vc), ("SQ", sq_ones)] if tt == 0 else []))
                        for tt in range(NTT):
                            r = P.emit("tensor",
                                   lambda e, c=c, tt=tt: e.matmul(
                                       PSQ[0:1, tt * 512:(tt + 1) * 512],
                                       ONES[:, 0:1],
                                       XSQ[:, tt * 512:(tt + 1) * 512],
                                       start=(c == 0), stop=(c == NCH - 1)),
                                   waits=([("SV", vs)] if tt == 0 else []),
                                   inc=("ST" if tt == NTT - 1 else None))
                            if r is not None:
                                sq_done_prev = r
                    st_stats = sq_done_prev
                    v1 = P.emit("vector",
                                lambda e: e.tensor_scalar(MROW[:], PSM[:], 1.0 / 1024.0,
                                                          None, ALU.mult),
                                waits=[("ST", st_stats), ("SQ", sq_rcp)], inc="SV")
                    s1 = P.emit("scalar",
                                lambda e: e.activation(VROW[:], MROW[:], AF.Square),
                                waits=[("SV", v1)], inc="SS")
                    v2 = P.emit("vector",
                                lambda e: e.scalar_tensor_tensor(
                                    VROW[:], PSQ[:], 1.0 / 1024.0, VROW[:],
                                    ALU.mult, ALU.subtract),
                                waits=[("SS", s1)], inc="SV")
                    s2 = P.emit("scalar",
                                lambda e: e.activation(LROW[:], VROW[:], AF.Ln,
                                                       bias=EPS[0:1, 0:1]),
                                waits=[("SV", v2), ("SQ", sq_eps)], inc="SS")
                    s3 = P.emit("scalar",
                                lambda e: e.activation(RSTD[:], LROW[:], AF.Exp,
                                                       scale=-0.5),
                                waits=[("SS", s2)], inc="SS")
                    v3 = P.emit("vector",
                                lambda e: e.tensor_mul(MRSTD[:], MROW[:], RSTD[:]),
                                waits=[("SS", s3)], inc="SV")
                with ExitStack() as pctx:
                    PB1 = pctx.enter_context(nc.psum_tensor("PB1" + sfx, [128, T], F32))
                    PB2 = pctx.enter_context(nc.psum_tensor("PB2" + sfx, [128, T], F32))
                    tb = []
                    for tt in range(NTT):
                        P.emit("tensor",
                               lambda e, tt=tt: e.matmul(
                                   PB1[:, tt * 512:(tt + 1) * 512],
                                   ONES[0:1, 0:128],
                                   RSTD[0:1, tt * 512:(tt + 1) * 512],
                                   start=True, stop=True),
                               waits=([("SS", s3), ("SV", v2)] if tt == 0 else []))
                        tb.append(P.emit("tensor",
                               lambda e, tt=tt: e.matmul(
                                   PB2[:, tt * 512:(tt + 1) * 512],
                                   ONES[0:1, 0:128],
                                   MRSTD[0:1, tt * 512:(tt + 1) * 512],
                                   start=True, stop=True),
                               waits=([("SV", v3)] if tt == 0 else []), inc="ST"))
                    applied = 0
                    for c in range(NCH):
                        for hf in range(2):
                            sl = slice(hf * 1024, (hf + 1) * 1024)
                            va = P.emit("vector",
                                        lambda e, c=c, sl=sl: e.tensor_mul(
                                            TMP[:], xres(c)[:, sl], PB1[:, sl]),
                                        waits=[("ST", tb[2 * hf + 1]),
                                               ("SV", applied)], inc="SV")
                            applied = P.emit("vector",
                                        lambda e, c=c, sl=sl: e.tensor_sub(
                                            xln(c)[:, sl], TMP[:], PB2[:, sl]),
                                        waits=[("SV", va)], inc="SV")
                    return applied

        try:
            ln1_done = emit_ln([[("SQ", sq_x[c])] for c in range(NCH)])
            _ckpt("ln1")

            barrier()
            _ckpt("bar1")

            with ExitStack() as actx:
                AE = actx.enter_context
                QT = AE(nc.sbuf_tensor("QT", [64, T], BF16))
                KT = AE(nc.sbuf_tensor("KT", [64, T], BF16))
                VA = AE(nc.sbuf_tensor("VA", [128, 16 * 65], BF16))
                PT = AE(nc.sbuf_tensor("PT", [128, 16 * 512], BF16))
                AVT = AE(nc.sbuf_tensor("AVT", [128, NCH * T], BF16))
                WRF = AE(nc.sbuf_tensor("WRF", [128, NCH * D], BF16))
                MASKS = AE(nc.sbuf_tensor("MASKS", [128, 4 * 512], F32))
                SMASK = AE(nc.sbuf_tensor("SMASK", [128, 2 * 512], F32))
                AVN = AE(nc.sbuf_tensor("AVN", [128, 4 * 64], BF16))
                RZ = AE(nc.sbuf_tensor("RZ", [128, 4], F32))
                IDB = AE(nc.sbuf_tensor("IDB", [128, 128], BF16))
                BQKVC = AE(nc.sbuf_tensor("BQKVC", [64, 2], F32))
                BQKVR = AE(nc.sbuf_tensor("BQKVR", [1, 192], BF16))
                BRR = AE(nc.sbuf_tensor("BRR", [1, D], BF16))
                PS4 = AE(nc.psum_tensor("PS4", [128, 2048], F32))
                PS2 = AE(nc.psum_tensor("PS2", [128, 1024], F32))
                PS1 = AE(nc.psum_tensor("PS1", [128, 260], F32))
                PTR = AE(nc.psum_tensor("PTR", [128, 256], BF16))

                sq_masks = P.emit("sync", lambda e: e.dma_start(out=MASKS[:], in_=c_masks),
                                  inc="SQ", inc_n=16)
                sq_idb = P.emit("sync", lambda e: e.dma_start(out=IDB[:], in_=c_idb),
                                inc="SQ", inc_n=16)
                gq_wrf = P.emit("gpsimd", lambda e: e.dma_start(out=WRF[:], in_=wr_rl),
                                inc="GQ", inc_n=16)
                gq_bqc = P.emit("gpsimd", lambda e: e.dma_start(out=BQKVC[:], in_=bqkv_col),
                                inc="GQ", inc_n=16)
                gq_bqr = P.emit("gpsimd", lambda e: e.dma_start(out=BQKVR[:], in_=bqkv_row),
                                inc="GQ", inc_n=16)
                gq_brr = P.emit("gpsimd", lambda e: e.dma_start(out=BRR[:], in_=br_row),
                                inc="GQ", inc_n=16)

                sv_vones = P.emit("vector",
                                  lambda e: e.memset(
                                      VA.ap().rearrange("p (g n) -> p g n", n=65)[:, :, 64:65],
                                      1.0),
                                  inc="SV")

                if DBG["stop"] == "adma":
                    gg = P.emit("gpsimd", lambda e: e.nop(),
                                waits=[("GQ", C["GQ"]), ("SQ", C["SQ"])])
                    _DBGREF["QT"], _DBGREF["KT"], _DBGREF["VA"] = QT, KT, VA
                    _ckpt("adma")
                slot_free = {r: ("ST", 0) for r in range(6)}
                ps2_free = ("SV", 0)
                ps1_free = ("SV", 0)
                ptr_free = {0: ("SV", 0), 1: ("SV", 0)}
                avn_free = {m: ("ST", 0) for m in range(4)}
                ss_exp_last = 0
                sc_hist = []
                st_av_prev = 0
                avt_done = 0

                for h in range(min(H, DBG["nheads"])):
                    ch, off = h // 2, (h % 2) * 64
                    xh = XLN[off:off + 64, ch * T:(ch + 1) * T]

                    ev_q, ev_k = [], []
                    for s in (range(NTT) if DBG["qkv_parts"] in ("all", "qk") else []):
                        tsl = slice(s * 512, (s + 1) * 512)
                        for which in ("q", "k"):
                            base = 0 if which == "q" else 1024
                            slot_id = (0 if which == "q" else 2) + s % 2
                            psv = PS4[0:64, base + (s % 2) * 512: base + (s % 2) * 512 + 512]
                            wcol = 0 if which == "q" else 64
                            evs = ev_q if which == "q" else ev_k
                            if s >= 2:
                                w = [("SS", evs[s - 2])]
                            else:
                                w = [slot_free[slot_id]]
                            if s == 0 and which == "q":
                                w += [("SV", ln1_done), ("GQ", gq_wq)]
                            tq = P.emit("tensor",
                                        lambda e, psv=psv, wcol=wcol, tsl=tsl, xh=xh,
                                        off=off: e.matmul(
                                            psv, WQ[off:off + 64, wcol:wcol + 64],
                                            xh[:, tsl],
                                            start=True, stop=True),
                                        waits=w, inc="ST")
                            dst = (QT if which == "q" else KT)[0:64, tsl]
                            bias = BQKVC[0:64, 0:1] if which == "q" else BQKVC[0:64, 1:2]
                            evs.append(P.emit("scalar",
                                        lambda e, dst=dst, psv=psv, bias=bias: e.activation(
                                            dst, psv, AF.Identity, bias=bias),
                                        waits=[("ST", tq), ("GQ", gq_bqc)], inc="SS"))
                    tv_last = 0
                    for mt in (range(NMT) if DBG["qkv_parts"] in ("all", "v") else []):
                        psl = PS2[:, mt * 64:(mt + 1) * 64]
                        w = ([ps2_free, slot_free[4], slot_free[5],
                              ("SV", ln1_done)] if mt == 0 else [])
                        P.emit("tensor",
                               lambda e, psl=psl, mt=mt, ch=ch, off=off: e.matmul(
                                   psl,
                                   XLN[off:off + 64,
                                       ch * T + mt * 128: ch * T + (mt + 1) * 128],
                                   WQ[off:off + 64, 128:192],
                                   start=(mt in (0, 8)), stop=False,
                                   skip_group_check=True),
                               waits=w)
                        tv_last = P.emit("tensor",
                               lambda e, psl=psl: e.matmul(
                                   psl, ONES[0:1, 0:128], BQKVR[0:1, 128:192],
                                   start=False, stop=True, skip_group_check=True),
                               waits=([("GQ", gq_bqr), ("SQ", sq_ones)] if mt == 0 else []),
                               inc=("ST" if mt == NMT - 1 else None)) or tv_last
                    if DBG["qkv_parts"] in ("all", "v"):
                        sv_va = P.emit("vector",
                                   lambda e: e.tensor_copy(
                                       VA.ap().rearrange("p (g n) -> p g n", n=65)[:, :, 0:64],
                                       PS2.ap().rearrange("p (g n) -> p g n", n=64)),
                                   waits=[("ST", tv_last), ("SV", sv_vones)], inc="SV")
                    else:
                        sv_va = sv_vones
                    ps2_free = ("SV", max(sv_va, avt_done))
                    if h == 0:
                        _DBGREF["QT"], _DBGREF["KT"], _DBGREF["VA"] = QT, KT, VA
                        _ckpt("qkv")

                    rot = 0
                    for j in range(NTT):
                        if rot % 2:
                            rot += 1
                        for kc0 in range(0, 4 * j, 2):
                            slot = rot % 6
                            rot += 2
                            base = (PS4 if slot < 4 else PS2)
                            boff = (slot if slot < 4 else slot - 4) * 512
                            ts = 0
                            for q in range(2):
                                kc = kc0 + q
                                psv = base[:, boff + q * 512: boff + (q + 1) * 512]
                                w = [slot_free[slot + q]]
                                if len(sc_hist) >= 2:
                                    w.append(sc_hist[-2])
                                if kc == 0 and j == 0:
                                    w += [("SS", ev_q[-1]), ("SS", ev_k[-1])]
                                ts = P.emit("tensor",
                                        lambda e, psv=psv, kc=kc, j=j: e.matmul(
                                            psv,
                                            KT[0:64, kc * 128:(kc + 1) * 128],
                                            QT[0:64, j * 512:(j + 1) * 512],
                                            start=True, stop=True),
                                        waits=w, inc="ST") or ts
                            ptguard = [("ST", st_av_prev)] if kc0 == 0 else []
                            ss_exp_last = P.emit("scalar",
                                    lambda e, base=base, boff=boff, kc0=kc0: e.activation(
                                        PT[:, kc0 * 512:(kc0 + 2) * 512],
                                        base[:, boff:boff + 1024],
                                        AF.Exp, scale=0.125),
                                    waits=[("ST", ts)] + ptguard, inc="SS")
                            slot_free[slot] = ("SS", ss_exp_last)
                            slot_free[slot + 1] = ("SS", ss_exp_last)
                            sc_hist.append(("SS", ss_exp_last))
                            sc_hist.append(("SS", ss_exp_last))
                        for kc0 in range(4 * j, 4 * j + 4, 2):
                            slot = rot % 6
                            rot += 2
                            base = (PS4 if slot < 4 else PS2)
                            boff = (slot if slot < 4 else slot - 4) * 512
                            vm = 0
                            for q in range(2):
                                kc = kc0 + q
                                psv = base[:, boff + q * 512: boff + (q + 1) * 512]
                                w = [slot_free[slot + q]]
                                if len(sc_hist) >= 2:
                                    w.append(sc_hist[-2])
                                if kc == 0 and j == 0:
                                    w += [("SS", ev_q[-1]), ("SS", ev_k[-1])]
                                ts = P.emit("tensor",
                                        lambda e, psv=psv, kc=kc, j=j: e.matmul(
                                            psv,
                                            KT[0:64, kc * 128:(kc + 1) * 128],
                                            QT[0:64, j * 512:(j + 1) * 512],
                                            start=True, stop=True),
                                        waits=w, inc="ST")
                                d = kc - 4 * j
                                vm = P.emit("vector",
                                        lambda e, psv=psv, d=d, q=q: e.tensor_mul(
                                            SMASK[:, q * 512:(q + 1) * 512],
                                            psv, MASKS[:, d * 512:(d + 1) * 512]),
                                        waits=[("ST", ts), ("SQ", sq_masks),
                                               ("SS", ss_exp_last)], inc="SV")
                                slot_free[slot + q] = ("SV", vm)
                                sc_hist.append(("SV", vm))
                            ptguard = [("ST", st_av_prev)] if kc0 == 0 else []
                            ss_exp_last = P.emit("scalar",
                                    lambda e, kc0=kc0: e.activation(
                                        PT[:, kc0 * 512:(kc0 + 2) * 512],
                                        SMASK[:, 0:1024],
                                        AF.Exp, scale=0.125),
                                    waits=[("SV", vm)] + ptguard, inc="SS")
                        st_av = 0
                        for mt in range(4):
                            for kc in range(16):
                                if kc < 4 * j + 4:
                                    lhs = PT[:, kc * 512 + mt * 128: kc * 512 + (mt + 1) * 128]
                                else:
                                    lhs = ONES[:, 0:128]
                                w = []
                                if mt == 0 and kc == 0:
                                    w = [("SS", ss_exp_last), ("SV", sv_va), ps1_free]
                                st_av = P.emit("tensor",
                                        lambda e, lhs=lhs, mt=mt, kc=kc: e.matmul(
                                            PS1[:, mt * 65:(mt + 1) * 65],
                                            lhs, VA[:, kc * 65:(kc + 1) * 65],
                                            start=(mt == 0 and kc == 0), stop=(kc == 15),
                                            skip_group_check=True),
                                        waits=w,
                                        inc=("ST" if (mt == 3 and kc == 15) else None)) or st_av
                        st_av_prev = st_av
                        sv_norm_last = 0
                        for mt in range(4):
                            rz_w = [("ST", st_av)] if mt == 0 else [("SV", sv_norm_last)]
                            rz = P.emit("vector",
                                    lambda e, mt=mt: e.reciprocal(
                                        RZ[:, mt:mt + 1],
                                        PS1[:, mt * 65 + 64: mt * 65 + 65]),
                                    waits=rz_w, inc="SV")
                            vn = P.emit("vector",
                                    lambda e, mt=mt: e.tensor_scalar(
                                        AVN[:, mt * 64:(mt + 1) * 64],
                                        PS1[:, mt * 65: mt * 65 + 64],
                                        RZ[:, mt:mt + 1], None, ALU.mult),
                                    waits=[("SV", rz), avn_free[mt]], inc="SV")
                            sv_norm_last = vn
                            tp = P.emit("tensor",
                                    lambda e, mt=mt, off=off: e.transpose(
                                        PTR[off:off + 64, (mt % 2) * 128:(mt % 2) * 128 + 128],
                                        AVN[:, mt * 64:(mt + 1) * 64],
                                        IDB[:],
                                        tile_position=(0, off)),
                                    waits=[("SV", vn), ("SQ", sq_idb), ptr_free[mt % 2]],
                                    inc="ST")
                            avn_free[mt] = ("ST", tp)
                            dst = AVT[off:off + 64,
                                      ch * T + j * 512 + mt * 128:
                                      ch * T + j * 512 + (mt + 1) * 128]
                            cp = P.emit("vector",
                                    lambda e, dst=dst, mt=mt, off=off: e.tensor_copy(
                                        dst,
                                        PTR[off:off + 64, (mt % 2) * 128:(mt % 2) * 128 + 128]),
                                    waits=[("ST", tp)], inc="SV")
                            ptr_free[mt % 2] = ("SV", cp)
                            avt_done = cp
                        ps1_free = ("SV", avt_done)
                        if h == 0 and j == 0:
                            _DBGREF["PT"] = PT
                            _DBGREF["AVN"] = AVN
                            _ckpt("j0")

                _DBGREF["AVT"] = AVT
                _ckpt("attnav")

                rot = 0
                vacc_hist = []
                for dtl in range(NCH):
                    for tt in range(NTT):
                        slot = rot % 4
                        rot += 1
                        psv = PS4[:, slot * 512:(slot + 1) * 512]
                        for kc in range(NCH):
                            w = []
                            if kc == 0:
                                w = [slot_free[slot], ("SV", avt_done), ("GQ", gq_wrf)]
                                if len(sc_hist) >= 2:
                                    w.append(sc_hist[-2])
                                if len(vacc_hist) >= 2:
                                    w.append(("SV", vacc_hist[-2]))
                            P.emit("tensor",
                                   lambda e, psv=psv, kc=kc, dtl=dtl, tt=tt: e.matmul(
                                       psv,
                                       WRF[:, kc * D + dtl * 128: kc * D + (dtl + 1) * 128],
                                       AVT[:, kc * T + tt * 512: kc * T + (tt + 1) * 512],
                                       start=(kc == 0), stop=False),
                                   waits=w)
                        twr = P.emit("tensor",
                               lambda e, psv=psv, dtl=dtl: e.matmul(
                                   psv, BRR[0:1, dtl * 128:(dtl + 1) * 128],
                                   ONES[0:1, 0:512], start=False, stop=True),
                               waits=[("GQ", gq_brr)], inc="ST")
                        vacc = P.emit("vector",
                               lambda e, psv=psv, dtl=dtl, tt=tt: e.tensor_add(
                                   xres(dtl)[:, tt * 512:(tt + 1) * 512],
                                   xres(dtl)[:, tt * 512:(tt + 1) * 512], psv),
                               waits=[("ST", twr)], inc="SV")
                        slot_free[slot] = ("SV", vacc)
                        vacc_hist.append(vacc)

            barrier()
            _ckpt("attn")

            with ExitStack() as fctx:
                FE = fctx.enter_context
                W1 = FE(nc.sbuf_tensor("W1", [128, NCH * DFF], BF16))
                W2 = FE(nc.sbuf_tensor("W2", [128, 16 * D], BF16))
                WB = FE(nc.sbuf_tensor("WB", [128, T], BF16))
                WROW = FE(nc.sbuf_tensor("WROW", [1, T], BF16))
                B2C = FE(nc.sbuf_tensor("B2C", [128, 16], F32))
                B1C = FE(nc.sbuf_tensor("B1C", [128, 32], F32))
                RWW = FE(nc.sbuf_tensor("RWW", [128, NCH * NE], BF16))
                RBR = FE(nc.sbuf_tensor("RBR", [1, NE], BF16))

                gq_rww = P.emit("gpsimd", lambda e: e.dma_start(out=RWW[:], in_=rww_rl),
                                inc="GQ", inc_n=16)
                gq_rbr = P.emit("gpsimd", lambda e: e.dma_start(out=RBR[:], in_=rwb_row),
                                inc="GQ", inc_n=16)
                gq_w1 = [P.emit("gpsimd", lambda e: e.dma_start(out=W1[:], in_=w1cat[0]),
                                inc="W1S", inc_n=16)]
                gq_w2 = [P.emit("gpsimd", lambda e: e.dma_start(out=W2[:], in_=w2cat[0]),
                                inc="W2S", inc_n=16)]
                gq_b1 = [P.emit("gpsimd",
                                lambda e: e.dma_start(out=B1C[:, 0:16], in_=b1col[0]),
                                inc="B1S", inc_n=16)]
                gq_b2 = [P.emit("gpsimd",
                                lambda e: e.dma_start(out=B2C[:, 0:8], in_=b2col_t[0]),
                                inc="B2S_", inc_n=16)]

                ln2_done = emit_ln([[] for _ in range(NCH)])
                _ckpt("ln2")

                with ExitStack() as rctx:
                    RE_ = rctx.enter_context
                    RLOG = RE_(nc.sbuf_tensor("RLOG", [8, 512], F32))
                    RT = RE_(nc.sbuf_tensor("RT", [128, 128], F32))
                    REX = RE_(nc.sbuf_tensor("REX", [128, 128], F32))
                    RS4 = RE_(nc.sbuf_tensor("RS4", [128, 64], F32))
                    RS2 = RE_(nc.sbuf_tensor("RS2", [128, 32], F32))
                    RS1 = RE_(nc.sbuf_tensor("RS1", [128, 16], F32))
                    RCPS = RE_(nc.sbuf_tensor("RCPS", [128, 16], F32))
                    RCPR = RE_(nc.sbuf_tensor("RCPR", [128, 128], F32))
                    MRA = RE_(nc.sbuf_tensor("MRA", [128, 128], F32))
                    PMA = RE_(nc.sbuf_tensor("PMA", [128, 128], F32))
                    M2R = RE_(nc.sbuf_tensor("M2R", [128, 128], F32))
                    MSKA = RE_(nc.sbuf_tensor("MSKA", [128, 128], F32))
                    WTA = RE_(nc.sbuf_tensor("WTA", [128, 128], F32))
                    IDF = RE_(nc.sbuf_tensor("IDF", [128, 128], F32))
                    PR = RE_(nc.psum_tensor("PR", [8, 512], F32))
                    PTRK = RE_(nc.psum_tensor("PTRK", [128, 8], F32))
                    PTW = RE_(nc.psum_tensor("PTW", [8, 128], F32))

                    sq_idf = P.emit("sync", lambda e: e.dma_start(out=IDF[:], in_=c_idf),
                                    waits=[("SV", ln2_done), ("ST", C["ST"]),
                                           ("SS", C["SS"])],
                                    inc="SQ", inc_n=16)
                    rt_last = 0
                    pr_free = ("SV", 0)
                    ptrk_free = ("SV", 0)
                    for tt in range(NTT):
                        for kc in range(NCH):
                            w = []
                            if kc == 0:
                                w = [("GQ", gq_rww), ("SV", ln2_done), pr_free]
                            P.emit("tensor",
                                   lambda e, kc=kc, tt=tt: e.matmul(
                                       PR[:], RWW[:, kc * NE:(kc + 1) * NE],
                                       xln(kc)[:, tt * 512:(tt + 1) * 512],
                                       start=(kc == 0), stop=False),
                                   waits=w)
                        tpr = P.emit("tensor",
                               lambda e: e.matmul(PR[:], RBR[0:1, 0:NE], ONES[0:1, 0:512],
                                                  start=False, stop=True),
                               waits=[("GQ", gq_rbr)], inc="ST")
                        vlog = P.emit("vector",
                               lambda e: e.tensor_copy(RLOG[:], PR[:]),
                               waits=[("ST", tpr)], inc="SV")
                        pr_free = ("SV", vlog)
                        for b in range(4):
                            tk = P.emit("tensor",
                                   lambda e, b=b: e.transpose(
                                       PTRK[:], RLOG[:, b * 128:(b + 1) * 128],
                                       IDF[0:8, 0:8]),
                                   waits=[("SV", vlog), ("SQ", sq_idf), ptrk_free],
                                   inc="ST")
                            g = tt * 4 + b
                            rt_last = P.emit("vector",
                                   lambda e, g=g: e.tensor_copy(
                                       RT[:, g * 8:(g + 1) * 8], PTRK[:]),
                                   waits=[("ST", tk)], inc="SV")
                            ptrk_free = ("SV", rt_last)

                    def r3(t_, n=8):
                        return t_.ap().rearrange("p (g n) -> p g n", n=n)

                    sx = P.emit("scalar",
                                lambda e: e.activation(REX[:], RT[:], AF.Exp),
                                waits=[("SV", rt_last)], inc="SS")
                    c0 = P.emit("vector", lambda e: e.tensor_add(
                            r3(RS4, 4), r3(REX)[:, :, 0:4], r3(REX)[:, :, 4:8]),
                            waits=[("SS", sx)], inc="SV")
                    c1 = P.emit("vector", lambda e: e.tensor_add(
                            r3(RS2, 2), r3(RS4, 4)[:, :, 0:2], r3(RS4, 4)[:, :, 2:4]),
                            waits=[("SV", c0)], inc="SV")
                    c2 = P.emit("vector", lambda e: e.tensor_add(
                            r3(RS1, 1), r3(RS2, 2)[:, :, 0:1], r3(RS2, 2)[:, :, 1:2]),
                            waits=[("SV", c1)], inc="SV")
                    c3 = P.emit("vector", lambda e: e.reciprocal(RCPS[:], RS1[:]),
                                waits=[("SV", c2)], inc="SV")
                    c4 = P.emit("vector", lambda e: e.tensor_copy(
                            r3(RCPR)[:, :, 0:1], r3(RCPS, 1)),
                            waits=[("SV", c3)], inc="SV")
                    c5 = P.emit("vector", lambda e: e.tensor_copy(
                            r3(RCPR)[:, :, 1:2], r3(RCPR)[:, :, 0:1]),
                            waits=[("SV", c4)], inc="SV")
                    c6 = P.emit("vector", lambda e: e.tensor_copy(
                            r3(RCPR)[:, :, 2:4], r3(RCPR)[:, :, 0:2]),
                            waits=[("SV", c5)], inc="SV")
                    c7 = P.emit("vector", lambda e: e.tensor_copy(
                            r3(RCPR)[:, :, 4:8], r3(RCPR)[:, :, 0:4]),
                            waits=[("SV", c6)], inc="SV")
                    m0 = P.emit("vector", lambda e: e.tensor_max(
                            r3(RS4, 4), r3(REX)[:, :, 0:4], r3(REX)[:, :, 4:8]),
                            waits=[("SV", c1)], inc="SV")
                    m1 = P.emit("vector", lambda e: e.tensor_max(
                            r3(RS2, 2), r3(RS4, 4)[:, :, 0:2], r3(RS4, 4)[:, :, 2:4]),
                            waits=[("SV", m0)], inc="SV")
                    m2 = P.emit("vector", lambda e: e.tensor_max(
                            r3(RS1, 1), r3(RS2, 2)[:, :, 0:1], r3(RS2, 2)[:, :, 1:2]),
                            waits=[("SV", m1), ("SV", c3)], inc="SV")
                    mfill = P.emit("vector", lambda e: e.memset(MRA[:], -1e30), inc="SV")
                    mput = P.emit("vector", lambda e: e.tensor_copy(
                            r3(MRA)[:, :, 0:1], r3(RS1, 1)),
                            waits=[("SV", m2), ("SV", mfill)], inc="SV")
                    mr_last = mput
                    for g in range(16):
                        mr_last = P.emit("vector",
                                lambda e, g=g: e.match_replace(
                                    PMA[:, g * 8:(g + 1) * 8], MRA[:, g * 8:(g + 1) * 8],
                                    REX[:, g * 8:(g + 1) * 8], -1e30),
                                waits=([("SV", mput)] if g == 0 else []), inc="SV")
                    n0 = P.emit("vector", lambda e: e.tensor_max(
                            r3(RS4, 4), r3(PMA)[:, :, 0:4], r3(PMA)[:, :, 4:8]),
                            waits=[("SV", mr_last)], inc="SV")
                    n1 = P.emit("vector", lambda e: e.tensor_max(
                            r3(RS2, 2), r3(RS4, 4)[:, :, 0:2], r3(RS4, 4)[:, :, 2:4]),
                            waits=[("SV", n0)], inc="SV")
                    n2 = P.emit("vector", lambda e: e.tensor_max(
                            r3(RS1, 1), r3(RS2, 2)[:, :, 0:1], r3(RS2, 2)[:, :, 1:2]),
                            waits=[("SV", n1)], inc="SV")
                    d0 = P.emit("vector", lambda e: e.tensor_copy(
                            r3(M2R)[:, :, 0:1], r3(RS1, 1)),
                            waits=[("SV", n2)], inc="SV")
                    d1 = P.emit("vector", lambda e: e.tensor_copy(
                            r3(M2R)[:, :, 1:2], r3(M2R)[:, :, 0:1]),
                            waits=[("SV", d0)], inc="SV")
                    d2 = P.emit("vector", lambda e: e.tensor_copy(
                            r3(M2R)[:, :, 2:4], r3(M2R)[:, :, 0:2]),
                            waits=[("SV", d1)], inc="SV")
                    d3 = P.emit("vector", lambda e: e.tensor_copy(
                            r3(M2R)[:, :, 4:8], r3(M2R)[:, :, 0:4]),
                            waits=[("SV", d2)], inc="SV")
                    e0 = P.emit("vector", lambda e: e.tensor_tensor(
                            MSKA[:], REX[:], M2R[:], ALU.is_ge),
                            waits=[("SV", d3)], inc="SV")
                    e1 = P.emit("vector", lambda e: e.tensor_mul(WTA[:], REX[:], MSKA[:]),
                            waits=[("SV", e0)], inc="SV")
                    e2 = P.emit("vector", lambda e: e.tensor_mul(WTA[:], WTA[:], RCPR[:]),
                            waits=[("SV", e1), ("SV", c7)], inc="SV")
                    ptw_free = ("SV", 0)
                    wsb_done = 0
                    for g in range(16):
                        tw = P.emit("tensor",
                                lambda e, g=g: e.transpose(
                                    PTW[:], WTA[:, g * 8:(g + 1) * 8], IDF[:]),
                                waits=[("SV", e2), ptw_free], inc="ST")
                        wsb_done = P.emit("vector",
                                lambda e, g=g: e.tensor_copy(
                                    WSB[:, g * 128:(g + 1) * 128], PTW[:]),
                                waits=[("ST", tw)], inc="SV")
                        ptw_free = ("SV", wsb_done)
                    router_done_sv = wsb_done
                    router_done_st = C["ST"]
                    _ckpt("router")

                H1G = FE(nc.sbuf_tensor("H1G", [128, 2 * 16 * 512], BF16))
                PM1 = FE(nc.psum_tensor("PM1", [128, 2048], F32))
                PM2 = FE(nc.psum_tensor("PM2", [128, 2048], F32))

                sg_hist = []
                vx_hist = []
                pm1_free = {r: ("SV", 0) for r in range(4)}
                pm2_free = {r: ("SV", 0) for r in range(4)}
                h1g_free = {0: ("ST", 0), 1: ("ST", 0)}
                wrow_free = ("ST", 0)
                wb_free = ("SV", 0)
                w1_free = ("ST", 0)
                w2_free = ("ST", 0)
                b1_free = {0: ("SS", 0), 1: ("SS", 0)}
                b2_free = ("ST", 0)
                state = {"rot1": 0, "rot2": 0, "xadd": 0}

                def emit_mm1(e_i, tt, wb_ready, mm1_first_w):
                    buf = tt % 2
                    last_marker = None
                    for ht in range(16):
                        slot = state["rot1"] % 4
                        state["rot1"] += 1
                        psv = PM1[:, slot * 512:(slot + 1) * 512]
                        tmm = 0
                        for kc in range(NCH):
                            w = []
                            if kc == 0:
                                w = [pm1_free[slot]]
                                if len(sg_hist) >= 2:
                                    w.append(("SS", sg_hist[-2]))
                                if ht == 0:
                                    w = w + mm1_first_w
                            r = P.emit("tensor",
                                   lambda e, psv=psv, kc=kc, ht=ht, tt=tt: e.matmul(
                                       psv,
                                       W1[:, kc * DFF + ht * 128: kc * DFF + (ht + 1) * 128],
                                       xln(kc)[:, tt * 512:(tt + 1) * 512],
                                       start=(kc == 0), stop=(kc == NCH - 1)),
                                   waits=w,
                                   inc=("ST" if kc == NCH - 1 else None))
                            if r is not None:
                                tmm = r
                        h1sl = H1G[:, buf * 8192 + ht * 512: buf * 8192 + (ht + 1) * 512]
                        gw = [("ST", tmm), ("B1S", gq_b1[e_i])]
                        if ht == 0:
                            gw.append(h1g_free[buf])
                        if e_i == 0 and tt == 0 and ht == 0:
                            gw += [("SV", router_done_sv), ("ST", router_done_st)]
                        if DBG["evac"] == "gelu":
                            sg = P.emit("scalar",
                                   lambda e, h1sl=h1sl, psv=psv, e_i=e_i, ht=ht: e.activation(
                                       h1sl, psv, AF.Gelu,
                                       bias=B1C[:, (e_i % 2) * 16 + ht:
                                                (e_i % 2) * 16 + ht + 1]),
                                   waits=gw, inc="SS")
                        elif DBG["evac"] == "gelunb":
                            sg = P.emit("scalar",
                                   lambda e, h1sl=h1sl, psv=psv: e.activation(
                                       h1sl, psv, AF.Gelu),
                                   waits=gw, inc="SS")
                        elif DBG["evac"].startswith("gelun_"):
                            lim = int(DBG["evac"].split("_")[1])
                            if e_i == 0 and (tt * 16 + ht) < lim:
                                sg = P.emit("scalar",
                                       lambda e, h1sl=h1sl, psv=psv: e.activation(
                                           h1sl, psv, AF.Gelu),
                                       waits=gw, inc="SS")
                            else:
                                sg = P.emit("scalar",
                                       lambda e, h1sl=h1sl, psv=psv: e.copy(h1sl, psv),
                                       waits=gw, inc="SS")
                        elif DBG["evac"] == "gelu1":
                            if e_i == 0 and tt == 0 and ht == 0:
                                sg = P.emit("scalar",
                                       lambda e, h1sl=h1sl, psv=psv: e.activation(
                                           h1sl, psv, AF.Gelu),
                                       waits=gw, inc="SS")
                            else:
                                sg = P.emit("scalar",
                                       lambda e, h1sl=h1sl, psv=psv: e.copy(h1sl, psv),
                                       waits=gw, inc="SS")
                        elif DBG["evac"] == "tinygelu":
                            sg = P.emit("scalar",
                                   lambda e, h1sl=h1sl, psv=psv: e.copy(h1sl, psv),
                                   waits=gw, inc="SS")
                            if e_i == 0 and tt == 0 and ht == 0:
                                sg = P.emit("scalar",
                                       lambda e: e.activation(
                                           DUMMY[0:1, 0:2], DUMMY[0:1, 4:6], AF.Gelu),
                                       inc="SS")
                        else:
                            sg = P.emit("scalar",
                                   lambda e, h1sl=h1sl, psv=psv: e.copy(h1sl, psv),
                                   waits=gw, inc="SS")
                        pm1_free[slot] = ("SS", sg)
                        sg_hist.append(sg)
                        last_marker = ("SS", sg)
                        if e_i > 0:
                            vw = P.emit("vector",
                                   lambda e, h1sl=h1sl, tt=tt: e.tensor_mul(
                                       h1sl, h1sl, WB[:, tt * 512:(tt + 1) * 512]),
                                   waits=[("SS", sg), wb_ready], inc="SV")
                            last_marker = ("SV", vw)
                    return last_marker

                def emit_mm2(e_i, tt, h1_ready):
                    buf = tt % 2
                    last_st = 0
                    for dtl in range(NCH):
                        slot = state["rot2"] % 4
                        state["rot2"] += 1
                        psv = PM2[:, slot * 512:(slot + 1) * 512]
                        tr = 0
                        for kc in range(16):
                            w = []
                            if kc == 0:
                                w = [pm2_free[slot], h1_ready]
                                if len(vx_hist) >= 2:
                                    w.append(("SV", vx_hist[-2]))
                                if dtl == 0 and tt == 0:
                                    w.append(("W2S", gq_w2[e_i]))
                            r = P.emit("tensor",
                                   lambda e, psv=psv, kc=kc, dtl=dtl, buf=buf: e.matmul(
                                       psv,
                                       W2[:, kc * D + dtl * 128: kc * D + (dtl + 1) * 128],
                                       H1G[:, buf * 8192 + kc * 512:
                                           buf * 8192 + (kc + 1) * 512],
                                       start=(kc == 0), stop=(kc == 15)),
                                   waits=w,
                                   inc=("ST" if kc == 15 else None))
                            if r is not None:
                                tr = r
                        last_st = tr
                        wsrc = (WB[:, tt * 512:(tt + 1) * 512] if e_i > 0
                                else ONES[:, 0:512])
                        b2ap = B2C[:, (e_i % 2) * 8 + dtl:(e_i % 2) * 8 + dtl + 1]
                        vx = P.emit("vector",
                               lambda e, psv=psv, dtl=dtl, tt=tt: e.tensor_add(
                                   xres(dtl)[:, tt * 512:(tt + 1) * 512],
                                   xres(dtl)[:, tt * 512:(tt + 1) * 512], psv),
                               waits=[("ST", tr)], inc="SV")
                        vb2 = P.emit("vector",
                               lambda e, wsrc=wsrc, b2ap=b2ap, dtl=dtl, tt=tt:
                                   e.scalar_tensor_tensor(
                                       xres(dtl)[:, tt * 512:(tt + 1) * 512],
                                       wsrc, b2ap,
                                       xres(dtl)[:, tt * 512:(tt + 1) * 512],
                                       ALU.mult, ALU.add),
                               waits=[("SV", vx), ("B2S_", gq_b2[e_i])], inc="SV")
                        pm2_free[slot] = ("SV", vx)
                        vx_hist.append(vx)
                        state["xadd"] = vb2
                    return ("ST", last_st)

                for e_i in range(min(NEXP, DBG["nexp"])):
                    if e_i > 0:
                        gq_w1.append(P.emit("gpsimd",
                                lambda e, e_i=e_i: e.dma_start(out=W1[:], in_=w1cat[e_i]),
                                waits=[w1_free], inc="W1S", inc_n=16))
                        gq_wrow = P.emit("gpsimd",
                                lambda e, e_i=e_i: e.dma_start(
                                    out=WROW[:], in_=WSB[e_i - 1:e_i, :]),
                                waits=[("SV", router_done_sv), wrow_free],
                                inc="WRS", inc_n=16)
                        gq_b1.append(P.emit("gpsimd",
                                lambda e, e_i=e_i: e.dma_start(
                                    out=B1C[:, (e_i % 2) * 16:(e_i % 2) * 16 + 16],
                                    in_=b1col[e_i]),
                                waits=[b1_free[e_i % 2]], inc="B1S", inc_n=16))
                        wb_evac = 0
                        for r in range(4):
                            tb_ = P.emit("tensor",
                                    lambda e, r=r: e.matmul(
                                        PM1[:, r * 512:(r + 1) * 512],
                                        ONES[0:1, 0:128],
                                        WROW[0:1, r * 512:(r + 1) * 512],
                                        start=True, stop=True),
                                    waits=[("WRS", gq_wrow), pm1_free[r],
                                           ("SS", C["SS"]), ("SV", C["SV"])],
                                    inc="ST")
                            wb_evac = P.emit("vector",
                                    lambda e, r=r: e.tensor_copy(
                                        WB[:, r * 512:(r + 1) * 512],
                                        PM1[:, r * 512:(r + 1) * 512]),
                                    waits=[("ST", tb_), wb_free], inc="SV")
                            pm1_free[r] = ("SV", wb_evac)
                        wrow_free = ("ST", tb_)
                        wb_ready = ("SV", wb_evac)
                    else:
                        wb_ready = ("SV", 0)

                    mm1_first_w = [("W1S", gq_w1[e_i])]
                    if e_i == 0:
                        mm1_first_w += [("SV", ln2_done),
                                        ("SV", router_done_sv),
                                        ("ST", router_done_st),
                                        ("SS", C["SS"])]

                    h1_ready = {}
                    h1_ready[0] = emit_mm1(e_i, 0, wb_ready, mm1_first_w)
                    if e_i > 0:
                        gq_b2.append(P.emit("gpsimd",
                                lambda e, e_i=e_i: e.dma_start(
                                    out=B2C[:, (e_i % 2) * 8:(e_i % 2) * 8 + 8],
                                    in_=b2col_t[e_i]),
                                waits=[b2_free], inc="B2S_", inc_n=16))
                        gq_w2.append(P.emit("gpsimd",
                                lambda e, e_i=e_i: e.dma_start(out=W2[:], in_=w2cat[e_i]),
                                waits=[w2_free], inc="W2S", inc_n=16))
                    if DBG["stop"] == "mm1" and e_i == 0:
                        h1_ready[1] = emit_mm1(e_i, 1, wb_ready, mm1_first_w)
                        _DBGREF["H1G"] = H1G
                        _ckpt("mm1")
                    h1_ready[1] = emit_mm1(e_i, 1, wb_ready, mm1_first_w)
                    h1g_free[0] = emit_mm2(e_i, 0, h1_ready[0])
                    h1_ready[2] = emit_mm1(e_i, 2, wb_ready, mm1_first_w)
                    h1g_free[1] = emit_mm2(e_i, 1, h1_ready[1])
                    h1_ready[3] = emit_mm1(e_i, 3, wb_ready, mm1_first_w)
                    w1_free = ("ST", C["ST"])
                    h1g_free[0] = emit_mm2(e_i, 2, h1_ready[2])
                    h1g_free[1] = emit_mm2(e_i, 3, h1_ready[3])

                    w2_free = h1g_free[1]
                    wb_free = ("SV", state["xadd"])
                    b1_free[e_i % 2] = ("SS", C["SS"])
                    b2_free = ("SV", state["xadd"])


        except _PhaseDone:
            DBG["_aborted"] = True

        barrier()
        if DBG["dump"] == "h1g":
            H1G_ = _DBGREF["H1G"]
            gq_out = 0
            dw = [("SS", C["SS"]), ("SV", C["SV"]), ("ST", C["ST"])]
            for c in range(NCH):
                gq_out = P.emit("gpsimd",
                       lambda e, c=c: e.dma_start(
                           out=ot[c * 128:(c + 1) * 128, :],
                           in_=H1G_[:, c * 2048:(c + 1) * 2048]),
                       waits=(dw if c == 0 else []),
                       inc="GQ", inc_n=16)
            P.emit("gpsimd", lambda e: e.nop(), waits=[("GQ", gq_out)])
        elif DBG["dump"] == "qkv":
            QT_, KT_, VA_ = _DBGREF["QT"], _DBGREF["KT"], _DBGREF["VA"]
            gq_out = P.emit("gpsimd", lambda e: e.dma_start(
                out=ot[0:64, :], in_=QT_[0:64, :]), inc="GQ", inc_n=16)
            gq_out = P.emit("gpsimd", lambda e: e.dma_start(
                out=ot[64:128, :], in_=KT_[0:64, :]), inc="GQ", inc_n=16)
            gq_out = P.emit("gpsimd", lambda e: e.dma_start(
                out=ot[128:256, 0:1040], in_=VA_[:, :]), inc="GQ", inc_n=16)
            P.emit("gpsimd", lambda e: e.nop(), waits=[("GQ", gq_out)])
        elif DBG["dump"] == "j0":
            PT_, AVN_ = _DBGREF["PT"], _DBGREF["AVN"]
            gq_out = 0
            for b in range(4):
                gq_out = P.emit("gpsimd", lambda e, b=b: e.dma_start(
                    out=ot[b * 128:(b + 1) * 128, :],
                    in_=PT_[:, b * 2048:(b + 1) * 2048]), inc="GQ", inc_n=16)
            gq_out = P.emit("gpsimd", lambda e: e.dma_start(
                out=ot[512:640, 0:256], in_=AVN_[:, :]), inc="GQ", inc_n=16)
            P.emit("gpsimd", lambda e: e.nop(), waits=[("GQ", gq_out)])
        elif DBG["dump"] == "avt":
            AVT_ = _DBGREF["AVT"]
            gq_out = 0
            for c in range(NCH):
                gq_out = P.emit("gpsimd",
                       lambda e, c=c: e.dma_start(
                           out=ot[c * 128:(c + 1) * 128, :],
                           in_=AVT_[:, c * T:(c + 1) * T]),
                       inc="GQ", inc_n=16)
            P.emit("gpsimd", lambda e: e.nop(), waits=[("GQ", gq_out)])
        elif DBG["dump"] == "xln":
            gq_out = 0
            for c in range(NCH):
                gq_out = P.emit("gpsimd",
                       lambda e, c=c: e.dma_start(out=ot[c * 128:(c + 1) * 128, :],
                                                  in_=xln(c)),
                       inc="GQ", inc_n=16)
            P.emit("gpsimd", lambda e: e.nop(), waits=[("GQ", gq_out)])
        else:
            sq_out = 0
            for c in range(NCH):
                sq_out = P.emit("sync",
                       lambda e, c=c: e.dma_start(out=ot[c * 128:(c + 1) * 128, :],
                                                  in_=xres(c)),
                       inc="SQ", inc_n=16)
            P.emit("sync", lambda e: e.nop(), waits=[("SQ", sq_out)])

        def _preload_both(e):
            try:
                from concourse.hw_specs import get_activation_tables
                for idx, funcs in enumerate(
                        get_activation_tables(nc.m.arch).values()):
                    if AF.Exp in funcs and AF.Ln in funcs:
                        e.add_instruction(mybir.InstLoadActFuncSet(
                            name=nc.get_next_instruction_name(),
                            act_func_set_id=idx, ins=[], outs=[]))
                        return
            except Exception:
                pass

        with nc.Block() as block:
            @block.tensor
            def _(e):
                for f in P.ops["tensor"]:
                    f(e)

            @block.vector
            def _(e):
                for f in P.ops["vector"]:
                    f(e)

            @block.scalar
            def _(e):
                for f in P.ops["scalar"]:
                    f(e)

            @block.sync
            def _(e):
                for f in P.ops["sync"]:
                    f(e)

            @block.gpsimd
            def _(e):
                for f in P.ops["gpsimd"]:
                    f(e)

        if DBG.get("_aborted"):
            ctx.pop_all()
            DBG["_aborted"] = False

    return nc


_CACHE = {}


def _host_layout(inputs):
    f32 = np.float32
    g = lambda k: np.asarray(inputs[k], dtype=f32)
    x = g("L0_attn_x")
    wqkv = g("L0_attn_Wqkv")
    bqkv = g("L0_attn_bqkv")
    wr = g("L0_attn_Wr")
    br = g("L0_attn_br")
    rww = g("L0_ffn_router_w")
    rwb = g("L0_ffn_router_b")
    sw1 = g("L0_ffn_sw1")
    sb1 = g("L0_ffn_sb1")
    sw2 = g("L0_ffn_sw2")
    sb2 = g("L0_ffn_sb2")
    rw1 = g("L0_ffn_rw1")
    rb1 = g("L0_ffn_rb1")
    rw2 = g("L0_ffn_rw2")
    rb2 = g("L0_ffn_rb2")

    cc = np.ascontiguousarray
    com = {}
    com["wqkv2"] = cc(np.concatenate([wqkv, wqkv], axis=0))
    com["bqkv_row"] = cc(bqkv.reshape(1, 192))
    com["bqkv_col"] = cc(np.stack([bqkv[0:64], bqkv[64:128]], axis=1))
    com["wr_rl"] = cc(wr.reshape(8, 128, D).transpose(1, 0, 2).reshape(128, 8 * D))
    com["br_row"] = cc(br.reshape(1, D))
    com["rww_rl"] = cc(rww.reshape(8, 128, NE).transpose(1, 0, 2).reshape(128, 8 * NE))
    com["rwb_row"] = cc(rwb.reshape(1, NE))
    w1 = np.concatenate([sw1[None], rw1], axis=0)
    w2 = np.concatenate([sw2[None], rw2], axis=0)
    com["w1cat"] = cc(w1.reshape(NEXP, 8, 128, DFF).transpose(0, 2, 1, 3)
                      .reshape(NEXP, 128, 8 * DFF))
    com["w2cat"] = cc(w2.reshape(NEXP, 16, 128, D).transpose(0, 2, 1, 3)
                      .reshape(NEXP, 128, 16 * D))
    b1 = np.concatenate([sb1[None], rb1], axis=0)
    com["b1col"] = cc(b1.reshape(NEXP, 16, 128).transpose(0, 2, 1))
    com["b2row"] = cc(np.concatenate([sb2[None], rb2], axis=0)
                      .reshape(NEXP, 1, D))
    b2all = np.concatenate([sb2[None], rb2], axis=0)
    com["b2col_t"] = cc(b2all.reshape(NEXP, 8, 128).transpose(0, 2, 1))

    bf = ml_dtypes.bfloat16
    com["c_ones"] = np.ones((128, 512), bf)
    com["c_idb"] = np.eye(128, dtype=bf)
    com["c_idf"] = np.eye(128, dtype=f32)
    p = np.arange(128)[:, None]
    fidx = np.arange(512)[None, :]
    m = np.zeros((128, 4 * 512), f32)
    for d in range(4):
        m[:, d * 512:(d + 1) * 512] = ((p + 128 * d) < fidx).astype(f32)
    com["c_masks"] = m
    com["c_eps"] = np.full((1, 1), 1e-5, f32)
    com["c_rcp1024"] = np.full((1, 1), 1.0 / 1024.0, f32)

    in_maps = []
    for c in range(8):
        mp = dict(com)
        mp["xt"] = cc(x[c].T)
        in_maps.append(mp)
    return in_maps


def kernel(**inputs):
    in_maps = _host_layout(inputs)
    if "nc" not in _CACHE:
        _CACHE["nc"] = build_program()
    nc = _CACHE["nc"]
    res = run_bass_kernel_spmd(nc, in_maps, core_ids=list(range(8)))
    out = np.stack([np.asarray(r["ot"]).T for r in res.results], axis=0)
    return out.astype(np.float32)


# version 09
# speedup vs baseline: 1.0265x; 1.0265x over previous
import numpy as np
import ml_dtypes
from contextlib import ExitStack
import concourse.bass as bass
import concourse.mybir as mybir
from concourse.bass_utils import run_bass_kernel_spmd

F32 = mybir.dt.float32
BF16 = mybir.dt.bfloat16
AF = mybir.ActivationFunctionType
ALU = mybir.AluOpType
AX = mybir.AxisListType

T = 2048
D = 1024
H = 16
HE = 64
DFF = 2048
NE = 8
NEXP = 9
NCH = D // 128
NTT = T // 512
NMT = T // 128

ENGS = ("tensor", "vector", "scalar", "sync", "gpsimd")

DBG = {"stop": "out", "nexp": NEXP, "dump": "xres", "nheads": H, "qkv_parts": "all", "evac": "gelu"}
_DBGREF = {}


class _PhaseDone(Exception):
    pass


def _ckpt(ph):
    if DBG["stop"] == ph:
        raise _PhaseDone()


class Plan:
    def __init__(self, sems):
        self.ops = {e: [] for e in ENGS}
        self.cnt = {k: 0 for k in sems}
        self.sems = sems
        self.tail_inc = {e: True for e in ENGS}

    def emit(self, eng, fn, waits=(), inc=None, inc_n=1, neutral=False):
        waits = tuple((s, self.cnt[s] if s in ("SQ", "GQ") else v)
                      for (s, v) in waits)
        waits = tuple((s, v) for (s, v) in waits if v > 0)
        for s, v in waits:
            assert v <= self.cnt[s], f"wait on future count: {s} {v} > {self.cnt[s]}"
        if inc is not None:
            self.cnt[inc] += inc_n
        sems = self.sems

        def run(e, _w=waits, _f=fn, _i=inc, _n=inc_n):
            for s, v in _w:
                e.wait_ge(sems[s], v)
            inst = _f(e)
            if _i is not None:
                inst.then_inc(sems[_i], _n)
        self.ops[eng].append(run)
        if not neutral:
            self.tail_inc[eng] = inc is not None
        return self.cnt[inc] if inc is not None else None


def build_program():
    nc = bass.Bass("TRN2", target_bir_lowering=False)
    dt = nc.dram_tensor
    xt = dt("xt", [D, T], F32, kind="ExternalInput").ap()
    wqkv2 = dt("wqkv2", [128, 192], F32, kind="ExternalInput").ap()
    bqkv_row = dt("bqkv_row", [1, 192], F32, kind="ExternalInput").ap()
    bqkv_col = dt("bqkv_col", [64, 2], F32, kind="ExternalInput").ap()
    wr_rl = dt("wr_rl", [128, NCH * D], F32, kind="ExternalInput").ap()
    br_row = dt("br_row", [1, D], F32, kind="ExternalInput").ap()
    rww_rl = dt("rww_rl", [128, NCH * NE], F32, kind="ExternalInput").ap()
    rwb_row = dt("rwb_row", [1, NE], F32, kind="ExternalInput").ap()
    w1cat = dt("w1cat", [NEXP, 128, NCH * DFF], F32, kind="ExternalInput").ap()
    w2cat = dt("w2cat", [NEXP, 128, (DFF // 128) * D], F32, kind="ExternalInput").ap()
    b1col = dt("b1col", [NEXP, 128, 16], F32, kind="ExternalInput").ap()
    b2row = dt("b2row", [NEXP, 1, D], F32, kind="ExternalInput").ap()
    b2col_t = dt("b2col_t", [NEXP, 128, 8], F32, kind="ExternalInput").ap()
    c_ones = dt("c_ones", [128, 512], BF16, kind="ExternalInput").ap()
    c_idb = dt("c_idb", [128, 128], BF16, kind="ExternalInput").ap()
    c_idf = dt("c_idf", [128, 128], F32, kind="ExternalInput").ap()
    c_masks = dt("c_masks", [128, 4 * 512], F32, kind="ExternalInput").ap()
    c_eps = dt("c_eps", [1, 1], F32, kind="ExternalInput").ap()
    c_rcp1024 = dt("c_rcp1024", [1, 1], F32, kind="ExternalInput").ap()
    ot = dt("ot", [D, T], F32, kind="ExternalOutput").ap()

    with ExitStack() as ctx:
        E = ctx.enter_context
        sems = {
            "ST": E(nc.semaphore("ST")),
            "SV": E(nc.semaphore("SV")),
            "SS": E(nc.semaphore("SS")),
            "SQ": E(nc.semaphore("SQ")),
            "GQ": E(nc.semaphore("GQ")),
            "W1S": E(nc.semaphore("W1S")),
            "W2S": E(nc.semaphore("W2S")),
            "WRS": E(nc.semaphore("WRS")),
            "B1S": E(nc.semaphore("B1S")),
            "B2S_": E(nc.semaphore("B2S_")),
        }
        P = Plan(sems)
        C = P.cnt

        XRES = E(nc.sbuf_tensor("XRES", [128, NCH * T], F32))
        XLN = E(nc.sbuf_tensor("XLN", [128, NCH * T], BF16))
        ONES = E(nc.sbuf_tensor("ONES", [128, 512], BF16))
        WSB = E(nc.sbuf_tensor("WSB", [8, T], BF16))
        WQ = E(nc.sbuf_tensor("WQ", [128, 192], BF16))
        EPS = E(nc.sbuf_tensor("EPS", [1, 1], F32))
        RCP1024 = E(nc.sbuf_tensor("RCP1024", [1, 1], F32))
        DUMMY = E(nc.sbuf_tensor("DUMMY", [1, 8], F32))

        nc.const_aps.register(F32, 1.0 / 1024.0, RCP1024[0:1, 0:1])

        def xres(c):
            return XRES[:, c * T:(c + 1) * T]

        def xln(c):
            return XLN[:, c * T:(c + 1) * T]

        def barrier():
            for eng in ("tensor", "vector", "scalar"):
                assert P.tail_inc[eng], f"barrier: {eng} tail lacks inc"
            w = [(k, C[k]) for k in C]
            for eng in ENGS:
                P.emit(eng, lambda e: e.nop(), waits=w, neutral=True)

        sq_x = []
        for c in range(NCH):
            sq_x.append(P.emit(
                "sync",
                lambda e, c=c: e.dma_start(out=xres(c), in_=xt[c * 128:(c + 1) * 128, :]),
                inc="SQ", inc_n=16))
        sq_ones = P.emit("sync", lambda e: e.dma_start(out=ONES[:], in_=c_ones),
                         inc="SQ", inc_n=16)
        sq_eps = P.emit("sync", lambda e: e.dma_start(out=EPS[:], in_=c_eps),
                        inc="SQ", inc_n=16)
        sq_rcp = P.emit("sync", lambda e: e.dma_start(out=RCP1024[:], in_=c_rcp1024),
                        inc="SQ", inc_n=16)
        gq_wq = P.emit("gpsimd", lambda e: e.dma_start(out=WQ[:], in_=wqkv2),
                       inc="GQ", inc_n=16)
        P.emit("scalar", lambda e: e.memzero(DUMMY[:]), inc="SS")

        _ln_n = [0]

        def emit_ln(x_waits):
            _ln_n[0] += 1
            sfx = str(_ln_n[0])
            with ExitStack() as lctx:
                LE = lctx.enter_context
                XSQ = LE(nc.sbuf_tensor("XSQ" + sfx, [128, T], BF16))
                MROW = LE(nc.sbuf_tensor("MROW" + sfx, [1, T], BF16))
                VROW = LE(nc.sbuf_tensor("VROW" + sfx, [1, T], BF16))
                LROW = LE(nc.sbuf_tensor("LROW" + sfx, [1, T], BF16))
                RSTD = LE(nc.sbuf_tensor("RSTD" + sfx, [1, T], BF16))
                MRSTD = LE(nc.sbuf_tensor("MRSTD" + sfx, [1, T], BF16))
                B2SB = LE(nc.sbuf_tensor("B2SB" + sfx, [128, T], BF16))
                with ExitStack() as pctx:
                    PSM = pctx.enter_context(nc.psum_tensor("PSM" + sfx, [1, T], F32))
                    PSQ = pctx.enter_context(nc.psum_tensor("PSQ" + sfx, [1, T], F32))
                    sq_done_prev = 0
                    for c in range(NCH):
                        vc = P.emit("vector",
                                    lambda e, c=c: e.tensor_copy(xln(c), xres(c)),
                                    waits=list(x_waits[c]), inc="SV")
                        vs = P.emit("vector",
                                    lambda e, c=c: e.tensor_mul(XSQ[:], xln(c), xln(c)),
                                    waits=[("SV", vc), ("ST", sq_done_prev)],
                                    inc="SV")
                        for tt in range(NTT):
                            P.emit("tensor",
                                   lambda e, c=c, tt=tt: e.matmul(
                                       PSM[0:1, tt * 512:(tt + 1) * 512],
                                       ONES[:, 0:1],
                                       xln(c)[:, tt * 512:(tt + 1) * 512],
                                       start=(c == 0), stop=(c == NCH - 1)),
                                   waits=([("SV", vc), ("SQ", sq_ones)] if tt == 0 else []))
                        for tt in range(NTT):
                            r = P.emit("tensor",
                                   lambda e, c=c, tt=tt: e.matmul(
                                       PSQ[0:1, tt * 512:(tt + 1) * 512],
                                       ONES[:, 0:1],
                                       XSQ[:, tt * 512:(tt + 1) * 512],
                                       start=(c == 0), stop=(c == NCH - 1)),
                                   waits=([("SV", vs)] if tt == 0 else []),
                                   inc=("ST" if tt == NTT - 1 else None))
                            if r is not None:
                                sq_done_prev = r
                    st_stats = sq_done_prev
                    v1 = P.emit("vector",
                                lambda e: e.tensor_scalar(MROW[:], PSM[:], 1.0 / 1024.0,
                                                          None, ALU.mult),
                                waits=[("ST", st_stats), ("SQ", sq_rcp)], inc="SV")
                    s1 = P.emit("scalar",
                                lambda e: e.activation(VROW[:], MROW[:], AF.Square),
                                waits=[("SV", v1)], inc="SS")
                    v2 = P.emit("vector",
                                lambda e: e.scalar_tensor_tensor(
                                    VROW[:], PSQ[:], 1.0 / 1024.0, VROW[:],
                                    ALU.mult, ALU.subtract),
                                waits=[("SS", s1)], inc="SV")
                    s2 = P.emit("scalar",
                                lambda e: e.activation(LROW[:], VROW[:], AF.Ln,
                                                       bias=EPS[0:1, 0:1]),
                                waits=[("SV", v2), ("SQ", sq_eps)], inc="SS")
                    s3 = P.emit("scalar",
                                lambda e: e.activation(RSTD[:], LROW[:], AF.Exp,
                                                       scale=-0.5),
                                waits=[("SS", s2)], inc="SS")
                    v3 = P.emit("vector",
                                lambda e: e.tensor_mul(MRSTD[:], MROW[:], RSTD[:]),
                                waits=[("SS", s3)], inc="SV")
                with ExitStack() as pctx:
                    PB1 = pctx.enter_context(nc.psum_tensor("PB1" + sfx, [128, T], F32))
                    PB2 = pctx.enter_context(nc.psum_tensor("PB2" + sfx, [128, T], F32))
                    tb = []
                    for tt in range(NTT):
                        P.emit("tensor",
                               lambda e, tt=tt: e.matmul(
                                   PB1[:, tt * 512:(tt + 1) * 512],
                                   ONES[0:1, 0:128],
                                   RSTD[0:1, tt * 512:(tt + 1) * 512],
                                   start=True, stop=True),
                               waits=([("SS", s3), ("SV", v2)] if tt == 0 else []))
                        tb.append(P.emit("tensor",
                               lambda e, tt=tt: e.matmul(
                                   PB2[:, tt * 512:(tt + 1) * 512],
                                   ONES[0:1, 0:128],
                                   MRSTD[0:1, tt * 512:(tt + 1) * 512],
                                   start=True, stop=True),
                               waits=([("SV", v3)] if tt == 0 else []), inc="ST"))
                    cb1 = P.emit("vector",
                            lambda e: e.tensor_copy(XSQ[:], PB1[:]),
                            waits=[("ST", tb[3])], inc="SV")
                    cb2 = P.emit("vector",
                            lambda e: e.tensor_copy(B2SB[:], PB2[:]),
                            waits=[("SV", cb1)], inc="SV")
                    applied = []
                    prev = cb2
                    for c in range(NCH):
                        va = P.emit("vector",
                                    lambda e, c=c: e.tensor_mul(
                                        xln(c), xres(c), XSQ[:]),
                                    waits=[("SV", prev)], inc="SV")
                        prev = P.emit("vector",
                                    lambda e, c=c: e.tensor_sub(
                                        xln(c), xln(c), B2SB[:]),
                                    waits=[("SV", va)], inc="SV")
                        applied.append(prev)
                    return applied

        try:
            ln1_applied = emit_ln([[("SQ", sq_x[c])] for c in range(NCH)])
            ln1_done = ln1_applied[-1]
            _ckpt("ln1")

            barrier()
            _ckpt("bar1")

            with ExitStack() as actx:
                AE = actx.enter_context
                QT = AE(nc.sbuf_tensor("QT", [64, T], BF16))
                KT = AE(nc.sbuf_tensor("KT", [64, T], BF16))
                VA = AE(nc.sbuf_tensor("VA", [128, 16 * 65], BF16))
                PT = AE(nc.sbuf_tensor("PT", [128, 16 * 512], BF16))
                AVT = AE(nc.sbuf_tensor("AVT", [128, NCH * T], BF16))
                WRF = AE(nc.sbuf_tensor("WRF", [128, NCH * D], BF16))
                MASKS = AE(nc.sbuf_tensor("MASKS", [128, 4 * 512], F32))
                SMASK = AE(nc.sbuf_tensor("SMASK", [128, 2 * 512], F32))
                AVN = AE(nc.sbuf_tensor("AVN", [128, 4 * 64], BF16))
                RZ = AE(nc.sbuf_tensor("RZ", [128, 4], F32))
                IDB = AE(nc.sbuf_tensor("IDB", [128, 128], BF16))
                BQKVC = AE(nc.sbuf_tensor("BQKVC", [64, 2], F32))
                BQKVR = AE(nc.sbuf_tensor("BQKVR", [1, 192], BF16))
                BRR = AE(nc.sbuf_tensor("BRR", [1, D], BF16))
                PS4 = AE(nc.psum_tensor("PS4", [128, 2048], F32))
                PS2 = AE(nc.psum_tensor("PS2", [128, 1024], F32))
                PS1 = AE(nc.psum_tensor("PS1", [128, 260], F32))
                PTR = AE(nc.psum_tensor("PTR", [128, 256], BF16))

                sq_masks = P.emit("sync", lambda e: e.dma_start(out=MASKS[:], in_=c_masks),
                                  inc="SQ", inc_n=16)
                sq_idb = P.emit("sync", lambda e: e.dma_start(out=IDB[:], in_=c_idb),
                                inc="SQ", inc_n=16)
                gq_wrf = P.emit("gpsimd", lambda e: e.dma_start(out=WRF[:], in_=wr_rl),
                                inc="GQ", inc_n=16)
                gq_bqc = P.emit("gpsimd", lambda e: e.dma_start(out=BQKVC[:], in_=bqkv_col),
                                inc="GQ", inc_n=16)
                gq_bqr = P.emit("gpsimd", lambda e: e.dma_start(out=BQKVR[:], in_=bqkv_row),
                                inc="GQ", inc_n=16)
                gq_brr = P.emit("gpsimd", lambda e: e.dma_start(out=BRR[:], in_=br_row),
                                inc="GQ", inc_n=16)

                sv_vones = P.emit("vector",
                                  lambda e: e.memset(
                                      VA.ap().rearrange("p (g n) -> p g n", n=65)[:, :, 64:65],
                                      1.0),
                                  inc="SV")

                if DBG["stop"] == "adma":
                    gg = P.emit("gpsimd", lambda e: e.nop(),
                                waits=[("GQ", C["GQ"]), ("SQ", C["SQ"])])
                    _DBGREF["QT"], _DBGREF["KT"], _DBGREF["VA"] = QT, KT, VA
                    _ckpt("adma")
                slot_free = {r: ("ST", 0) for r in range(6)}
                ps2_free = ("SV", 0)
                ps1_free = ("SV", 0)
                ptr_free = {0: ("SV", 0), 1: ("SV", 0)}
                avn_free = {m: ("ST", 0) for m in range(4)}
                ss_exp_last = 0
                sc_hist = []
                st_av_prev = 0
                avt_done = 0

                for h in range(min(H, DBG["nheads"])):
                    ch, off = h // 2, (h % 2) * 64
                    xh = XLN[off:off + 64, ch * T:(ch + 1) * T]

                    ev_q, ev_k = [], []
                    for s in (range(NTT) if DBG["qkv_parts"] in ("all", "qk") else []):
                        tsl = slice(s * 512, (s + 1) * 512)
                        for which in ("q", "k"):
                            base = 0 if which == "q" else 1024
                            slot_id = (0 if which == "q" else 2) + s % 2
                            psv = PS4[0:64, base + (s % 2) * 512: base + (s % 2) * 512 + 512]
                            wcol = 0 if which == "q" else 64
                            evs = ev_q if which == "q" else ev_k
                            if s >= 2:
                                w = [("SS", evs[s - 2])]
                            else:
                                w = [slot_free[slot_id]]
                            if s == 0 and which == "q":
                                w += [("SV", ln1_applied[ch]), ("GQ", gq_wq)]
                            tq = P.emit("tensor",
                                        lambda e, psv=psv, wcol=wcol, tsl=tsl, xh=xh,
                                        off=off: e.matmul(
                                            psv, WQ[off:off + 64, wcol:wcol + 64],
                                            xh[:, tsl],
                                            start=True, stop=True),
                                        waits=w, inc="ST")
                            dst = (QT if which == "q" else KT)[0:64, tsl]
                            bias = BQKVC[0:64, 0:1] if which == "q" else BQKVC[0:64, 1:2]
                            evs.append(P.emit("scalar",
                                        lambda e, dst=dst, psv=psv, bias=bias: e.activation(
                                            dst, psv, AF.Identity, bias=bias),
                                        waits=[("ST", tq), ("GQ", gq_bqc)], inc="SS"))
                    tv_last = 0
                    for mt in (range(NMT) if DBG["qkv_parts"] in ("all", "v") else []):
                        psl = PS2[:, mt * 64:(mt + 1) * 64]
                        w = ([ps2_free, slot_free[4], slot_free[5],
                              ("SV", ln1_applied[ch])] if mt == 0 else [])
                        P.emit("tensor",
                               lambda e, psl=psl, mt=mt, ch=ch, off=off: e.matmul(
                                   psl,
                                   XLN[off:off + 64,
                                       ch * T + mt * 128: ch * T + (mt + 1) * 128],
                                   WQ[off:off + 64, 128:192],
                                   start=(mt in (0, 8)), stop=False,
                                   skip_group_check=True),
                               waits=w)
                        tv_last = P.emit("tensor",
                               lambda e, psl=psl: e.matmul(
                                   psl, ONES[0:1, 0:128], BQKVR[0:1, 128:192],
                                   start=False, stop=True, skip_group_check=True),
                               waits=([("GQ", gq_bqr), ("SQ", sq_ones)] if mt == 0 else []),
                               inc=("ST" if mt == NMT - 1 else None)) or tv_last
                    if DBG["qkv_parts"] in ("all", "v"):
                        sv_va = P.emit("vector",
                                   lambda e: e.tensor_copy(
                                       VA.ap().rearrange("p (g n) -> p g n", n=65)[:, :, 0:64],
                                       PS2.ap().rearrange("p (g n) -> p g n", n=64)),
                                   waits=[("ST", tv_last), ("SV", sv_vones)], inc="SV")
                    else:
                        sv_va = sv_vones
                    ps2_free = ("SV", max(sv_va, avt_done))
                    if h == 0:
                        _DBGREF["QT"], _DBGREF["KT"], _DBGREF["VA"] = QT, KT, VA
                        _ckpt("qkv")

                    rot = 0
                    for j in range(NTT):
                        for kc in range(4 * j + 4):
                            slot = rot % 6
                            rot += 1
                            psv = (PS4[:, slot * 512:(slot + 1) * 512] if slot < 4
                                   else PS2[:, (slot - 4) * 512:(slot - 3) * 512])
                            w = [slot_free[slot]]
                            if len(sc_hist) >= 2:
                                w.append(sc_hist[-2])
                            if kc == 0 and j == 0:
                                w += [("SS", ev_q[-1]), ("SS", ev_k[-1])]
                            ts = P.emit("tensor",
                                        lambda e, psv=psv, kc=kc, j=j: e.matmul(
                                            psv,
                                            KT[0:64, kc * 128:(kc + 1) * 128],
                                            QT[0:64, j * 512:(j + 1) * 512],
                                            start=True, stop=True),
                                        waits=w, inc="ST")
                            pts = PT[:, kc * 512:(kc + 1) * 512]
                            ptguard = [("ST", st_av_prev)] if kc == 0 else []
                            if kc < 4 * j:
                                ss_exp_last = P.emit("scalar",
                                        lambda e, pts=pts, psv=psv: e.activation(
                                            pts, psv, AF.Exp, scale=0.125),
                                        waits=[("ST", ts)] + ptguard, inc="SS")
                                slot_free[slot] = ("SS", ss_exp_last)
                                sc_hist.append(("SS", ss_exp_last))
                            else:
                                d = kc - 4 * j
                                smsl = SMASK[:, (kc % 2) * 512:(kc % 2) * 512 + 512]
                                vm = P.emit("vector",
                                        lambda e, smsl=smsl, psv=psv, d=d: e.tensor_mul(
                                            smsl, psv, MASKS[:, d * 512:(d + 1) * 512]),
                                        waits=[("ST", ts), ("SQ", sq_masks),
                                               ("SS", ss_exp_last)], inc="SV")
                                ss_exp_last = P.emit("scalar",
                                        lambda e, pts=pts, smsl=smsl: e.activation(
                                            pts, smsl, AF.Exp, scale=0.125),
                                        waits=[("SV", vm)] + ptguard, inc="SS")
                                slot_free[slot] = ("SV", vm)
                        st_av = 0
                        for mt in range(4):
                            for kc in range(16):
                                if kc < 4 * j + 4:
                                    lhs = PT[:, kc * 512 + mt * 128: kc * 512 + (mt + 1) * 128]
                                else:
                                    lhs = ONES[:, 0:128]
                                w = []
                                if mt == 0 and kc == 0:
                                    w = [("SS", ss_exp_last), ("SV", sv_va), ps1_free]
                                st_av = P.emit("tensor",
                                        lambda e, lhs=lhs, mt=mt, kc=kc: e.matmul(
                                            PS1[:, mt * 65:(mt + 1) * 65],
                                            lhs, VA[:, kc * 65:(kc + 1) * 65],
                                            start=(mt == 0 and kc == 0), stop=(kc == 15),
                                            skip_group_check=True),
                                        waits=w,
                                        inc=("ST" if (mt == 3 and kc == 15) else None)) or st_av
                        st_av_prev = st_av
                        sv_norm_last = 0
                        for mt in range(4):
                            rz_w = [("ST", st_av)] if mt == 0 else [("SV", sv_norm_last)]
                            rz = P.emit("vector",
                                    lambda e, mt=mt: e.reciprocal(
                                        RZ[:, mt:mt + 1],
                                        PS1[:, mt * 65 + 64: mt * 65 + 65]),
                                    waits=rz_w, inc="SV")
                            vn = P.emit("vector",
                                    lambda e, mt=mt: e.tensor_scalar(
                                        AVN[:, mt * 64:(mt + 1) * 64],
                                        PS1[:, mt * 65: mt * 65 + 64],
                                        RZ[:, mt:mt + 1], None, ALU.mult),
                                    waits=[("SV", rz), avn_free[mt]], inc="SV")
                            sv_norm_last = vn
                            tp = P.emit("tensor",
                                    lambda e, mt=mt, off=off: e.transpose(
                                        PTR[off:off + 64, (mt % 2) * 128:(mt % 2) * 128 + 128],
                                        AVN[:, mt * 64:(mt + 1) * 64],
                                        IDB[:],
                                        tile_position=(0, off)),
                                    waits=[("SV", vn), ("SQ", sq_idb), ptr_free[mt % 2]],
                                    inc="ST")
                            avn_free[mt] = ("ST", tp)
                            dst = AVT[off:off + 64,
                                      ch * T + j * 512 + mt * 128:
                                      ch * T + j * 512 + (mt + 1) * 128]
                            cp = P.emit("vector",
                                    lambda e, dst=dst, mt=mt, off=off: e.tensor_copy(
                                        dst,
                                        PTR[off:off + 64, (mt % 2) * 128:(mt % 2) * 128 + 128]),
                                    waits=[("ST", tp)], inc="SV")
                            ptr_free[mt % 2] = ("SV", cp)
                            avt_done = cp
                        ps1_free = ("SV", avt_done)
                        if h == 0 and j == 0:
                            _DBGREF["PT"] = PT
                            _DBGREF["AVN"] = AVN
                            _ckpt("j0")

                _DBGREF["AVT"] = AVT
                _ckpt("attnav")

                rot = 0
                vacc_hist = []
                for dtl in range(NCH):
                    for tt in range(NTT):
                        slot = rot % 4
                        rot += 1
                        psv = PS4[:, slot * 512:(slot + 1) * 512]
                        for kc in range(NCH):
                            w = []
                            if kc == 0:
                                w = [slot_free[slot], ("SV", avt_done), ("GQ", gq_wrf)]
                                if len(sc_hist) >= 2:
                                    w.append(sc_hist[-2])
                                if len(vacc_hist) >= 2:
                                    w.append(("SV", vacc_hist[-2]))
                            P.emit("tensor",
                                   lambda e, psv=psv, kc=kc, dtl=dtl, tt=tt: e.matmul(
                                       psv,
                                       WRF[:, kc * D + dtl * 128: kc * D + (dtl + 1) * 128],
                                       AVT[:, kc * T + tt * 512: kc * T + (tt + 1) * 512],
                                       start=(kc == 0), stop=False),
                                   waits=w)
                        twr = P.emit("tensor",
                               lambda e, psv=psv, dtl=dtl: e.matmul(
                                   psv, BRR[0:1, dtl * 128:(dtl + 1) * 128],
                                   ONES[0:1, 0:512], start=False, stop=True),
                               waits=[("GQ", gq_brr)], inc="ST")
                        vacc = P.emit("vector",
                               lambda e, psv=psv, dtl=dtl, tt=tt: e.tensor_add(
                                   xres(dtl)[:, tt * 512:(tt + 1) * 512],
                                   xres(dtl)[:, tt * 512:(tt + 1) * 512], psv),
                               waits=[("ST", twr)], inc="SV")
                        slot_free[slot] = ("SV", vacc)
                        vacc_hist.append(vacc)

            barrier()
            _ckpt("attn")

            with ExitStack() as fctx:
                FE = fctx.enter_context
                W1 = FE(nc.sbuf_tensor("W1", [128, NCH * DFF], BF16))
                W2 = FE(nc.sbuf_tensor("W2", [128, 16 * D], BF16))
                WB = FE(nc.sbuf_tensor("WB", [128, T], BF16))
                WROW = FE(nc.sbuf_tensor("WROW", [1, T], BF16))
                B2C = FE(nc.sbuf_tensor("B2C", [128, 16], F32))
                B1C = FE(nc.sbuf_tensor("B1C", [128, 32], F32))
                RWW = FE(nc.sbuf_tensor("RWW", [128, NCH * NE], BF16))
                RBR = FE(nc.sbuf_tensor("RBR", [1, NE], BF16))

                gq_rww = P.emit("gpsimd", lambda e: e.dma_start(out=RWW[:], in_=rww_rl),
                                inc="GQ", inc_n=16)
                gq_rbr = P.emit("gpsimd", lambda e: e.dma_start(out=RBR[:], in_=rwb_row),
                                inc="GQ", inc_n=16)
                gq_w1 = [P.emit("gpsimd", lambda e: e.dma_start(out=W1[:], in_=w1cat[0]),
                                inc="W1S", inc_n=16)]
                gq_w2 = [P.emit("gpsimd", lambda e: e.dma_start(out=W2[:], in_=w2cat[0]),
                                inc="W2S", inc_n=16)]
                gq_b1 = [P.emit("gpsimd",
                                lambda e: e.dma_start(out=B1C[:, 0:16], in_=b1col[0]),
                                inc="B1S", inc_n=16)]
                gq_b2 = [P.emit("gpsimd",
                                lambda e: e.dma_start(out=B2C[:, 0:8], in_=b2col_t[0]),
                                inc="B2S_", inc_n=16)]

                ln2_applied = emit_ln([[] for _ in range(NCH)])
                ln2_done = ln2_applied[-1]
                _ckpt("ln2")

                with ExitStack() as rctx:
                    RE_ = rctx.enter_context
                    RLOG = RE_(nc.sbuf_tensor("RLOG", [8, 512], F32))
                    RT = RE_(nc.sbuf_tensor("RT", [128, 128], F32))
                    REX = RE_(nc.sbuf_tensor("REX", [128, 128], F32))
                    RS4 = RE_(nc.sbuf_tensor("RS4", [128, 64], F32))
                    RS2 = RE_(nc.sbuf_tensor("RS2", [128, 32], F32))
                    RS1 = RE_(nc.sbuf_tensor("RS1", [128, 16], F32))
                    RCPS = RE_(nc.sbuf_tensor("RCPS", [128, 16], F32))
                    RCPR = RE_(nc.sbuf_tensor("RCPR", [128, 128], F32))
                    MRA = RE_(nc.sbuf_tensor("MRA", [128, 128], F32))
                    PMA = RE_(nc.sbuf_tensor("PMA", [128, 128], F32))
                    M2R = RE_(nc.sbuf_tensor("M2R", [128, 128], F32))
                    MSKA = RE_(nc.sbuf_tensor("MSKA", [128, 128], F32))
                    WTA = RE_(nc.sbuf_tensor("WTA", [128, 128], F32))
                    IDF = RE_(nc.sbuf_tensor("IDF", [128, 128], F32))
                    PR = RE_(nc.psum_tensor("PR", [8, 512], F32))
                    PTRK = RE_(nc.psum_tensor("PTRK", [128, 8], F32))
                    PTW = RE_(nc.psum_tensor("PTW", [8, 128], F32))

                    sq_idf = P.emit("sync", lambda e: e.dma_start(out=IDF[:], in_=c_idf),
                                    waits=[("SV", ln2_done), ("ST", C["ST"]),
                                           ("SS", C["SS"])],
                                    inc="SQ", inc_n=16)
                    rt_last = 0
                    pr_free = ("SV", 0)
                    ptrk_free = ("SV", 0)
                    for tt in range(NTT):
                        for kc in range(NCH):
                            w = []
                            if kc == 0:
                                w = [("GQ", gq_rww), ("SV", ln2_applied[kc]), pr_free]
                            elif tt == 0:
                                w = [("SV", ln2_applied[kc])]
                            P.emit("tensor",
                                   lambda e, kc=kc, tt=tt: e.matmul(
                                       PR[:], RWW[:, kc * NE:(kc + 1) * NE],
                                       xln(kc)[:, tt * 512:(tt + 1) * 512],
                                       start=(kc == 0), stop=False),
                                   waits=w)
                        tpr = P.emit("tensor",
                               lambda e: e.matmul(PR[:], RBR[0:1, 0:NE], ONES[0:1, 0:512],
                                                  start=False, stop=True),
                               waits=[("GQ", gq_rbr)], inc="ST")
                        vlog = P.emit("vector",
                               lambda e: e.tensor_copy(RLOG[:], PR[:]),
                               waits=[("ST", tpr)], inc="SV")
                        pr_free = ("SV", vlog)
                        for b in range(4):
                            tk = P.emit("tensor",
                                   lambda e, b=b: e.transpose(
                                       PTRK[:], RLOG[:, b * 128:(b + 1) * 128],
                                       IDF[0:8, 0:8]),
                                   waits=[("SV", vlog), ("SQ", sq_idf), ptrk_free],
                                   inc="ST")
                            g = tt * 4 + b
                            rt_last = P.emit("vector",
                                   lambda e, g=g: e.tensor_copy(
                                       RT[:, g * 8:(g + 1) * 8], PTRK[:]),
                                   waits=[("ST", tk)], inc="SV")
                            ptrk_free = ("SV", rt_last)

                    def r3(t_, n=8):
                        return t_.ap().rearrange("p (g n) -> p g n", n=n)

                    sx = P.emit("scalar",
                                lambda e: e.activation(REX[:], RT[:], AF.Exp),
                                waits=[("SV", rt_last)], inc="SS")
                    c0 = P.emit("vector", lambda e: e.tensor_add(
                            r3(RS4, 4), r3(REX)[:, :, 0:4], r3(REX)[:, :, 4:8]),
                            waits=[("SS", sx)], inc="SV")
                    c1 = P.emit("vector", lambda e: e.tensor_add(
                            r3(RS2, 2), r3(RS4, 4)[:, :, 0:2], r3(RS4, 4)[:, :, 2:4]),
                            waits=[("SV", c0)], inc="SV")
                    c2 = P.emit("vector", lambda e: e.tensor_add(
                            r3(RS1, 1), r3(RS2, 2)[:, :, 0:1], r3(RS2, 2)[:, :, 1:2]),
                            waits=[("SV", c1)], inc="SV")
                    c3 = P.emit("vector", lambda e: e.reciprocal(RCPS[:], RS1[:]),
                                waits=[("SV", c2)], inc="SV")
                    c4 = P.emit("vector", lambda e: e.tensor_copy(
                            r3(RCPR)[:, :, 0:1], r3(RCPS, 1)),
                            waits=[("SV", c3)], inc="SV")
                    c5 = P.emit("vector", lambda e: e.tensor_copy(
                            r3(RCPR)[:, :, 1:2], r3(RCPR)[:, :, 0:1]),
                            waits=[("SV", c4)], inc="SV")
                    c6 = P.emit("vector", lambda e: e.tensor_copy(
                            r3(RCPR)[:, :, 2:4], r3(RCPR)[:, :, 0:2]),
                            waits=[("SV", c5)], inc="SV")
                    c7 = P.emit("vector", lambda e: e.tensor_copy(
                            r3(RCPR)[:, :, 4:8], r3(RCPR)[:, :, 0:4]),
                            waits=[("SV", c6)], inc="SV")
                    m0 = P.emit("vector", lambda e: e.tensor_max(
                            r3(RS4, 4), r3(REX)[:, :, 0:4], r3(REX)[:, :, 4:8]),
                            waits=[("SV", c1)], inc="SV")
                    m1 = P.emit("vector", lambda e: e.tensor_max(
                            r3(RS2, 2), r3(RS4, 4)[:, :, 0:2], r3(RS4, 4)[:, :, 2:4]),
                            waits=[("SV", m0)], inc="SV")
                    m2 = P.emit("vector", lambda e: e.tensor_max(
                            r3(RS1, 1), r3(RS2, 2)[:, :, 0:1], r3(RS2, 2)[:, :, 1:2]),
                            waits=[("SV", m1), ("SV", c3)], inc="SV")
                    mfill = P.emit("vector", lambda e: e.memset(MRA[:], -1e30), inc="SV")
                    mput = P.emit("vector", lambda e: e.tensor_copy(
                            r3(MRA)[:, :, 0:1], r3(RS1, 1)),
                            waits=[("SV", m2), ("SV", mfill)], inc="SV")
                    mr_last = mput
                    for g in range(16):
                        mr_last = P.emit("vector",
                                lambda e, g=g: e.match_replace(
                                    PMA[:, g * 8:(g + 1) * 8], MRA[:, g * 8:(g + 1) * 8],
                                    REX[:, g * 8:(g + 1) * 8], -1e30),
                                waits=([("SV", mput)] if g == 0 else []), inc="SV")
                    n0 = P.emit("vector", lambda e: e.tensor_max(
                            r3(RS4, 4), r3(PMA)[:, :, 0:4], r3(PMA)[:, :, 4:8]),
                            waits=[("SV", mr_last)], inc="SV")
                    n1 = P.emit("vector", lambda e: e.tensor_max(
                            r3(RS2, 2), r3(RS4, 4)[:, :, 0:2], r3(RS4, 4)[:, :, 2:4]),
                            waits=[("SV", n0)], inc="SV")
                    n2 = P.emit("vector", lambda e: e.tensor_max(
                            r3(RS1, 1), r3(RS2, 2)[:, :, 0:1], r3(RS2, 2)[:, :, 1:2]),
                            waits=[("SV", n1)], inc="SV")
                    d0 = P.emit("vector", lambda e: e.tensor_copy(
                            r3(M2R)[:, :, 0:1], r3(RS1, 1)),
                            waits=[("SV", n2)], inc="SV")
                    d1 = P.emit("vector", lambda e: e.tensor_copy(
                            r3(M2R)[:, :, 1:2], r3(M2R)[:, :, 0:1]),
                            waits=[("SV", d0)], inc="SV")
                    d2 = P.emit("vector", lambda e: e.tensor_copy(
                            r3(M2R)[:, :, 2:4], r3(M2R)[:, :, 0:2]),
                            waits=[("SV", d1)], inc="SV")
                    d3 = P.emit("vector", lambda e: e.tensor_copy(
                            r3(M2R)[:, :, 4:8], r3(M2R)[:, :, 0:4]),
                            waits=[("SV", d2)], inc="SV")
                    e0 = P.emit("vector", lambda e: e.tensor_tensor(
                            MSKA[:], REX[:], M2R[:], ALU.is_ge),
                            waits=[("SV", d3)], inc="SV")
                    e1 = P.emit("vector", lambda e: e.tensor_mul(WTA[:], REX[:], MSKA[:]),
                            waits=[("SV", e0)], inc="SV")
                    e2 = P.emit("vector", lambda e: e.tensor_mul(WTA[:], WTA[:], RCPR[:]),
                            waits=[("SV", e1), ("SV", c7)], inc="SV")
                    ptw_free = ("SV", 0)
                    wsb_done = 0
                    for g in range(16):
                        tw = P.emit("tensor",
                                lambda e, g=g: e.transpose(
                                    PTW[:], WTA[:, g * 8:(g + 1) * 8], IDF[:]),
                                waits=[("SV", e2), ptw_free], inc="ST")
                        wsb_done = P.emit("vector",
                                lambda e, g=g: e.tensor_copy(
                                    WSB[:, g * 128:(g + 1) * 128], PTW[:]),
                                waits=[("ST", tw)], inc="SV")
                        ptw_free = ("SV", wsb_done)
                    router_done_sv = wsb_done
                    router_done_st = C["ST"]
                    _ckpt("router")

                H1G = FE(nc.sbuf_tensor("H1G", [128, 2 * 16 * 512], BF16))
                PM1 = FE(nc.psum_tensor("PM1", [128, 2048], F32))
                PM2 = FE(nc.psum_tensor("PM2", [128, 2048], F32))

                sg_hist = []
                vx_hist = []
                pm1_free = {r: ("SV", 0) for r in range(4)}
                pm2_free = {r: ("SV", 0) for r in range(4)}
                h1g_free = {0: ("ST", 0), 1: ("ST", 0)}
                wrow_free = ("ST", 0)
                wb_free = ("SV", 0)
                w1_free = ("ST", 0)
                w2_free = ("ST", 0)
                b1_free = {0: ("SS", 0), 1: ("SS", 0)}
                b2_free = ("ST", 0)
                state = {"rot1": 0, "rot2": 0, "xadd": 0}

                def emit_mm1(e_i, tt, wb_ready, mm1_first_w):
                    buf = tt % 2
                    last_marker = None
                    for ht in range(16):
                        slot = state["rot1"] % 4
                        state["rot1"] += 1
                        psv = PM1[:, slot * 512:(slot + 1) * 512]
                        tmm = 0
                        for kc in range(NCH):
                            w = []
                            if kc == 0:
                                w = [pm1_free[slot]]
                                if len(sg_hist) >= 2:
                                    w.append(("SS", sg_hist[-2]))
                                if ht == 0:
                                    w = w + mm1_first_w
                            r = P.emit("tensor",
                                   lambda e, psv=psv, kc=kc, ht=ht, tt=tt: e.matmul(
                                       psv,
                                       W1[:, kc * DFF + ht * 128: kc * DFF + (ht + 1) * 128],
                                       xln(kc)[:, tt * 512:(tt + 1) * 512],
                                       start=(kc == 0), stop=(kc == NCH - 1)),
                                   waits=w,
                                   inc=("ST" if kc == NCH - 1 else None))
                            if r is not None:
                                tmm = r
                        h1sl = H1G[:, buf * 8192 + ht * 512: buf * 8192 + (ht + 1) * 512]
                        gw = [("ST", tmm), ("B1S", gq_b1[e_i])]
                        if ht == 0:
                            gw.append(h1g_free[buf])
                        if e_i == 0 and tt == 0 and ht == 0:
                            gw += [("SV", router_done_sv), ("ST", router_done_st)]
                        if DBG["evac"] == "gelu":
                            sg = P.emit("scalar",
                                   lambda e, h1sl=h1sl, psv=psv, e_i=e_i, ht=ht: e.activation(
                                       h1sl, psv, AF.Gelu,
                                       bias=B1C[:, (e_i % 2) * 16 + ht:
                                                (e_i % 2) * 16 + ht + 1]),
                                   waits=gw, inc="SS")
                        elif DBG["evac"] == "gelunb":
                            sg = P.emit("scalar",
                                   lambda e, h1sl=h1sl, psv=psv: e.activation(
                                       h1sl, psv, AF.Gelu),
                                   waits=gw, inc="SS")
                        elif DBG["evac"].startswith("gelun_"):
                            lim = int(DBG["evac"].split("_")[1])
                            if e_i == 0 and (tt * 16 + ht) < lim:
                                sg = P.emit("scalar",
                                       lambda e, h1sl=h1sl, psv=psv: e.activation(
                                           h1sl, psv, AF.Gelu),
                                       waits=gw, inc="SS")
                            else:
                                sg = P.emit("scalar",
                                       lambda e, h1sl=h1sl, psv=psv: e.copy(h1sl, psv),
                                       waits=gw, inc="SS")
                        elif DBG["evac"] == "gelu1":
                            if e_i == 0 and tt == 0 and ht == 0:
                                sg = P.emit("scalar",
                                       lambda e, h1sl=h1sl, psv=psv: e.activation(
                                           h1sl, psv, AF.Gelu),
                                       waits=gw, inc="SS")
                            else:
                                sg = P.emit("scalar",
                                       lambda e, h1sl=h1sl, psv=psv: e.copy(h1sl, psv),
                                       waits=gw, inc="SS")
                        elif DBG["evac"] == "tinygelu":
                            sg = P.emit("scalar",
                                   lambda e, h1sl=h1sl, psv=psv: e.copy(h1sl, psv),
                                   waits=gw, inc="SS")
                            if e_i == 0 and tt == 0 and ht == 0:
                                sg = P.emit("scalar",
                                       lambda e: e.activation(
                                           DUMMY[0:1, 0:2], DUMMY[0:1, 4:6], AF.Gelu),
                                       inc="SS")
                        else:
                            sg = P.emit("scalar",
                                   lambda e, h1sl=h1sl, psv=psv: e.copy(h1sl, psv),
                                   waits=gw, inc="SS")
                        pm1_free[slot] = ("SS", sg)
                        sg_hist.append(sg)
                        last_marker = ("SS", sg)
                        if e_i > 0:
                            vw = P.emit("vector",
                                   lambda e, h1sl=h1sl, tt=tt: e.tensor_mul(
                                       h1sl, h1sl, WB[:, tt * 512:(tt + 1) * 512]),
                                   waits=[("SS", sg), wb_ready], inc="SV")
                            last_marker = ("SV", vw)
                    return last_marker

                def emit_mm2(e_i, tt, h1_ready):
                    buf = tt % 2
                    last_st = 0
                    for dtl in range(NCH):
                        slot = state["rot2"] % 4
                        state["rot2"] += 1
                        psv = PM2[:, slot * 512:(slot + 1) * 512]
                        tr = 0
                        for kc in range(16):
                            w = []
                            if kc == 0:
                                w = [pm2_free[slot], h1_ready]
                                if len(vx_hist) >= 2:
                                    w.append(("SV", vx_hist[-2]))
                                if dtl == 0 and tt == 0:
                                    w.append(("W2S", gq_w2[e_i]))
                            r = P.emit("tensor",
                                   lambda e, psv=psv, kc=kc, dtl=dtl, buf=buf: e.matmul(
                                       psv,
                                       W2[:, kc * D + dtl * 128: kc * D + (dtl + 1) * 128],
                                       H1G[:, buf * 8192 + kc * 512:
                                           buf * 8192 + (kc + 1) * 512],
                                       start=(kc == 0), stop=(kc == 15)),
                                   waits=w,
                                   inc=("ST" if kc == 15 else None))
                            if r is not None:
                                tr = r
                        last_st = tr
                        wsrc = (WB[:, tt * 512:(tt + 1) * 512] if e_i > 0
                                else ONES[:, 0:512])
                        b2ap = B2C[:, (e_i % 2) * 8 + dtl:(e_i % 2) * 8 + dtl + 1]
                        vx = P.emit("vector",
                               lambda e, psv=psv, dtl=dtl, tt=tt: e.tensor_add(
                                   xres(dtl)[:, tt * 512:(tt + 1) * 512],
                                   xres(dtl)[:, tt * 512:(tt + 1) * 512], psv),
                               waits=[("ST", tr)], inc="SV")
                        vb2 = P.emit("vector",
                               lambda e, wsrc=wsrc, b2ap=b2ap, dtl=dtl, tt=tt:
                                   e.scalar_tensor_tensor(
                                       xres(dtl)[:, tt * 512:(tt + 1) * 512],
                                       wsrc, b2ap,
                                       xres(dtl)[:, tt * 512:(tt + 1) * 512],
                                       ALU.mult, ALU.add),
                               waits=[("SV", vx), ("B2S_", gq_b2[e_i])], inc="SV")
                        pm2_free[slot] = ("SV", vx)
                        vx_hist.append(vx)
                        state["xadd"] = vb2
                    return ("ST", last_st)

                for e_i in range(min(NEXP, DBG["nexp"])):
                    if e_i > 0:
                        gq_w1.append(P.emit("gpsimd",
                                lambda e, e_i=e_i: e.dma_start(out=W1[:], in_=w1cat[e_i]),
                                waits=[w1_free], inc="W1S", inc_n=16))
                        gq_wrow = P.emit("gpsimd",
                                lambda e, e_i=e_i: e.dma_start(
                                    out=WROW[:], in_=WSB[e_i - 1:e_i, :]),
                                waits=[("SV", router_done_sv), wrow_free],
                                inc="WRS", inc_n=16)
                        gq_b1.append(P.emit("gpsimd",
                                lambda e, e_i=e_i: e.dma_start(
                                    out=B1C[:, (e_i % 2) * 16:(e_i % 2) * 16 + 16],
                                    in_=b1col[e_i]),
                                waits=[b1_free[e_i % 2]], inc="B1S", inc_n=16))
                        wb_evac = 0
                        for r in range(4):
                            tb_ = P.emit("tensor",
                                    lambda e, r=r: e.matmul(
                                        PM1[:, r * 512:(r + 1) * 512],
                                        ONES[0:1, 0:128],
                                        WROW[0:1, r * 512:(r + 1) * 512],
                                        start=True, stop=True),
                                    waits=[("WRS", gq_wrow), pm1_free[r],
                                           ("SS", C["SS"]), ("SV", C["SV"])],
                                    inc="ST")
                            wb_evac = P.emit("vector",
                                    lambda e, r=r: e.tensor_copy(
                                        WB[:, r * 512:(r + 1) * 512],
                                        PM1[:, r * 512:(r + 1) * 512]),
                                    waits=[("ST", tb_), wb_free], inc="SV")
                            pm1_free[r] = ("SV", wb_evac)
                        wrow_free = ("ST", tb_)
                        wb_ready = ("SV", wb_evac)
                    else:
                        wb_ready = ("SV", 0)

                    mm1_first_w = [("W1S", gq_w1[e_i])]
                    if e_i == 0:
                        mm1_first_w += [("SV", ln2_done),
                                        ("SV", router_done_sv),
                                        ("ST", router_done_st),
                                        ("SS", C["SS"])]

                    h1_ready = {}
                    h1_ready[0] = emit_mm1(e_i, 0, wb_ready, mm1_first_w)
                    if e_i > 0:
                        gq_b2.append(P.emit("gpsimd",
                                lambda e, e_i=e_i: e.dma_start(
                                    out=B2C[:, (e_i % 2) * 8:(e_i % 2) * 8 + 8],
                                    in_=b2col_t[e_i]),
                                waits=[b2_free], inc="B2S_", inc_n=16))
                        gq_w2.append(P.emit("gpsimd",
                                lambda e, e_i=e_i: e.dma_start(out=W2[:], in_=w2cat[e_i]),
                                waits=[w2_free], inc="W2S", inc_n=16))
                    if DBG["stop"] == "mm1" and e_i == 0:
                        h1_ready[1] = emit_mm1(e_i, 1, wb_ready, mm1_first_w)
                        _DBGREF["H1G"] = H1G
                        _ckpt("mm1")
                    h1_ready[1] = emit_mm1(e_i, 1, wb_ready, mm1_first_w)
                    h1g_free[0] = emit_mm2(e_i, 0, h1_ready[0])
                    h1_ready[2] = emit_mm1(e_i, 2, wb_ready, mm1_first_w)
                    h1g_free[1] = emit_mm2(e_i, 1, h1_ready[1])
                    h1_ready[3] = emit_mm1(e_i, 3, wb_ready, mm1_first_w)
                    w1_free = ("ST", C["ST"])
                    h1g_free[0] = emit_mm2(e_i, 2, h1_ready[2])
                    h1g_free[1] = emit_mm2(e_i, 3, h1_ready[3])

                    w2_free = h1g_free[1]
                    wb_free = ("SV", state["xadd"])
                    b1_free[e_i % 2] = ("SS", C["SS"])
                    b2_free = ("SV", state["xadd"])


        except _PhaseDone:
            DBG["_aborted"] = True

        barrier()
        if DBG["dump"] == "h1g":
            H1G_ = _DBGREF["H1G"]
            gq_out = 0
            dw = [("SS", C["SS"]), ("SV", C["SV"]), ("ST", C["ST"])]
            for c in range(NCH):
                gq_out = P.emit("gpsimd",
                       lambda e, c=c: e.dma_start(
                           out=ot[c * 128:(c + 1) * 128, :],
                           in_=H1G_[:, c * 2048:(c + 1) * 2048]),
                       waits=(dw if c == 0 else []),
                       inc="GQ", inc_n=16)
            P.emit("gpsimd", lambda e: e.nop(), waits=[("GQ", gq_out)])
        elif DBG["dump"] == "qkv":
            QT_, KT_, VA_ = _DBGREF["QT"], _DBGREF["KT"], _DBGREF["VA"]
            gq_out = P.emit("gpsimd", lambda e: e.dma_start(
                out=ot[0:64, :], in_=QT_[0:64, :]), inc="GQ", inc_n=16)
            gq_out = P.emit("gpsimd", lambda e: e.dma_start(
                out=ot[64:128, :], in_=KT_[0:64, :]), inc="GQ", inc_n=16)
            gq_out = P.emit("gpsimd", lambda e: e.dma_start(
                out=ot[128:256, 0:1040], in_=VA_[:, :]), inc="GQ", inc_n=16)
            P.emit("gpsimd", lambda e: e.nop(), waits=[("GQ", gq_out)])
        elif DBG["dump"] == "j0":
            PT_, AVN_ = _DBGREF["PT"], _DBGREF["AVN"]
            gq_out = 0
            for b in range(4):
                gq_out = P.emit("gpsimd", lambda e, b=b: e.dma_start(
                    out=ot[b * 128:(b + 1) * 128, :],
                    in_=PT_[:, b * 2048:(b + 1) * 2048]), inc="GQ", inc_n=16)
            gq_out = P.emit("gpsimd", lambda e: e.dma_start(
                out=ot[512:640, 0:256], in_=AVN_[:, :]), inc="GQ", inc_n=16)
            P.emit("gpsimd", lambda e: e.nop(), waits=[("GQ", gq_out)])
        elif DBG["dump"] == "avt":
            AVT_ = _DBGREF["AVT"]
            gq_out = 0
            for c in range(NCH):
                gq_out = P.emit("gpsimd",
                       lambda e, c=c: e.dma_start(
                           out=ot[c * 128:(c + 1) * 128, :],
                           in_=AVT_[:, c * T:(c + 1) * T]),
                       inc="GQ", inc_n=16)
            P.emit("gpsimd", lambda e: e.nop(), waits=[("GQ", gq_out)])
        elif DBG["dump"] == "xln":
            gq_out = 0
            for c in range(NCH):
                gq_out = P.emit("gpsimd",
                       lambda e, c=c: e.dma_start(out=ot[c * 128:(c + 1) * 128, :],
                                                  in_=xln(c)),
                       inc="GQ", inc_n=16)
            P.emit("gpsimd", lambda e: e.nop(), waits=[("GQ", gq_out)])
        else:
            sq_out = 0
            for c in range(NCH):
                sq_out = P.emit("sync",
                       lambda e, c=c: e.dma_start(out=ot[c * 128:(c + 1) * 128, :],
                                                  in_=xres(c)),
                       inc="SQ", inc_n=16)
            P.emit("sync", lambda e: e.nop(), waits=[("SQ", sq_out)])

        def _preload_both(e):
            try:
                from concourse.hw_specs import get_activation_tables
                for idx, funcs in enumerate(
                        get_activation_tables(nc.m.arch).values()):
                    if AF.Exp in funcs and AF.Ln in funcs:
                        e.add_instruction(mybir.InstLoadActFuncSet(
                            name=nc.get_next_instruction_name(),
                            act_func_set_id=idx, ins=[], outs=[]))
                        return
            except Exception:
                pass

        with nc.Block() as block:
            @block.tensor
            def _(e):
                for f in P.ops["tensor"]:
                    f(e)

            @block.vector
            def _(e):
                for f in P.ops["vector"]:
                    f(e)

            @block.scalar
            def _(e):
                for f in P.ops["scalar"]:
                    f(e)

            @block.sync
            def _(e):
                for f in P.ops["sync"]:
                    f(e)

            @block.gpsimd
            def _(e):
                for f in P.ops["gpsimd"]:
                    f(e)

        if DBG.get("_aborted"):
            ctx.pop_all()
            DBG["_aborted"] = False

    return nc


_CACHE = {}


def _host_layout(inputs):
    f32 = np.float32
    g = lambda k: np.asarray(inputs[k], dtype=f32)
    x = g("L0_attn_x")
    wqkv = g("L0_attn_Wqkv")
    bqkv = g("L0_attn_bqkv")
    wr = g("L0_attn_Wr")
    br = g("L0_attn_br")
    rww = g("L0_ffn_router_w")
    rwb = g("L0_ffn_router_b")
    sw1 = g("L0_ffn_sw1")
    sb1 = g("L0_ffn_sb1")
    sw2 = g("L0_ffn_sw2")
    sb2 = g("L0_ffn_sb2")
    rw1 = g("L0_ffn_rw1")
    rb1 = g("L0_ffn_rb1")
    rw2 = g("L0_ffn_rw2")
    rb2 = g("L0_ffn_rb2")

    cc = np.ascontiguousarray
    com = {}
    com["wqkv2"] = cc(np.concatenate([wqkv, wqkv], axis=0))
    com["bqkv_row"] = cc(bqkv.reshape(1, 192))
    com["bqkv_col"] = cc(np.stack([bqkv[0:64], bqkv[64:128]], axis=1))
    com["wr_rl"] = cc(wr.reshape(8, 128, D).transpose(1, 0, 2).reshape(128, 8 * D))
    com["br_row"] = cc(br.reshape(1, D))
    com["rww_rl"] = cc(rww.reshape(8, 128, NE).transpose(1, 0, 2).reshape(128, 8 * NE))
    com["rwb_row"] = cc(rwb.reshape(1, NE))
    w1 = np.concatenate([sw1[None], rw1], axis=0)
    w2 = np.concatenate([sw2[None], rw2], axis=0)
    com["w1cat"] = cc(w1.reshape(NEXP, 8, 128, DFF).transpose(0, 2, 1, 3)
                      .reshape(NEXP, 128, 8 * DFF))
    com["w2cat"] = cc(w2.reshape(NEXP, 16, 128, D).transpose(0, 2, 1, 3)
                      .reshape(NEXP, 128, 16 * D))
    b1 = np.concatenate([sb1[None], rb1], axis=0)
    com["b1col"] = cc(b1.reshape(NEXP, 16, 128).transpose(0, 2, 1))
    com["b2row"] = cc(np.concatenate([sb2[None], rb2], axis=0)
                      .reshape(NEXP, 1, D))
    b2all = np.concatenate([sb2[None], rb2], axis=0)
    com["b2col_t"] = cc(b2all.reshape(NEXP, 8, 128).transpose(0, 2, 1))

    bf = ml_dtypes.bfloat16
    com["c_ones"] = np.ones((128, 512), bf)
    com["c_idb"] = np.eye(128, dtype=bf)
    com["c_idf"] = np.eye(128, dtype=f32)
    p = np.arange(128)[:, None]
    fidx = np.arange(512)[None, :]
    m = np.zeros((128, 4 * 512), f32)
    for d in range(4):
        m[:, d * 512:(d + 1) * 512] = ((p + 128 * d) < fidx).astype(f32)
    com["c_masks"] = m
    com["c_eps"] = np.full((1, 1), 1e-5, f32)
    com["c_rcp1024"] = np.full((1, 1), 1.0 / 1024.0, f32)

    in_maps = []
    for c in range(8):
        mp = dict(com)
        mp["xt"] = cc(x[c].T)
        in_maps.append(mp)
    return in_maps


def kernel(**inputs):
    in_maps = _host_layout(inputs)
    if "nc" not in _CACHE:
        _CACHE["nc"] = build_program()
    nc = _CACHE["nc"]
    res = run_bass_kernel_spmd(nc, in_maps, core_ids=list(range(8)))
    out = np.stack([np.asarray(r["ot"]).T for r in res.results], axis=0)
    return out.astype(np.float32)


# version 10
# speedup vs baseline: 1.0529x; 1.0257x over previous
import numpy as np
import ml_dtypes
from contextlib import ExitStack
import concourse.bass as bass
import concourse.mybir as mybir
from concourse.bass_utils import run_bass_kernel_spmd

F32 = mybir.dt.float32
BF16 = mybir.dt.bfloat16
AF = mybir.ActivationFunctionType
ALU = mybir.AluOpType
AX = mybir.AxisListType

T = 2048
D = 1024
H = 16
HE = 64
DFF = 2048
NE = 8
NEXP = 9
NCH = D // 128
NTT = T // 512
NMT = T // 128

ENGS = ("tensor", "vector", "scalar", "sync", "gpsimd")

DBG = {"stop": "out", "nexp": NEXP, "dump": "xres", "nheads": H, "qkv_parts": "all", "evac": "gelu"}
_DBGREF = {}


class _PhaseDone(Exception):
    pass


def _ckpt(ph):
    if DBG["stop"] == ph:
        raise _PhaseDone()


class Plan:
    def __init__(self, sems):
        self.ops = {e: [] for e in ENGS}
        self.cnt = {k: 0 for k in sems}
        self.sems = sems
        self.tail_inc = {e: True for e in ENGS}

    def emit(self, eng, fn, waits=(), inc=None, inc_n=1, neutral=False):
        waits = tuple((s, self.cnt[s] if s in ("SQ", "GQ") else v)
                      for (s, v) in waits)
        waits = tuple((s, v) for (s, v) in waits if v > 0)
        for s, v in waits:
            assert v <= self.cnt[s], f"wait on future count: {s} {v} > {self.cnt[s]}"
        if inc is not None:
            self.cnt[inc] += inc_n
        sems = self.sems

        def run(e, _w=waits, _f=fn, _i=inc, _n=inc_n):
            for s, v in _w:
                e.wait_ge(sems[s], v)
            inst = _f(e)
            if _i is not None:
                inst.then_inc(sems[_i], _n)
        self.ops[eng].append(run)
        if not neutral:
            self.tail_inc[eng] = inc is not None
        return self.cnt[inc] if inc is not None else None


def build_program():
    nc = bass.Bass("TRN2", target_bir_lowering=False)
    dt = nc.dram_tensor
    xt = dt("xt", [D, T], F32, kind="ExternalInput").ap()
    wqkv2 = dt("wqkv2", [128, 192], F32, kind="ExternalInput").ap()
    bqkv_row = dt("bqkv_row", [1, 192], F32, kind="ExternalInput").ap()
    bqkv_col = dt("bqkv_col", [64, 2], F32, kind="ExternalInput").ap()
    wr_rl = dt("wr_rl", [128, NCH * D], F32, kind="ExternalInput").ap()
    br_row = dt("br_row", [1, D], F32, kind="ExternalInput").ap()
    rww_rl = dt("rww_rl", [128, NCH * NE], F32, kind="ExternalInput").ap()
    rwb_row = dt("rwb_row", [1, NE], F32, kind="ExternalInput").ap()
    w1cat = dt("w1cat", [NEXP, 128, NCH * DFF], F32, kind="ExternalInput").ap()
    w2cat = dt("w2cat", [NEXP, 128, (DFF // 128) * D], F32, kind="ExternalInput").ap()
    b1col = dt("b1col", [NEXP, 128, 16], F32, kind="ExternalInput").ap()
    b2row = dt("b2row", [NEXP, 1, D], F32, kind="ExternalInput").ap()
    b2col_t = dt("b2col_t", [NEXP, 128, 8], F32, kind="ExternalInput").ap()
    c_ones = dt("c_ones", [128, 512], BF16, kind="ExternalInput").ap()
    c_idb = dt("c_idb", [128, 128], BF16, kind="ExternalInput").ap()
    c_idf = dt("c_idf", [128, 128], F32, kind="ExternalInput").ap()
    c_masks = dt("c_masks", [128, 4 * 512], F32, kind="ExternalInput").ap()
    c_eps = dt("c_eps", [1, 1], F32, kind="ExternalInput").ap()
    c_rcp1024 = dt("c_rcp1024", [1, 1], F32, kind="ExternalInput").ap()
    ot = dt("ot", [D, T], F32, kind="ExternalOutput").ap()

    with ExitStack() as ctx:
        E = ctx.enter_context
        sems = {
            "ST": E(nc.semaphore("ST")),
            "SV": E(nc.semaphore("SV")),
            "SS": E(nc.semaphore("SS")),
            "SQ": E(nc.semaphore("SQ")),
            "GQ": E(nc.semaphore("GQ")),
            "W1S": E(nc.semaphore("W1S")),
            "W2S": E(nc.semaphore("W2S")),
            "WRS": E(nc.semaphore("WRS")),
            "B1S": E(nc.semaphore("B1S")),
            "B2S_": E(nc.semaphore("B2S_")),
        }
        P = Plan(sems)
        C = P.cnt

        XRES = E(nc.sbuf_tensor("XRES", [128, NCH * T], F32))
        XLN = E(nc.sbuf_tensor("XLN", [128, NCH * T], BF16))
        ONES = E(nc.sbuf_tensor("ONES", [128, 512], BF16))
        WSB = E(nc.sbuf_tensor("WSB", [8, T], BF16))
        WQ = E(nc.sbuf_tensor("WQ", [128, 192], BF16))
        EPS = E(nc.sbuf_tensor("EPS", [1, 1], F32))
        RCP1024 = E(nc.sbuf_tensor("RCP1024", [1, 1], F32))
        DUMMY = E(nc.sbuf_tensor("DUMMY", [1, 8], F32))

        nc.const_aps.register(F32, 1.0 / 1024.0, RCP1024[0:1, 0:1])

        def xres(c):
            return XRES[:, c * T:(c + 1) * T]

        def xln(c):
            return XLN[:, c * T:(c + 1) * T]

        def barrier():
            for eng in ("tensor", "vector", "scalar"):
                assert P.tail_inc[eng], f"barrier: {eng} tail lacks inc"
            w = [(k, C[k]) for k in C]
            for eng in ENGS:
                P.emit(eng, lambda e: e.nop(), waits=w, neutral=True)

        sq_x = []
        for c in range(NCH):
            sq_x.append(P.emit(
                "sync",
                lambda e, c=c: e.dma_start(out=xres(c), in_=xt[c * 128:(c + 1) * 128, :]),
                inc="SQ", inc_n=16))
        sq_ones = P.emit("sync", lambda e: e.dma_start(out=ONES[:], in_=c_ones),
                         inc="SQ", inc_n=16)
        sq_eps = P.emit("sync", lambda e: e.dma_start(out=EPS[:], in_=c_eps),
                        inc="SQ", inc_n=16)
        sq_rcp = P.emit("sync", lambda e: e.dma_start(out=RCP1024[:], in_=c_rcp1024),
                        inc="SQ", inc_n=16)
        gq_wq = P.emit("gpsimd", lambda e: e.dma_start(out=WQ[:], in_=wqkv2),
                       inc="GQ", inc_n=16)
        P.emit("scalar", lambda e: e.memzero(DUMMY[:]), inc="SS")

        _ln_n = [0]

        def emit_ln(x_waits):
            _ln_n[0] += 1
            sfx = str(_ln_n[0])
            with ExitStack() as lctx:
                LE = lctx.enter_context
                XSQ = LE(nc.sbuf_tensor("XSQ" + sfx, [128, T], BF16))
                MROW = LE(nc.sbuf_tensor("MROW" + sfx, [1, T], BF16))
                VROW = LE(nc.sbuf_tensor("VROW" + sfx, [1, T], BF16))
                LROW = LE(nc.sbuf_tensor("LROW" + sfx, [1, T], BF16))
                RSTD = LE(nc.sbuf_tensor("RSTD" + sfx, [1, T], BF16))
                MRSTD = LE(nc.sbuf_tensor("MRSTD" + sfx, [1, T], BF16))
                B2SB = LE(nc.sbuf_tensor("B2SB" + sfx, [128, T], BF16))
                with ExitStack() as pctx:
                    PSM = pctx.enter_context(nc.psum_tensor("PSM" + sfx, [1, T], F32))
                    PSQ = pctx.enter_context(nc.psum_tensor("PSQ" + sfx, [1, T], F32))
                    sq_done_prev = 0
                    for c in range(NCH):
                        vc = P.emit("vector",
                                    lambda e, c=c: e.tensor_copy(xln(c), xres(c)),
                                    waits=list(x_waits[c]), inc="SV")
                        vs = P.emit("vector",
                                    lambda e, c=c: e.tensor_mul(XSQ[:], xln(c), xln(c)),
                                    waits=[("SV", vc), ("ST", sq_done_prev)],
                                    inc="SV")
                        for tt in range(NTT):
                            P.emit("tensor",
                                   lambda e, c=c, tt=tt: e.matmul(
                                       PSM[0:1, tt * 512:(tt + 1) * 512],
                                       ONES[:, 0:1],
                                       xln(c)[:, tt * 512:(tt + 1) * 512],
                                       start=(c == 0), stop=(c == NCH - 1)),
                                   waits=([("SV", vc), ("SQ", sq_ones)] if tt == 0 else []))
                        for tt in range(NTT):
                            r = P.emit("tensor",
                                   lambda e, c=c, tt=tt: e.matmul(
                                       PSQ[0:1, tt * 512:(tt + 1) * 512],
                                       ONES[:, 0:1],
                                       XSQ[:, tt * 512:(tt + 1) * 512],
                                       start=(c == 0), stop=(c == NCH - 1)),
                                   waits=([("SV", vs)] if tt == 0 else []),
                                   inc=("ST" if tt == NTT - 1 else None))
                            if r is not None:
                                sq_done_prev = r
                    st_stats = sq_done_prev
                    v1 = P.emit("vector",
                                lambda e: e.tensor_scalar(MROW[:], PSM[:], 1.0 / 1024.0,
                                                          None, ALU.mult),
                                waits=[("ST", st_stats), ("SQ", sq_rcp)], inc="SV")
                    s1 = P.emit("scalar",
                                lambda e: e.activation(VROW[:], MROW[:], AF.Square),
                                waits=[("SV", v1)], inc="SS")
                    v2 = P.emit("vector",
                                lambda e: e.scalar_tensor_tensor(
                                    VROW[:], PSQ[:], 1.0 / 1024.0, VROW[:],
                                    ALU.mult, ALU.subtract),
                                waits=[("SS", s1)], inc="SV")
                    s2 = P.emit("scalar",
                                lambda e: e.activation(LROW[:], VROW[:], AF.Ln,
                                                       bias=EPS[0:1, 0:1]),
                                waits=[("SV", v2), ("SQ", sq_eps)], inc="SS")
                    s3 = P.emit("scalar",
                                lambda e: e.activation(RSTD[:], LROW[:], AF.Exp,
                                                       scale=-0.5),
                                waits=[("SS", s2)], inc="SS")
                    v3 = P.emit("vector",
                                lambda e: e.tensor_mul(MRSTD[:], MROW[:], RSTD[:]),
                                waits=[("SS", s3)], inc="SV")
                with ExitStack() as pctx:
                    PB1 = pctx.enter_context(nc.psum_tensor("PB1" + sfx, [128, T], F32))
                    PB2 = pctx.enter_context(nc.psum_tensor("PB2" + sfx, [128, T], F32))
                    tb = []
                    for tt in range(NTT):
                        P.emit("tensor",
                               lambda e, tt=tt: e.matmul(
                                   PB1[:, tt * 512:(tt + 1) * 512],
                                   ONES[0:1, 0:128],
                                   RSTD[0:1, tt * 512:(tt + 1) * 512],
                                   start=True, stop=True),
                               waits=([("SS", s3), ("SV", v2)] if tt == 0 else []))
                        tb.append(P.emit("tensor",
                               lambda e, tt=tt: e.matmul(
                                   PB2[:, tt * 512:(tt + 1) * 512],
                                   ONES[0:1, 0:128],
                                   MRSTD[0:1, tt * 512:(tt + 1) * 512],
                                   start=True, stop=True),
                               waits=([("SV", v3)] if tt == 0 else []), inc="ST"))
                    cb1 = P.emit("vector",
                            lambda e: e.tensor_copy(XSQ[:], PB1[:]),
                            waits=[("ST", tb[3])], inc="SV")
                    cb2 = P.emit("vector",
                            lambda e: e.tensor_copy(B2SB[:], PB2[:]),
                            waits=[("SV", cb1)], inc="SV")
                    applied = []
                    prev = cb2
                    for c in range(NCH):
                        va = P.emit("vector",
                                    lambda e, c=c: e.tensor_mul(
                                        xln(c), xres(c), XSQ[:]),
                                    waits=[("SV", prev)], inc="SV")
                        prev = P.emit("vector",
                                    lambda e, c=c: e.tensor_sub(
                                        xln(c), xln(c), B2SB[:]),
                                    waits=[("SV", va)], inc="SV")
                        applied.append(prev)
                    return applied

        try:
            ln1_applied = emit_ln([[("SQ", sq_x[c])] for c in range(NCH)])
            ln1_done = ln1_applied[-1]
            _ckpt("ln1")

            barrier()
            _ckpt("bar1")

            with ExitStack() as actx:
                AE = actx.enter_context
                QT = AE(nc.sbuf_tensor("QT", [64, T], BF16))
                KT = AE(nc.sbuf_tensor("KT", [64, T], BF16))
                VA = AE(nc.sbuf_tensor("VA", [128, 16 * 65], BF16))
                PT = AE(nc.sbuf_tensor("PT", [128, 16 * 512], BF16))
                AVT = AE(nc.sbuf_tensor("AVT", [128, NCH * T], BF16))
                WRF = AE(nc.sbuf_tensor("WRF", [128, NCH * D], BF16))
                MASKS = AE(nc.sbuf_tensor("MASKS", [128, 4 * 512], F32))
                SMASK = AE(nc.sbuf_tensor("SMASK", [128, 2 * 512], F32))
                AVN = AE(nc.sbuf_tensor("AVN", [128, 4 * 64], BF16))
                RZ = AE(nc.sbuf_tensor("RZ", [128, 4], F32))
                IDB = AE(nc.sbuf_tensor("IDB", [128, 128], BF16))
                BQKVC = AE(nc.sbuf_tensor("BQKVC", [64, 2], F32))
                BQKVR = AE(nc.sbuf_tensor("BQKVR", [1, 192], BF16))
                BRR = AE(nc.sbuf_tensor("BRR", [1, D], BF16))
                PS4 = AE(nc.psum_tensor("PS4", [128, 2048], F32))
                PS2 = AE(nc.psum_tensor("PS2", [128, 1024], F32))
                PS1 = AE(nc.psum_tensor("PS1", [128, 260], F32))
                PTR = AE(nc.psum_tensor("PTR", [128, 256], BF16))

                sq_masks = P.emit("sync", lambda e: e.dma_start(out=MASKS[:], in_=c_masks),
                                  inc="SQ", inc_n=16)
                sq_idb = P.emit("sync", lambda e: e.dma_start(out=IDB[:], in_=c_idb),
                                inc="SQ", inc_n=16)
                gq_wrf = P.emit("gpsimd", lambda e: e.dma_start(out=WRF[:], in_=wr_rl),
                                inc="GQ", inc_n=16)
                gq_bqc = P.emit("gpsimd", lambda e: e.dma_start(out=BQKVC[:], in_=bqkv_col),
                                inc="GQ", inc_n=16)
                gq_bqr = P.emit("gpsimd", lambda e: e.dma_start(out=BQKVR[:], in_=bqkv_row),
                                inc="GQ", inc_n=16)
                gq_brr = P.emit("gpsimd", lambda e: e.dma_start(out=BRR[:], in_=br_row),
                                inc="GQ", inc_n=16)

                sv_vones = P.emit("vector",
                                  lambda e: e.memset(
                                      VA.ap().rearrange("p (g n) -> p g n", n=65)[:, :, 64:65],
                                      1.0),
                                  inc="SV")

                if DBG["stop"] == "adma":
                    gg = P.emit("gpsimd", lambda e: e.nop(),
                                waits=[("GQ", C["GQ"]), ("SQ", C["SQ"])])
                    _DBGREF["QT"], _DBGREF["KT"], _DBGREF["VA"] = QT, KT, VA
                    _ckpt("adma")
                slot_free = {r: ("ST", 0) for r in range(6)}
                ps2_free = ("SV", 0)
                ps1_free = ("SV", 0)
                ptr_free = {0: ("SV", 0), 1: ("SV", 0)}
                avn_free = {m: ("ST", 0) for m in range(4)}
                ss_exp_last = 0
                sc_hist = []
                st_av_prev = 0
                avt_done = 0

                for h in range(min(H, DBG["nheads"])):
                    ch, off = h // 2, (h % 2) * 64
                    xh = XLN[off:off + 64, ch * T:(ch + 1) * T]

                    ev_q, ev_k = [], []
                    for s in (range(NTT) if DBG["qkv_parts"] in ("all", "qk") else []):
                        tsl = slice(s * 512, (s + 1) * 512)
                        for which in ("q", "k"):
                            base = 0 if which == "q" else 1024
                            slot_id = (0 if which == "q" else 2) + s % 2
                            psv = PS4[0:64, base + (s % 2) * 512: base + (s % 2) * 512 + 512]
                            wcol = 0 if which == "q" else 64
                            evs = ev_q if which == "q" else ev_k
                            if s >= 2:
                                w = [("SS", evs[s - 2])]
                            else:
                                w = [slot_free[slot_id]]
                            if s == 0 and which == "q":
                                w += [("SV", ln1_applied[ch]), ("GQ", gq_wq)]
                            tq = P.emit("tensor",
                                        lambda e, psv=psv, wcol=wcol, tsl=tsl, xh=xh,
                                        off=off: e.matmul(
                                            psv, WQ[off:off + 64, wcol:wcol + 64],
                                            xh[:, tsl],
                                            start=True, stop=True),
                                        waits=w, inc="ST")
                            dst = (QT if which == "q" else KT)[0:64, tsl]
                            bias = BQKVC[0:64, 0:1] if which == "q" else BQKVC[0:64, 1:2]
                            evs.append(P.emit("scalar",
                                        lambda e, dst=dst, psv=psv, bias=bias: e.activation(
                                            dst, psv, AF.Identity, bias=bias),
                                        waits=[("ST", tq), ("GQ", gq_bqc)], inc="SS"))
                    tv_last = 0
                    for mt in (range(NMT) if DBG["qkv_parts"] in ("all", "v") else []):
                        psl = PS2[:, mt * 64:(mt + 1) * 64]
                        w = ([ps2_free, slot_free[4], slot_free[5],
                              ("SV", ln1_applied[ch])] if mt == 0 else [])
                        P.emit("tensor",
                               lambda e, psl=psl, mt=mt, ch=ch, off=off: e.matmul(
                                   psl,
                                   XLN[off:off + 64,
                                       ch * T + mt * 128: ch * T + (mt + 1) * 128],
                                   WQ[off:off + 64, 128:192],
                                   start=(mt in (0, 8)), stop=False,
                                   skip_group_check=True),
                               waits=w)
                        tv_last = P.emit("tensor",
                               lambda e, psl=psl: e.matmul(
                                   psl, ONES[0:1, 0:128], BQKVR[0:1, 128:192],
                                   start=False, stop=True, skip_group_check=True),
                               waits=([("GQ", gq_bqr), ("SQ", sq_ones)] if mt == 0 else []),
                               inc=("ST" if mt == NMT - 1 else None)) or tv_last
                    if DBG["qkv_parts"] in ("all", "v"):
                        sv_va = P.emit("vector",
                                   lambda e: e.tensor_copy(
                                       VA.ap().rearrange("p (g n) -> p g n", n=65)[:, :, 0:64],
                                       PS2.ap().rearrange("p (g n) -> p g n", n=64)),
                                   waits=[("ST", tv_last), ("SV", sv_vones)], inc="SV")
                    else:
                        sv_va = sv_vones
                    ps2_free = ("SV", max(sv_va, avt_done))
                    if h == 0:
                        _DBGREF["QT"], _DBGREF["KT"], _DBGREF["VA"] = QT, KT, VA
                        _ckpt("qkv")

                    rot = 0
                    for j in range(NTT):
                        for kc in range(4 * j + 4):
                            slot = rot % 6
                            rot += 1
                            psv = (PS4[:, slot * 512:(slot + 1) * 512] if slot < 4
                                   else PS2[:, (slot - 4) * 512:(slot - 3) * 512])
                            w = [slot_free[slot]]
                            if kc == 0 and j == 0:
                                w += [("SS", ev_q[-1]), ("SS", ev_k[-1])]
                            ts = P.emit("tensor",
                                        lambda e, psv=psv, kc=kc, j=j: e.matmul(
                                            psv,
                                            KT[0:64, kc * 128:(kc + 1) * 128],
                                            QT[0:64, j * 512:(j + 1) * 512],
                                            start=True, stop=True),
                                        waits=w, inc="ST")
                            pts = PT[:, kc * 512:(kc + 1) * 512]
                            ptguard = [("ST", st_av_prev)] if kc == 0 else []
                            if kc < 4 * j:
                                ss_exp_last = P.emit("scalar",
                                        lambda e, pts=pts, psv=psv: e.activation(
                                            pts, psv, AF.Exp, scale=0.125),
                                        waits=[("ST", ts)] + ptguard, inc="SS")
                                slot_free[slot] = ("SS", ss_exp_last)
                                sc_hist.append(("SS", ss_exp_last))
                            else:
                                d = kc - 4 * j
                                smsl = SMASK[:, (kc % 2) * 512:(kc % 2) * 512 + 512]
                                vm = P.emit("vector",
                                        lambda e, smsl=smsl, psv=psv, d=d: e.tensor_mul(
                                            smsl, psv, MASKS[:, d * 512:(d + 1) * 512]),
                                        waits=[("ST", ts), ("SQ", sq_masks),
                                               ("SS", ss_exp_last)], inc="SV")
                                ss_exp_last = P.emit("scalar",
                                        lambda e, pts=pts, smsl=smsl: e.activation(
                                            pts, smsl, AF.Exp, scale=0.125),
                                        waits=[("SV", vm)] + ptguard, inc="SS")
                                slot_free[slot] = ("SV", vm)
                        st_av = 0
                        for mt in range(4):
                            for kc in range(16):
                                if kc < 4 * j + 4:
                                    lhs = PT[:, kc * 512 + mt * 128: kc * 512 + (mt + 1) * 128]
                                else:
                                    lhs = ONES[:, 0:128]
                                w = []
                                if mt == 0 and kc == 0:
                                    w = [("SS", ss_exp_last), ("SV", sv_va), ps1_free]
                                st_av = P.emit("tensor",
                                        lambda e, lhs=lhs, mt=mt, kc=kc: e.matmul(
                                            PS1[:, mt * 65:(mt + 1) * 65],
                                            lhs, VA[:, kc * 65:(kc + 1) * 65],
                                            start=(mt == 0 and kc == 0), stop=(kc == 15),
                                            skip_group_check=True),
                                        waits=w,
                                        inc=("ST" if (mt == 3 and kc == 15) else None)) or st_av
                        st_av_prev = st_av
                        sv_norm_last = 0
                        for mt in range(4):
                            rz_w = [("ST", st_av)] if mt == 0 else [("SV", sv_norm_last)]
                            rz = P.emit("vector",
                                    lambda e, mt=mt: e.reciprocal(
                                        RZ[:, mt:mt + 1],
                                        PS1[:, mt * 65 + 64: mt * 65 + 65]),
                                    waits=rz_w, inc="SV")
                            vn = P.emit("vector",
                                    lambda e, mt=mt: e.tensor_scalar(
                                        AVN[:, mt * 64:(mt + 1) * 64],
                                        PS1[:, mt * 65: mt * 65 + 64],
                                        RZ[:, mt:mt + 1], None, ALU.mult),
                                    waits=[("SV", rz), avn_free[mt]], inc="SV")
                            sv_norm_last = vn
                            tp = P.emit("tensor",
                                    lambda e, mt=mt, off=off: e.transpose(
                                        PTR[off:off + 64, (mt % 2) * 128:(mt % 2) * 128 + 128],
                                        AVN[:, mt * 64:(mt + 1) * 64],
                                        IDB[:],
                                        tile_position=(0, off)),
                                    waits=[("SV", vn), ("SQ", sq_idb), ptr_free[mt % 2]],
                                    inc="ST")
                            avn_free[mt] = ("ST", tp)
                            dst = AVT[off:off + 64,
                                      ch * T + j * 512 + mt * 128:
                                      ch * T + j * 512 + (mt + 1) * 128]
                            cp = P.emit("vector",
                                    lambda e, dst=dst, mt=mt, off=off: e.tensor_copy(
                                        dst,
                                        PTR[off:off + 64, (mt % 2) * 128:(mt % 2) * 128 + 128]),
                                    waits=[("ST", tp)], inc="SV")
                            ptr_free[mt % 2] = ("SV", cp)
                            avt_done = cp
                        ps1_free = ("SV", avt_done)
                        if h == 0 and j == 0:
                            _DBGREF["PT"] = PT
                            _DBGREF["AVN"] = AVN
                            _ckpt("j0")

                _DBGREF["AVT"] = AVT
                _ckpt("attnav")

                rot = 0
                vacc_hist = []
                for dtl in range(NCH):
                    for tt in range(NTT):
                        slot = rot % 4
                        rot += 1
                        psv = PS4[:, slot * 512:(slot + 1) * 512]
                        for kc in range(NCH):
                            w = []
                            if kc == 0:
                                w = [slot_free[slot], ("SV", avt_done), ("GQ", gq_wrf)]
                                if len(sc_hist) >= 2:
                                    w.append(sc_hist[-2])
                                if len(vacc_hist) >= 2:
                                    w.append(("SV", vacc_hist[-2]))
                            P.emit("tensor",
                                   lambda e, psv=psv, kc=kc, dtl=dtl, tt=tt: e.matmul(
                                       psv,
                                       WRF[:, kc * D + dtl * 128: kc * D + (dtl + 1) * 128],
                                       AVT[:, kc * T + tt * 512: kc * T + (tt + 1) * 512],
                                       start=(kc == 0), stop=False),
                                   waits=w)
                        twr = P.emit("tensor",
                               lambda e, psv=psv, dtl=dtl: e.matmul(
                                   psv, BRR[0:1, dtl * 128:(dtl + 1) * 128],
                                   ONES[0:1, 0:512], start=False, stop=True),
                               waits=[("GQ", gq_brr)], inc="ST")
                        vacc = P.emit("vector",
                               lambda e, psv=psv, dtl=dtl, tt=tt: e.tensor_add(
                                   xres(dtl)[:, tt * 512:(tt + 1) * 512],
                                   xres(dtl)[:, tt * 512:(tt + 1) * 512], psv),
                               waits=[("ST", twr)], inc="SV")
                        slot_free[slot] = ("SV", vacc)
                        vacc_hist.append(vacc)

            barrier()
            _ckpt("attn")

            with ExitStack() as fctx:
                FE = fctx.enter_context
                W1 = FE(nc.sbuf_tensor("W1", [128, NCH * DFF], BF16))
                W2 = FE(nc.sbuf_tensor("W2", [128, 16 * D], BF16))
                WB = FE(nc.sbuf_tensor("WB", [128, T], BF16))
                WROW = FE(nc.sbuf_tensor("WROW", [1, T], BF16))
                B2C = FE(nc.sbuf_tensor("B2C", [128, 16], F32))
                B1C = FE(nc.sbuf_tensor("B1C", [128, 32], F32))
                RWW = FE(nc.sbuf_tensor("RWW", [128, NCH * NE], BF16))
                RBR = FE(nc.sbuf_tensor("RBR", [1, NE], BF16))

                gq_rww = P.emit("gpsimd", lambda e: e.dma_start(out=RWW[:], in_=rww_rl),
                                inc="GQ", inc_n=16)
                gq_rbr = P.emit("gpsimd", lambda e: e.dma_start(out=RBR[:], in_=rwb_row),
                                inc="GQ", inc_n=16)
                gq_w1 = [P.emit("gpsimd", lambda e: e.dma_start(out=W1[:], in_=w1cat[0]),
                                inc="W1S", inc_n=16)]
                gq_w2 = [P.emit("gpsimd", lambda e: e.dma_start(out=W2[:], in_=w2cat[0]),
                                inc="W2S", inc_n=16)]
                gq_b1 = [P.emit("gpsimd",
                                lambda e: e.dma_start(out=B1C[:, 0:16], in_=b1col[0]),
                                inc="B1S", inc_n=16)]
                gq_b2 = [P.emit("gpsimd",
                                lambda e: e.dma_start(out=B2C[:, 0:8], in_=b2col_t[0]),
                                inc="B2S_", inc_n=16)]

                ln2_applied = emit_ln([[] for _ in range(NCH)])
                ln2_done = ln2_applied[-1]
                _ckpt("ln2")

                with ExitStack() as rctx:
                    RE_ = rctx.enter_context
                    RLOG = RE_(nc.sbuf_tensor("RLOG", [8, 512], F32))
                    RT = RE_(nc.sbuf_tensor("RT", [128, 128], F32))
                    REX = RE_(nc.sbuf_tensor("REX", [128, 128], F32))
                    RS4 = RE_(nc.sbuf_tensor("RS4", [128, 64], F32))
                    RS2 = RE_(nc.sbuf_tensor("RS2", [128, 32], F32))
                    RS1 = RE_(nc.sbuf_tensor("RS1", [128, 16], F32))
                    RCPS = RE_(nc.sbuf_tensor("RCPS", [128, 16], F32))
                    RCPR = RE_(nc.sbuf_tensor("RCPR", [128, 128], F32))
                    MRA = RE_(nc.sbuf_tensor("MRA", [128, 128], F32))
                    PMA = RE_(nc.sbuf_tensor("PMA", [128, 128], F32))
                    M2R = RE_(nc.sbuf_tensor("M2R", [128, 128], F32))
                    MSKA = RE_(nc.sbuf_tensor("MSKA", [128, 128], F32))
                    WTA = RE_(nc.sbuf_tensor("WTA", [128, 128], F32))
                    IDF = RE_(nc.sbuf_tensor("IDF", [128, 128], F32))
                    PR = RE_(nc.psum_tensor("PR", [8, 512], F32))
                    PTRK = RE_(nc.psum_tensor("PTRK", [128, 8], F32))
                    PTW = RE_(nc.psum_tensor("PTW", [8, 128], F32))

                    sq_idf = P.emit("sync", lambda e: e.dma_start(out=IDF[:], in_=c_idf),
                                    waits=[("SV", ln2_done), ("ST", C["ST"]),
                                           ("SS", C["SS"])],
                                    inc="SQ", inc_n=16)
                    rt_last = 0
                    pr_free = ("SV", 0)
                    ptrk_free = ("SV", 0)
                    for tt in range(NTT):
                        for kc in range(NCH):
                            w = []
                            if kc == 0:
                                w = [("GQ", gq_rww), ("SV", ln2_applied[kc]), pr_free]
                            elif tt == 0:
                                w = [("SV", ln2_applied[kc])]
                            P.emit("tensor",
                                   lambda e, kc=kc, tt=tt: e.matmul(
                                       PR[:], RWW[:, kc * NE:(kc + 1) * NE],
                                       xln(kc)[:, tt * 512:(tt + 1) * 512],
                                       start=(kc == 0), stop=False),
                                   waits=w)
                        tpr = P.emit("tensor",
                               lambda e: e.matmul(PR[:], RBR[0:1, 0:NE], ONES[0:1, 0:512],
                                                  start=False, stop=True),
                               waits=[("GQ", gq_rbr)], inc="ST")
                        vlog = P.emit("vector",
                               lambda e: e.tensor_copy(RLOG[:], PR[:]),
                               waits=[("ST", tpr)], inc="SV")
                        pr_free = ("SV", vlog)
                        for b in range(4):
                            tk = P.emit("tensor",
                                   lambda e, b=b: e.transpose(
                                       PTRK[:], RLOG[:, b * 128:(b + 1) * 128],
                                       IDF[0:8, 0:8]),
                                   waits=[("SV", vlog), ("SQ", sq_idf), ptrk_free],
                                   inc="ST")
                            g = tt * 4 + b
                            rt_last = P.emit("vector",
                                   lambda e, g=g: e.tensor_copy(
                                       RT[:, g * 8:(g + 1) * 8], PTRK[:]),
                                   waits=[("ST", tk)], inc="SV")
                            ptrk_free = ("SV", rt_last)

                    def r3(t_, n=8):
                        return t_.ap().rearrange("p (g n) -> p g n", n=n)

                    sx = P.emit("scalar",
                                lambda e: e.activation(REX[:], RT[:], AF.Exp),
                                waits=[("SV", rt_last)], inc="SS")
                    c0 = P.emit("vector", lambda e: e.tensor_add(
                            r3(RS4, 4), r3(REX)[:, :, 0:4], r3(REX)[:, :, 4:8]),
                            waits=[("SS", sx)], inc="SV")
                    c1 = P.emit("vector", lambda e: e.tensor_add(
                            r3(RS2, 2), r3(RS4, 4)[:, :, 0:2], r3(RS4, 4)[:, :, 2:4]),
                            waits=[("SV", c0)], inc="SV")
                    c2 = P.emit("vector", lambda e: e.tensor_add(
                            r3(RS1, 1), r3(RS2, 2)[:, :, 0:1], r3(RS2, 2)[:, :, 1:2]),
                            waits=[("SV", c1)], inc="SV")
                    c3 = P.emit("vector", lambda e: e.reciprocal(RCPS[:], RS1[:]),
                                waits=[("SV", c2)], inc="SV")
                    c4 = P.emit("vector", lambda e: e.tensor_copy(
                            r3(RCPR)[:, :, 0:1], r3(RCPS, 1)),
                            waits=[("SV", c3)], inc="SV")
                    c5 = P.emit("vector", lambda e: e.tensor_copy(
                            r3(RCPR)[:, :, 1:2], r3(RCPR)[:, :, 0:1]),
                            waits=[("SV", c4)], inc="SV")
                    c6 = P.emit("vector", lambda e: e.tensor_copy(
                            r3(RCPR)[:, :, 2:4], r3(RCPR)[:, :, 0:2]),
                            waits=[("SV", c5)], inc="SV")
                    c7 = P.emit("vector", lambda e: e.tensor_copy(
                            r3(RCPR)[:, :, 4:8], r3(RCPR)[:, :, 0:4]),
                            waits=[("SV", c6)], inc="SV")
                    m0 = P.emit("vector", lambda e: e.tensor_max(
                            r3(RS4, 4), r3(REX)[:, :, 0:4], r3(REX)[:, :, 4:8]),
                            waits=[("SV", c1)], inc="SV")
                    m1 = P.emit("vector", lambda e: e.tensor_max(
                            r3(RS2, 2), r3(RS4, 4)[:, :, 0:2], r3(RS4, 4)[:, :, 2:4]),
                            waits=[("SV", m0)], inc="SV")
                    m2 = P.emit("vector", lambda e: e.tensor_max(
                            r3(RS1, 1), r3(RS2, 2)[:, :, 0:1], r3(RS2, 2)[:, :, 1:2]),
                            waits=[("SV", m1), ("SV", c3)], inc="SV")
                    mfill = P.emit("vector", lambda e: e.memset(MRA[:], -1e30), inc="SV")
                    mput = P.emit("vector", lambda e: e.tensor_copy(
                            r3(MRA)[:, :, 0:1], r3(RS1, 1)),
                            waits=[("SV", m2), ("SV", mfill)], inc="SV")
                    mr_last = mput
                    for g in range(16):
                        mr_last = P.emit("vector",
                                lambda e, g=g: e.match_replace(
                                    PMA[:, g * 8:(g + 1) * 8], MRA[:, g * 8:(g + 1) * 8],
                                    REX[:, g * 8:(g + 1) * 8], -1e30),
                                waits=([("SV", mput)] if g == 0 else []), inc="SV")
                    n0 = P.emit("vector", lambda e: e.tensor_max(
                            r3(RS4, 4), r3(PMA)[:, :, 0:4], r3(PMA)[:, :, 4:8]),
                            waits=[("SV", mr_last)], inc="SV")
                    n1 = P.emit("vector", lambda e: e.tensor_max(
                            r3(RS2, 2), r3(RS4, 4)[:, :, 0:2], r3(RS4, 4)[:, :, 2:4]),
                            waits=[("SV", n0)], inc="SV")
                    n2 = P.emit("vector", lambda e: e.tensor_max(
                            r3(RS1, 1), r3(RS2, 2)[:, :, 0:1], r3(RS2, 2)[:, :, 1:2]),
                            waits=[("SV", n1)], inc="SV")
                    d0 = P.emit("vector", lambda e: e.tensor_copy(
                            r3(M2R)[:, :, 0:1], r3(RS1, 1)),
                            waits=[("SV", n2)], inc="SV")
                    d1 = P.emit("vector", lambda e: e.tensor_copy(
                            r3(M2R)[:, :, 1:2], r3(M2R)[:, :, 0:1]),
                            waits=[("SV", d0)], inc="SV")
                    d2 = P.emit("vector", lambda e: e.tensor_copy(
                            r3(M2R)[:, :, 2:4], r3(M2R)[:, :, 0:2]),
                            waits=[("SV", d1)], inc="SV")
                    d3 = P.emit("vector", lambda e: e.tensor_copy(
                            r3(M2R)[:, :, 4:8], r3(M2R)[:, :, 0:4]),
                            waits=[("SV", d2)], inc="SV")
                    e0 = P.emit("vector", lambda e: e.tensor_tensor(
                            MSKA[:], REX[:], M2R[:], ALU.is_ge),
                            waits=[("SV", d3)], inc="SV")
                    e1 = P.emit("vector", lambda e: e.tensor_mul(WTA[:], REX[:], MSKA[:]),
                            waits=[("SV", e0)], inc="SV")
                    e2 = P.emit("vector", lambda e: e.tensor_mul(WTA[:], WTA[:], RCPR[:]),
                            waits=[("SV", e1), ("SV", c7)], inc="SV")
                    ptw_free = ("SV", 0)
                    wsb_done = 0
                    for g in range(16):
                        tw = P.emit("tensor",
                                lambda e, g=g: e.transpose(
                                    PTW[:], WTA[:, g * 8:(g + 1) * 8], IDF[:]),
                                waits=[("SV", e2), ptw_free], inc="ST")
                        wsb_done = P.emit("vector",
                                lambda e, g=g: e.tensor_copy(
                                    WSB[:, g * 128:(g + 1) * 128], PTW[:]),
                                waits=[("ST", tw)], inc="SV")
                        ptw_free = ("SV", wsb_done)
                    router_done_sv = wsb_done
                    router_done_st = C["ST"]
                    _ckpt("router")

                H1G = FE(nc.sbuf_tensor("H1G", [128, 2 * 16 * 512], BF16))
                PM1 = FE(nc.psum_tensor("PM1", [128, 2048], F32))
                PM2 = FE(nc.psum_tensor("PM2", [128, 2048], F32))

                sg_hist = []
                vx_hist = []
                pm1_free = {r: ("SV", 0) for r in range(4)}
                pm2_free = {r: ("SV", 0) for r in range(4)}
                h1g_free = {0: ("ST", 0), 1: ("ST", 0)}
                wrow_free = ("ST", 0)
                wb_free = ("SV", 0)
                w1_free = ("ST", 0)
                w2_free = ("ST", 0)
                b1_free = {0: ("SS", 0), 1: ("SS", 0)}
                b2_free = ("ST", 0)
                state = {"rot1": 0, "rot2": 0, "xadd": 0}

                def emit_mm1(e_i, tt, wb_ready, mm1_first_w):
                    buf = tt % 2
                    last_marker = None
                    for ht in range(16):
                        slot = state["rot1"] % 4
                        state["rot1"] += 1
                        psv = PM1[:, slot * 512:(slot + 1) * 512]
                        tmm = 0
                        for kc in range(NCH):
                            w = []
                            if kc == 0:
                                w = [pm1_free[slot]]
                                if ht == 0:
                                    w = w + mm1_first_w
                            r = P.emit("tensor",
                                   lambda e, psv=psv, kc=kc, ht=ht, tt=tt: e.matmul(
                                       psv,
                                       W1[:, kc * DFF + ht * 128: kc * DFF + (ht + 1) * 128],
                                       xln(kc)[:, tt * 512:(tt + 1) * 512],
                                       start=(kc == 0), stop=(kc == NCH - 1)),
                                   waits=w,
                                   inc=("ST" if kc == NCH - 1 else None))
                            if r is not None:
                                tmm = r
                        h1sl = H1G[:, buf * 8192 + ht * 512: buf * 8192 + (ht + 1) * 512]
                        gw = [("ST", tmm), ("B1S", gq_b1[e_i])]
                        if ht == 0:
                            gw.append(h1g_free[buf])
                        if e_i == 0 and tt == 0 and ht == 0:
                            gw += [("SV", router_done_sv), ("ST", router_done_st)]
                        if DBG["evac"] == "gelu":
                            sg = P.emit("scalar",
                                   lambda e, h1sl=h1sl, psv=psv, e_i=e_i, ht=ht: e.activation(
                                       h1sl, psv, AF.Gelu,
                                       bias=B1C[:, (e_i % 2) * 16 + ht:
                                                (e_i % 2) * 16 + ht + 1]),
                                   waits=gw, inc="SS")
                        elif DBG["evac"] == "gelunb":
                            sg = P.emit("scalar",
                                   lambda e, h1sl=h1sl, psv=psv: e.activation(
                                       h1sl, psv, AF.Gelu),
                                   waits=gw, inc="SS")
                        elif DBG["evac"].startswith("gelun_"):
                            lim = int(DBG["evac"].split("_")[1])
                            if e_i == 0 and (tt * 16 + ht) < lim:
                                sg = P.emit("scalar",
                                       lambda e, h1sl=h1sl, psv=psv: e.activation(
                                           h1sl, psv, AF.Gelu),
                                       waits=gw, inc="SS")
                            else:
                                sg = P.emit("scalar",
                                       lambda e, h1sl=h1sl, psv=psv: e.copy(h1sl, psv),
                                       waits=gw, inc="SS")
                        elif DBG["evac"] == "gelu1":
                            if e_i == 0 and tt == 0 and ht == 0:
                                sg = P.emit("scalar",
                                       lambda e, h1sl=h1sl, psv=psv: e.activation(
                                           h1sl, psv, AF.Gelu),
                                       waits=gw, inc="SS")
                            else:
                                sg = P.emit("scalar",
                                       lambda e, h1sl=h1sl, psv=psv: e.copy(h1sl, psv),
                                       waits=gw, inc="SS")
                        elif DBG["evac"] == "tinygelu":
                            sg = P.emit("scalar",
                                   lambda e, h1sl=h1sl, psv=psv: e.copy(h1sl, psv),
                                   waits=gw, inc="SS")
                            if e_i == 0 and tt == 0 and ht == 0:
                                sg = P.emit("scalar",
                                       lambda e: e.activation(
                                           DUMMY[0:1, 0:2], DUMMY[0:1, 4:6], AF.Gelu),
                                       inc="SS")
                        else:
                            sg = P.emit("scalar",
                                   lambda e, h1sl=h1sl, psv=psv: e.copy(h1sl, psv),
                                   waits=gw, inc="SS")
                        pm1_free[slot] = ("SS", sg)
                        sg_hist.append(sg)
                        last_marker = ("SS", sg)
                        if e_i > 0:
                            vw = P.emit("vector",
                                   lambda e, h1sl=h1sl, tt=tt: e.tensor_mul(
                                       h1sl, h1sl, WB[:, tt * 512:(tt + 1) * 512]),
                                   waits=[("SS", sg), wb_ready], inc="SV")
                            last_marker = ("SV", vw)
                    return last_marker

                def emit_mm2(e_i, tt, h1_ready):
                    buf = tt % 2
                    last_st = 0
                    for dtl in range(NCH):
                        slot = state["rot2"] % 4
                        state["rot2"] += 1
                        psv = PM2[:, slot * 512:(slot + 1) * 512]
                        tr = 0
                        for kc in range(16):
                            w = []
                            if kc == 0:
                                w = [pm2_free[slot], h1_ready]
                                if dtl == 0 and tt == 0:
                                    w.append(("W2S", gq_w2[e_i]))
                            r = P.emit("tensor",
                                   lambda e, psv=psv, kc=kc, dtl=dtl, buf=buf: e.matmul(
                                       psv,
                                       W2[:, kc * D + dtl * 128: kc * D + (dtl + 1) * 128],
                                       H1G[:, buf * 8192 + kc * 512:
                                           buf * 8192 + (kc + 1) * 512],
                                       start=(kc == 0), stop=(kc == 15)),
                                   waits=w,
                                   inc=("ST" if kc == 15 else None))
                            if r is not None:
                                tr = r
                        last_st = tr
                        wsrc = (WB[:, tt * 512:(tt + 1) * 512] if e_i > 0
                                else ONES[:, 0:512])
                        b2ap = B2C[:, (e_i % 2) * 8 + dtl:(e_i % 2) * 8 + dtl + 1]
                        vx = P.emit("vector",
                               lambda e, psv=psv, dtl=dtl, tt=tt: e.tensor_add(
                                   xres(dtl)[:, tt * 512:(tt + 1) * 512],
                                   xres(dtl)[:, tt * 512:(tt + 1) * 512], psv),
                               waits=[("ST", tr)], inc="SV")
                        vb2 = P.emit("vector",
                               lambda e, wsrc=wsrc, b2ap=b2ap, dtl=dtl, tt=tt:
                                   e.scalar_tensor_tensor(
                                       xres(dtl)[:, tt * 512:(tt + 1) * 512],
                                       wsrc, b2ap,
                                       xres(dtl)[:, tt * 512:(tt + 1) * 512],
                                       ALU.mult, ALU.add),
                               waits=[("SV", vx), ("B2S_", gq_b2[e_i])], inc="SV")
                        pm2_free[slot] = ("SV", vx)
                        vx_hist.append(vx)
                        state["xadd"] = vb2
                    return ("ST", last_st)

                for e_i in range(min(NEXP, DBG["nexp"])):
                    if e_i > 0:
                        gq_w1.append(P.emit("gpsimd",
                                lambda e, e_i=e_i: e.dma_start(out=W1[:], in_=w1cat[e_i]),
                                waits=[w1_free], inc="W1S", inc_n=16))
                        gq_wrow = P.emit("gpsimd",
                                lambda e, e_i=e_i: e.dma_start(
                                    out=WROW[:], in_=WSB[e_i - 1:e_i, :]),
                                waits=[("SV", router_done_sv), wrow_free],
                                inc="WRS", inc_n=16)
                        gq_b1.append(P.emit("gpsimd",
                                lambda e, e_i=e_i: e.dma_start(
                                    out=B1C[:, (e_i % 2) * 16:(e_i % 2) * 16 + 16],
                                    in_=b1col[e_i]),
                                waits=[b1_free[e_i % 2]], inc="B1S", inc_n=16))
                        wb_evac = 0
                        for r in range(4):
                            tb_ = P.emit("tensor",
                                    lambda e, r=r: e.matmul(
                                        PM1[:, r * 512:(r + 1) * 512],
                                        ONES[0:1, 0:128],
                                        WROW[0:1, r * 512:(r + 1) * 512],
                                        start=True, stop=True),
                                    waits=[("WRS", gq_wrow), pm1_free[r]],
                                    inc="ST")
                            wb_evac = P.emit("vector",
                                    lambda e, r=r: e.tensor_copy(
                                        WB[:, r * 512:(r + 1) * 512],
                                        PM1[:, r * 512:(r + 1) * 512]),
                                    waits=[("ST", tb_), wb_free], inc="SV")
                            pm1_free[r] = ("SV", wb_evac)
                        wrow_free = ("ST", tb_)
                        wb_ready = ("SV", wb_evac)
                    else:
                        wb_ready = ("SV", 0)

                    mm1_first_w = [("W1S", gq_w1[e_i])]
                    if e_i == 0:
                        mm1_first_w += [("SV", ln2_done),
                                        ("SV", router_done_sv),
                                        ("ST", router_done_st),
                                        ("SS", C["SS"])]

                    h1_ready = {}
                    h1_ready[0] = emit_mm1(e_i, 0, wb_ready, mm1_first_w)
                    if e_i > 0:
                        gq_b2.append(P.emit("gpsimd",
                                lambda e, e_i=e_i: e.dma_start(
                                    out=B2C[:, (e_i % 2) * 8:(e_i % 2) * 8 + 8],
                                    in_=b2col_t[e_i]),
                                waits=[b2_free], inc="B2S_", inc_n=16))
                        gq_w2.append(P.emit("gpsimd",
                                lambda e, e_i=e_i: e.dma_start(out=W2[:], in_=w2cat[e_i]),
                                waits=[w2_free], inc="W2S", inc_n=16))
                    if DBG["stop"] == "mm1" and e_i == 0:
                        h1_ready[1] = emit_mm1(e_i, 1, wb_ready, mm1_first_w)
                        _DBGREF["H1G"] = H1G
                        _ckpt("mm1")
                    h1_ready[1] = emit_mm1(e_i, 1, wb_ready, mm1_first_w)
                    h1g_free[0] = emit_mm2(e_i, 0, h1_ready[0])
                    h1_ready[2] = emit_mm1(e_i, 2, wb_ready, mm1_first_w)
                    h1g_free[1] = emit_mm2(e_i, 1, h1_ready[1])
                    h1_ready[3] = emit_mm1(e_i, 3, wb_ready, mm1_first_w)
                    w1_free = ("ST", C["ST"])
                    h1g_free[0] = emit_mm2(e_i, 2, h1_ready[2])
                    h1g_free[1] = emit_mm2(e_i, 3, h1_ready[3])

                    w2_free = h1g_free[1]
                    wb_free = ("SV", state["xadd"])
                    b1_free[e_i % 2] = ("SS", C["SS"])
                    b2_free = ("SV", state["xadd"])


        except _PhaseDone:
            DBG["_aborted"] = True

        barrier()
        if DBG["dump"] == "h1g":
            H1G_ = _DBGREF["H1G"]
            gq_out = 0
            dw = [("SS", C["SS"]), ("SV", C["SV"]), ("ST", C["ST"])]
            for c in range(NCH):
                gq_out = P.emit("gpsimd",
                       lambda e, c=c: e.dma_start(
                           out=ot[c * 128:(c + 1) * 128, :],
                           in_=H1G_[:, c * 2048:(c + 1) * 2048]),
                       waits=(dw if c == 0 else []),
                       inc="GQ", inc_n=16)
            P.emit("gpsimd", lambda e: e.nop(), waits=[("GQ", gq_out)])
        elif DBG["dump"] == "qkv":
            QT_, KT_, VA_ = _DBGREF["QT"], _DBGREF["KT"], _DBGREF["VA"]
            gq_out = P.emit("gpsimd", lambda e: e.dma_start(
                out=ot[0:64, :], in_=QT_[0:64, :]), inc="GQ", inc_n=16)
            gq_out = P.emit("gpsimd", lambda e: e.dma_start(
                out=ot[64:128, :], in_=KT_[0:64, :]), inc="GQ", inc_n=16)
            gq_out = P.emit("gpsimd", lambda e: e.dma_start(
                out=ot[128:256, 0:1040], in_=VA_[:, :]), inc="GQ", inc_n=16)
            P.emit("gpsimd", lambda e: e.nop(), waits=[("GQ", gq_out)])
        elif DBG["dump"] == "j0":
            PT_, AVN_ = _DBGREF["PT"], _DBGREF["AVN"]
            gq_out = 0
            for b in range(4):
                gq_out = P.emit("gpsimd", lambda e, b=b: e.dma_start(
                    out=ot[b * 128:(b + 1) * 128, :],
                    in_=PT_[:, b * 2048:(b + 1) * 2048]), inc="GQ", inc_n=16)
            gq_out = P.emit("gpsimd", lambda e: e.dma_start(
                out=ot[512:640, 0:256], in_=AVN_[:, :]), inc="GQ", inc_n=16)
            P.emit("gpsimd", lambda e: e.nop(), waits=[("GQ", gq_out)])
        elif DBG["dump"] == "avt":
            AVT_ = _DBGREF["AVT"]
            gq_out = 0
            for c in range(NCH):
                gq_out = P.emit("gpsimd",
                       lambda e, c=c: e.dma_start(
                           out=ot[c * 128:(c + 1) * 128, :],
                           in_=AVT_[:, c * T:(c + 1) * T]),
                       inc="GQ", inc_n=16)
            P.emit("gpsimd", lambda e: e.nop(), waits=[("GQ", gq_out)])
        elif DBG["dump"] == "xln":
            gq_out = 0
            for c in range(NCH):
                gq_out = P.emit("gpsimd",
                       lambda e, c=c: e.dma_start(out=ot[c * 128:(c + 1) * 128, :],
                                                  in_=xln(c)),
                       inc="GQ", inc_n=16)
            P.emit("gpsimd", lambda e: e.nop(), waits=[("GQ", gq_out)])
        else:
            sq_out = 0
            for c in range(NCH):
                sq_out = P.emit("sync",
                       lambda e, c=c: e.dma_start(out=ot[c * 128:(c + 1) * 128, :],
                                                  in_=xres(c)),
                       inc="SQ", inc_n=16)
            P.emit("sync", lambda e: e.nop(), waits=[("SQ", sq_out)])

        def _preload_both(e):
            try:
                from concourse.hw_specs import get_activation_tables
                for idx, funcs in enumerate(
                        get_activation_tables(nc.m.arch).values()):
                    if AF.Exp in funcs and AF.Ln in funcs:
                        e.add_instruction(mybir.InstLoadActFuncSet(
                            name=nc.get_next_instruction_name(),
                            act_func_set_id=idx, ins=[], outs=[]))
                        return
            except Exception:
                pass

        with nc.Block() as block:
            @block.tensor
            def _(e):
                for f in P.ops["tensor"]:
                    f(e)

            @block.vector
            def _(e):
                for f in P.ops["vector"]:
                    f(e)

            @block.scalar
            def _(e):
                for f in P.ops["scalar"]:
                    f(e)

            @block.sync
            def _(e):
                for f in P.ops["sync"]:
                    f(e)

            @block.gpsimd
            def _(e):
                for f in P.ops["gpsimd"]:
                    f(e)

        if DBG.get("_aborted"):
            ctx.pop_all()
            DBG["_aborted"] = False

    return nc


_CACHE = {}


def _host_layout(inputs):
    f32 = np.float32
    g = lambda k: np.asarray(inputs[k], dtype=f32)
    x = g("L0_attn_x")
    wqkv = g("L0_attn_Wqkv")
    bqkv = g("L0_attn_bqkv")
    wr = g("L0_attn_Wr")
    br = g("L0_attn_br")
    rww = g("L0_ffn_router_w")
    rwb = g("L0_ffn_router_b")
    sw1 = g("L0_ffn_sw1")
    sb1 = g("L0_ffn_sb1")
    sw2 = g("L0_ffn_sw2")
    sb2 = g("L0_ffn_sb2")
    rw1 = g("L0_ffn_rw1")
    rb1 = g("L0_ffn_rb1")
    rw2 = g("L0_ffn_rw2")
    rb2 = g("L0_ffn_rb2")

    cc = np.ascontiguousarray
    com = {}
    com["wqkv2"] = cc(np.concatenate([wqkv, wqkv], axis=0))
    com["bqkv_row"] = cc(bqkv.reshape(1, 192))
    com["bqkv_col"] = cc(np.stack([bqkv[0:64], bqkv[64:128]], axis=1))
    com["wr_rl"] = cc(wr.reshape(8, 128, D).transpose(1, 0, 2).reshape(128, 8 * D))
    com["br_row"] = cc(br.reshape(1, D))
    com["rww_rl"] = cc(rww.reshape(8, 128, NE).transpose(1, 0, 2).reshape(128, 8 * NE))
    com["rwb_row"] = cc(rwb.reshape(1, NE))
    w1 = np.concatenate([sw1[None], rw1], axis=0)
    w2 = np.concatenate([sw2[None], rw2], axis=0)
    com["w1cat"] = cc(w1.reshape(NEXP, 8, 128, DFF).transpose(0, 2, 1, 3)
                      .reshape(NEXP, 128, 8 * DFF))
    com["w2cat"] = cc(w2.reshape(NEXP, 16, 128, D).transpose(0, 2, 1, 3)
                      .reshape(NEXP, 128, 16 * D))
    b1 = np.concatenate([sb1[None], rb1], axis=0)
    com["b1col"] = cc(b1.reshape(NEXP, 16, 128).transpose(0, 2, 1))
    com["b2row"] = cc(np.concatenate([sb2[None], rb2], axis=0)
                      .reshape(NEXP, 1, D))
    b2all = np.concatenate([sb2[None], rb2], axis=0)
    com["b2col_t"] = cc(b2all.reshape(NEXP, 8, 128).transpose(0, 2, 1))

    bf = ml_dtypes.bfloat16
    com["c_ones"] = np.ones((128, 512), bf)
    com["c_idb"] = np.eye(128, dtype=bf)
    com["c_idf"] = np.eye(128, dtype=f32)
    p = np.arange(128)[:, None]
    fidx = np.arange(512)[None, :]
    m = np.zeros((128, 4 * 512), f32)
    for d in range(4):
        m[:, d * 512:(d + 1) * 512] = ((p + 128 * d) < fidx).astype(f32)
    com["c_masks"] = m
    com["c_eps"] = np.full((1, 1), 1e-5, f32)
    com["c_rcp1024"] = np.full((1, 1), 1.0 / 1024.0, f32)

    in_maps = []
    for c in range(8):
        mp = dict(com)
        mp["xt"] = cc(x[c].T)
        in_maps.append(mp)
    return in_maps


def kernel(**inputs):
    in_maps = _host_layout(inputs)
    if "nc" not in _CACHE:
        _CACHE["nc"] = build_program()
    nc = _CACHE["nc"]
    res = run_bass_kernel_spmd(nc, in_maps, core_ids=list(range(8)))
    out = np.stack([np.asarray(r["ot"]).T for r in res.results], axis=0)
    return out.astype(np.float32)


# version 11
# speedup vs baseline: 1.0559x; 1.0029x over previous
import numpy as np
import ml_dtypes
from contextlib import ExitStack
import concourse.bass as bass
import concourse.mybir as mybir
from concourse.bass_utils import run_bass_kernel_spmd

F32 = mybir.dt.float32
BF16 = mybir.dt.bfloat16
AF = mybir.ActivationFunctionType
ALU = mybir.AluOpType
AX = mybir.AxisListType

T = 2048
D = 1024
H = 16
HE = 64
DFF = 2048
NE = 8
NEXP = 9
NCH = D // 128
NTT = T // 512
NMT = T // 128

ENGS = ("tensor", "vector", "scalar", "sync", "gpsimd")

DBG = {"stop": "out", "nexp": NEXP, "dump": "xres", "nheads": H, "qkv_parts": "all", "evac": "gelu"}
_DBGREF = {}


class _PhaseDone(Exception):
    pass


def _ckpt(ph):
    if DBG["stop"] == ph:
        raise _PhaseDone()


class Plan:
    def __init__(self, sems):
        self.ops = {e: [] for e in ENGS}
        self.cnt = {k: 0 for k in sems}
        self.sems = sems
        self.tail_inc = {e: True for e in ENGS}

    def emit(self, eng, fn, waits=(), inc=None, inc_n=1, neutral=False):
        waits = tuple((s, self.cnt[s] if s in ("SQ", "GQ") else v)
                      for (s, v) in waits)
        waits = tuple((s, v) for (s, v) in waits if v > 0)
        for s, v in waits:
            assert v <= self.cnt[s], f"wait on future count: {s} {v} > {self.cnt[s]}"
        if inc is not None:
            self.cnt[inc] += inc_n
        sems = self.sems

        def run(e, _w=waits, _f=fn, _i=inc, _n=inc_n):
            for s, v in _w:
                e.wait_ge(sems[s], v)
            inst = _f(e)
            if _i is not None:
                inst.then_inc(sems[_i], _n)
        self.ops[eng].append(run)
        if not neutral:
            self.tail_inc[eng] = inc is not None
        return self.cnt[inc] if inc is not None else None


def build_program():
    nc = bass.Bass("TRN2", target_bir_lowering=False)
    dt = nc.dram_tensor
    xt = dt("xt", [D, T], F32, kind="ExternalInput").ap()
    wqkv2 = dt("wqkv2", [128, 192], F32, kind="ExternalInput").ap()
    bqkv_row = dt("bqkv_row", [1, 192], F32, kind="ExternalInput").ap()
    bqkv_col = dt("bqkv_col", [64, 2], F32, kind="ExternalInput").ap()
    wr_rl = dt("wr_rl", [128, NCH * D], F32, kind="ExternalInput").ap()
    br_row = dt("br_row", [1, D], F32, kind="ExternalInput").ap()
    rww_rl = dt("rww_rl", [128, NCH * NE], F32, kind="ExternalInput").ap()
    rwb_row = dt("rwb_row", [1, NE], F32, kind="ExternalInput").ap()
    w1cat = dt("w1cat", [NEXP, 128, NCH * DFF], F32, kind="ExternalInput").ap()
    w2cat = dt("w2cat", [NEXP, 128, (DFF // 128) * D], F32, kind="ExternalInput").ap()
    b1col = dt("b1col", [NEXP, 128, 16], F32, kind="ExternalInput").ap()
    b2row = dt("b2row", [NEXP, 1, D], F32, kind="ExternalInput").ap()
    b2col_t = dt("b2col_t", [NEXP, 128, 8], F32, kind="ExternalInput").ap()
    c_ones = dt("c_ones", [128, 512], BF16, kind="ExternalInput").ap()
    c_idb = dt("c_idb", [128, 128], BF16, kind="ExternalInput").ap()
    c_idf = dt("c_idf", [128, 128], F32, kind="ExternalInput").ap()
    c_masks = dt("c_masks", [128, 4 * 512], BF16, kind="ExternalInput").ap()
    c_eps = dt("c_eps", [1, 1], F32, kind="ExternalInput").ap()
    c_rcp1024 = dt("c_rcp1024", [1, 1], F32, kind="ExternalInput").ap()
    ot = dt("ot", [D, T], F32, kind="ExternalOutput").ap()

    with ExitStack() as ctx:
        E = ctx.enter_context
        sems = {
            "ST": E(nc.semaphore("ST")),
            "SV": E(nc.semaphore("SV")),
            "SS": E(nc.semaphore("SS")),
            "SQ": E(nc.semaphore("SQ")),
            "GQ": E(nc.semaphore("GQ")),
            "W1S": E(nc.semaphore("W1S")),
            "W2S": E(nc.semaphore("W2S")),
            "WRS": E(nc.semaphore("WRS")),
            "B1S": E(nc.semaphore("B1S")),
            "B2S_": E(nc.semaphore("B2S_")),
        }
        P = Plan(sems)
        C = P.cnt

        XRES = E(nc.sbuf_tensor("XRES", [128, NCH * T], F32))
        XLN = E(nc.sbuf_tensor("XLN", [128, NCH * T], BF16))
        ONES = E(nc.sbuf_tensor("ONES", [128, 512], BF16))
        WSB = E(nc.sbuf_tensor("WSB", [8, T], BF16))
        WQ = E(nc.sbuf_tensor("WQ", [128, 192], BF16))
        EPS = E(nc.sbuf_tensor("EPS", [1, 1], F32))
        RCP1024 = E(nc.sbuf_tensor("RCP1024", [1, 1], F32))
        DUMMY = E(nc.sbuf_tensor("DUMMY", [1, 8], F32))

        nc.const_aps.register(F32, 1.0 / 1024.0, RCP1024[0:1, 0:1])

        def xres(c):
            return XRES[:, c * T:(c + 1) * T]

        def xln(c):
            return XLN[:, c * T:(c + 1) * T]

        def barrier():
            for eng in ("tensor", "vector", "scalar"):
                assert P.tail_inc[eng], f"barrier: {eng} tail lacks inc"
            w = [(k, C[k]) for k in C]
            for eng in ENGS:
                P.emit(eng, lambda e: e.nop(), waits=w, neutral=True)

        sq_x = []
        for c in range(NCH):
            sq_x.append(P.emit(
                "sync",
                lambda e, c=c: e.dma_start(out=xres(c), in_=xt[c * 128:(c + 1) * 128, :]),
                inc="SQ", inc_n=16))
        sq_ones = P.emit("sync", lambda e: e.dma_start(out=ONES[:], in_=c_ones),
                         inc="SQ", inc_n=16)
        sq_eps = P.emit("sync", lambda e: e.dma_start(out=EPS[:], in_=c_eps),
                        inc="SQ", inc_n=16)
        sq_rcp = P.emit("sync", lambda e: e.dma_start(out=RCP1024[:], in_=c_rcp1024),
                        inc="SQ", inc_n=16)
        gq_wq = P.emit("gpsimd", lambda e: e.dma_start(out=WQ[:], in_=wqkv2),
                       inc="GQ", inc_n=16)
        P.emit("scalar", lambda e: e.memzero(DUMMY[:]), inc="SS")

        _ln_n = [0]

        def emit_ln(x_waits):
            _ln_n[0] += 1
            sfx = str(_ln_n[0])
            with ExitStack() as lctx:
                LE = lctx.enter_context
                XSQ = LE(nc.sbuf_tensor("XSQ" + sfx, [128, T], BF16))
                MROW = LE(nc.sbuf_tensor("MROW" + sfx, [1, T], BF16))
                VROW = LE(nc.sbuf_tensor("VROW" + sfx, [1, T], BF16))
                LROW = LE(nc.sbuf_tensor("LROW" + sfx, [1, T], BF16))
                RSTD = LE(nc.sbuf_tensor("RSTD" + sfx, [1, T], BF16))
                MRSTD = LE(nc.sbuf_tensor("MRSTD" + sfx, [1, T], BF16))
                B2SB = LE(nc.sbuf_tensor("B2SB" + sfx, [128, T], BF16))
                with ExitStack() as pctx:
                    PSM = pctx.enter_context(nc.psum_tensor("PSM" + sfx, [1, T], F32))
                    PSQ = pctx.enter_context(nc.psum_tensor("PSQ" + sfx, [1, T], F32))
                    sq_done_prev = 0
                    for c in range(NCH):
                        vc = P.emit("vector",
                                    lambda e, c=c: e.tensor_copy(xln(c), xres(c)),
                                    waits=list(x_waits[c]), inc="SV")
                        vs = P.emit("vector",
                                    lambda e, c=c: e.tensor_mul(XSQ[:], xln(c), xln(c)),
                                    waits=[("SV", vc), ("ST", sq_done_prev)],
                                    inc="SV")
                        for tt in range(NTT):
                            P.emit("tensor",
                                   lambda e, c=c, tt=tt: e.matmul(
                                       PSM[0:1, tt * 512:(tt + 1) * 512],
                                       ONES[:, 0:1],
                                       xln(c)[:, tt * 512:(tt + 1) * 512],
                                       start=(c == 0), stop=(c == NCH - 1)),
                                   waits=([("SV", vc), ("SQ", sq_ones)] if tt == 0 else []))
                        for tt in range(NTT):
                            r = P.emit("tensor",
                                   lambda e, c=c, tt=tt: e.matmul(
                                       PSQ[0:1, tt * 512:(tt + 1) * 512],
                                       ONES[:, 0:1],
                                       XSQ[:, tt * 512:(tt + 1) * 512],
                                       start=(c == 0), stop=(c == NCH - 1)),
                                   waits=([("SV", vs)] if tt == 0 else []),
                                   inc=("ST" if tt == NTT - 1 else None))
                            if r is not None:
                                sq_done_prev = r
                    st_stats = sq_done_prev
                    v1 = P.emit("vector",
                                lambda e: e.tensor_scalar(MROW[:], PSM[:], 1.0 / 1024.0,
                                                          None, ALU.mult),
                                waits=[("ST", st_stats), ("SQ", sq_rcp)], inc="SV")
                    s1 = P.emit("scalar",
                                lambda e: e.activation(VROW[:], MROW[:], AF.Square),
                                waits=[("SV", v1)], inc="SS")
                    v2 = P.emit("vector",
                                lambda e: e.scalar_tensor_tensor(
                                    VROW[:], PSQ[:], 1.0 / 1024.0, VROW[:],
                                    ALU.mult, ALU.subtract),
                                waits=[("SS", s1)], inc="SV")
                    s2 = P.emit("scalar",
                                lambda e: e.activation(LROW[:], VROW[:], AF.Ln,
                                                       bias=EPS[0:1, 0:1]),
                                waits=[("SV", v2), ("SQ", sq_eps)], inc="SS")
                    s3 = P.emit("scalar",
                                lambda e: e.activation(RSTD[:], LROW[:], AF.Exp,
                                                       scale=-0.5),
                                waits=[("SS", s2)], inc="SS")
                    v3 = P.emit("vector",
                                lambda e: e.tensor_mul(MRSTD[:], MROW[:], RSTD[:]),
                                waits=[("SS", s3)], inc="SV")
                with ExitStack() as pctx:
                    PB1 = pctx.enter_context(nc.psum_tensor("PB1" + sfx, [128, T], F32))
                    PB2 = pctx.enter_context(nc.psum_tensor("PB2" + sfx, [128, T], F32))
                    tb = []
                    for tt in range(NTT):
                        P.emit("tensor",
                               lambda e, tt=tt: e.matmul(
                                   PB1[:, tt * 512:(tt + 1) * 512],
                                   ONES[0:1, 0:128],
                                   RSTD[0:1, tt * 512:(tt + 1) * 512],
                                   start=True, stop=True),
                               waits=([("SS", s3), ("SV", v2)] if tt == 0 else []))
                        tb.append(P.emit("tensor",
                               lambda e, tt=tt: e.matmul(
                                   PB2[:, tt * 512:(tt + 1) * 512],
                                   ONES[0:1, 0:128],
                                   MRSTD[0:1, tt * 512:(tt + 1) * 512],
                                   start=True, stop=True),
                               waits=([("SV", v3)] if tt == 0 else []), inc="ST"))
                    cb1 = P.emit("vector",
                            lambda e: e.tensor_copy(XSQ[:], PB1[:]),
                            waits=[("ST", tb[3])], inc="SV")
                    cb2 = P.emit("vector",
                            lambda e: e.tensor_copy(B2SB[:], PB2[:]),
                            waits=[("SV", cb1)], inc="SV")
                    applied = []
                    prev = cb2
                    for c in range(NCH):
                        va = P.emit("vector",
                                    lambda e, c=c: e.tensor_mul(
                                        xln(c), xres(c), XSQ[:]),
                                    waits=[("SV", prev)], inc="SV")
                        prev = P.emit("vector",
                                    lambda e, c=c: e.tensor_sub(
                                        xln(c), xln(c), B2SB[:]),
                                    waits=[("SV", va)], inc="SV")
                        applied.append(prev)
                    return applied

        try:
            ln1_applied = emit_ln([[("SQ", sq_x[c])] for c in range(NCH)])
            ln1_done = ln1_applied[-1]
            _ckpt("ln1")

            barrier()
            _ckpt("bar1")

            with ExitStack() as actx:
                AE = actx.enter_context
                QT = AE(nc.sbuf_tensor("QT", [64, T], BF16))
                KT = AE(nc.sbuf_tensor("KT", [64, T], BF16))
                VA = AE(nc.sbuf_tensor("VA", [128, 16 * 65], BF16))
                PTA = AE(nc.sbuf_tensor("PTA", [128, 16 * 512], BF16))
                PTB = AE(nc.sbuf_tensor("PTB", [128, 16 * 512], BF16))
                AVT = AE(nc.sbuf_tensor("AVT", [128, NCH * T], BF16))
                WRF = AE(nc.sbuf_tensor("WRF", [128, NCH * D], BF16))
                MASKS = AE(nc.sbuf_tensor("MASKS", [128, 4 * 512], BF16))
                SMASK = AE(nc.sbuf_tensor("SMASK", [128, 2 * 512], F32))
                AVN = AE(nc.sbuf_tensor("AVN", [128, 4 * 64], BF16))
                RZ = AE(nc.sbuf_tensor("RZ", [128, 4], F32))
                IDB = AE(nc.sbuf_tensor("IDB", [128, 128], BF16))
                BQKVC = AE(nc.sbuf_tensor("BQKVC", [64, 2], F32))
                BQKVR = AE(nc.sbuf_tensor("BQKVR", [1, 192], BF16))
                BRR = AE(nc.sbuf_tensor("BRR", [1, D], BF16))
                PS4 = AE(nc.psum_tensor("PS4", [128, 2048], F32))
                PS2 = AE(nc.psum_tensor("PS2", [128, 1024], F32))
                PS1 = AE(nc.psum_tensor("PS1", [128, 260], F32))
                PTR = AE(nc.psum_tensor("PTR", [128, 256], BF16))

                sq_masks = P.emit("sync", lambda e: e.dma_start(out=MASKS[:], in_=c_masks),
                                  inc="SQ", inc_n=16)
                sq_idb = P.emit("sync", lambda e: e.dma_start(out=IDB[:], in_=c_idb),
                                inc="SQ", inc_n=16)
                gq_wrf = P.emit("gpsimd", lambda e: e.dma_start(out=WRF[:], in_=wr_rl),
                                inc="GQ", inc_n=16)
                gq_bqc = P.emit("gpsimd", lambda e: e.dma_start(out=BQKVC[:], in_=bqkv_col),
                                inc="GQ", inc_n=16)
                gq_bqr = P.emit("gpsimd", lambda e: e.dma_start(out=BQKVR[:], in_=bqkv_row),
                                inc="GQ", inc_n=16)
                gq_brr = P.emit("gpsimd", lambda e: e.dma_start(out=BRR[:], in_=br_row),
                                inc="GQ", inc_n=16)

                sv_vones = P.emit("vector",
                                  lambda e: e.memset(
                                      VA.ap().rearrange("p (g n) -> p g n", n=65)[:, :, 64:65],
                                      1.0),
                                  inc="SV")

                if DBG["stop"] == "adma":
                    gg = P.emit("gpsimd", lambda e: e.nop(),
                                waits=[("GQ", C["GQ"]), ("SQ", C["SQ"])])
                    _DBGREF["QT"], _DBGREF["KT"], _DBGREF["VA"] = QT, KT, VA
                    _ckpt("adma")
                slot_free = {r: ("ST", 0) for r in range(6)}
                ps2_free = ("SV", 0)
                ps1_free = ("SV", 0)
                ptr_free = {0: ("SV", 0), 1: ("SV", 0)}
                avn_free = {m: ("ST", 0) for m in range(4)}
                ss_exp_last = 0
                sc_hist = []
                st_av_prev = {0: 0, 1: 0}
                jglob = [0]
                avt_done = 0

                for h in range(min(H, DBG["nheads"])):
                    ch, off = h // 2, (h % 2) * 64
                    xh = XLN[off:off + 64, ch * T:(ch + 1) * T]

                    ev_q, ev_k = [], []
                    for s in (range(NTT) if DBG["qkv_parts"] in ("all", "qk") else []):
                        tsl = slice(s * 512, (s + 1) * 512)
                        for which in ("q", "k"):
                            base = 0 if which == "q" else 1024
                            slot_id = (0 if which == "q" else 2) + s % 2
                            psv = PS4[0:64, base + (s % 2) * 512: base + (s % 2) * 512 + 512]
                            wcol = 0 if which == "q" else 64
                            evs = ev_q if which == "q" else ev_k
                            if s >= 2:
                                w = [("SS", evs[s - 2])]
                            else:
                                w = [slot_free[slot_id]]
                            if s == 0 and which == "q":
                                w += [("SV", ln1_applied[ch]), ("GQ", gq_wq)]
                            tq = P.emit("tensor",
                                        lambda e, psv=psv, wcol=wcol, tsl=tsl, xh=xh,
                                        off=off: e.matmul(
                                            psv, WQ[off:off + 64, wcol:wcol + 64],
                                            xh[:, tsl],
                                            start=True, stop=True),
                                        waits=w, inc="ST")
                            dst = (QT if which == "q" else KT)[0:64, tsl]
                            bias = BQKVC[0:64, 0:1] if which == "q" else BQKVC[0:64, 1:2]
                            evs.append(P.emit("scalar",
                                        lambda e, dst=dst, psv=psv, bias=bias: e.activation(
                                            dst, psv, AF.Identity, bias=bias),
                                        waits=[("ST", tq), ("GQ", gq_bqc)], inc="SS"))
                    tv_last = 0
                    for mt in (range(NMT) if DBG["qkv_parts"] in ("all", "v") else []):
                        psl = PS2[:, mt * 64:(mt + 1) * 64]
                        w = ([ps2_free, slot_free[4], slot_free[5],
                              ("SV", ln1_applied[ch])] if mt == 0 else [])
                        P.emit("tensor",
                               lambda e, psl=psl, mt=mt, ch=ch, off=off: e.matmul(
                                   psl,
                                   XLN[off:off + 64,
                                       ch * T + mt * 128: ch * T + (mt + 1) * 128],
                                   WQ[off:off + 64, 128:192],
                                   start=(mt in (0, 8)), stop=False,
                                   skip_group_check=True),
                               waits=w)
                        tv_last = P.emit("tensor",
                               lambda e, psl=psl: e.matmul(
                                   psl, ONES[0:1, 0:128], BQKVR[0:1, 128:192],
                                   start=False, stop=True, skip_group_check=True),
                               waits=([("GQ", gq_bqr), ("SQ", sq_ones)] if mt == 0 else []),
                               inc=("ST" if mt == NMT - 1 else None)) or tv_last
                    if DBG["qkv_parts"] in ("all", "v"):
                        sv_va = P.emit("vector",
                                   lambda e: e.tensor_copy(
                                       VA.ap().rearrange("p (g n) -> p g n", n=65)[:, :, 0:64],
                                       PS2.ap().rearrange("p (g n) -> p g n", n=64)),
                                   waits=[("ST", tv_last), ("SV", sv_vones)], inc="SV")
                    else:
                        sv_va = sv_vones
                    ps2_free = ("SV", max(sv_va, avt_done))
                    if h == 0:
                        _DBGREF["QT"], _DBGREF["KT"], _DBGREF["VA"] = QT, KT, VA
                        _ckpt("qkv")

                    rot = 0
                    for j in range(NTT):
                        jp = jglob[0] % 2
                        jglob[0] += 1
                        PT = PTA if jp == 0 else PTB
                        for kc in range(4 * j + 4):
                            slot = rot % 6
                            rot += 1
                            psv = (PS4[:, slot * 512:(slot + 1) * 512] if slot < 4
                                   else PS2[:, (slot - 4) * 512:(slot - 3) * 512])
                            w = [slot_free[slot]]
                            if kc == 0 and j == 0:
                                w += [("SS", ev_q[-1]), ("SS", ev_k[-1])]
                            ts = P.emit("tensor",
                                        lambda e, psv=psv, kc=kc, j=j: e.matmul(
                                            psv,
                                            KT[0:64, kc * 128:(kc + 1) * 128],
                                            QT[0:64, j * 512:(j + 1) * 512],
                                            start=True, stop=True),
                                        waits=w, inc="ST")
                            pts = PT[:, kc * 512:(kc + 1) * 512]
                            ptguard = [("ST", st_av_prev[jp])] if kc == 0 else []
                            if kc < 4 * j:
                                ss_exp_last = P.emit("scalar",
                                        lambda e, pts=pts, psv=psv: e.activation(
                                            pts, psv, AF.Exp, scale=0.125),
                                        waits=[("ST", ts)] + ptguard, inc="SS")
                                slot_free[slot] = ("SS", ss_exp_last)
                                sc_hist.append(("SS", ss_exp_last))
                            else:
                                d = kc - 4 * j
                                smsl = SMASK[:, (kc % 2) * 512:(kc % 2) * 512 + 512]
                                vm = P.emit("vector",
                                        lambda e, smsl=smsl, psv=psv, d=d: e.tensor_mul(
                                            smsl, psv, MASKS[:, d * 512:(d + 1) * 512]),
                                        waits=[("ST", ts), ("SQ", sq_masks),
                                               ("SS", ss_exp_last)], inc="SV")
                                ss_exp_last = P.emit("scalar",
                                        lambda e, pts=pts, smsl=smsl: e.activation(
                                            pts, smsl, AF.Exp, scale=0.125),
                                        waits=[("SV", vm)] + ptguard, inc="SS")
                                slot_free[slot] = ("SV", vm)
                        st_av = 0
                        for mt in range(4):
                            for kc in range(16):
                                if kc < 4 * j + 4:
                                    lhs = PT[:, kc * 512 + mt * 128: kc * 512 + (mt + 1) * 128]
                                else:
                                    lhs = ONES[:, 0:128]
                                w = []
                                if mt == 0 and kc == 0:
                                    w = [("SS", ss_exp_last), ("SV", sv_va), ps1_free]
                                st_av = P.emit("tensor",
                                        lambda e, lhs=lhs, mt=mt, kc=kc: e.matmul(
                                            PS1[:, mt * 65:(mt + 1) * 65],
                                            lhs, VA[:, kc * 65:(kc + 1) * 65],
                                            start=(mt == 0 and kc == 0), stop=(kc == 15),
                                            skip_group_check=True),
                                        waits=w,
                                        inc=("ST" if (mt == 3 and kc == 15) else None)) or st_av
                        st_av_prev[jp] = st_av
                        sv_norm_last = 0
                        for mt in range(4):
                            rz_w = [("ST", st_av)] if mt == 0 else [("SV", sv_norm_last)]
                            rz = P.emit("vector",
                                    lambda e, mt=mt: e.reciprocal(
                                        RZ[:, mt:mt + 1],
                                        PS1[:, mt * 65 + 64: mt * 65 + 65]),
                                    waits=rz_w, inc="SV")
                            vn = P.emit("vector",
                                    lambda e, mt=mt: e.tensor_scalar(
                                        AVN[:, mt * 64:(mt + 1) * 64],
                                        PS1[:, mt * 65: mt * 65 + 64],
                                        RZ[:, mt:mt + 1], None, ALU.mult),
                                    waits=[("SV", rz), avn_free[mt]], inc="SV")
                            sv_norm_last = vn
                            tp = P.emit("tensor",
                                    lambda e, mt=mt, off=off: e.transpose(
                                        PTR[off:off + 64, (mt % 2) * 128:(mt % 2) * 128 + 128],
                                        AVN[:, mt * 64:(mt + 1) * 64],
                                        IDB[:],
                                        tile_position=(0, off)),
                                    waits=[("SV", vn), ("SQ", sq_idb), ptr_free[mt % 2]],
                                    inc="ST")
                            avn_free[mt] = ("ST", tp)
                            dst = AVT[off:off + 64,
                                      ch * T + j * 512 + mt * 128:
                                      ch * T + j * 512 + (mt + 1) * 128]
                            cp = P.emit("vector",
                                    lambda e, dst=dst, mt=mt, off=off: e.tensor_copy(
                                        dst,
                                        PTR[off:off + 64, (mt % 2) * 128:(mt % 2) * 128 + 128]),
                                    waits=[("ST", tp)], inc="SV")
                            ptr_free[mt % 2] = ("SV", cp)
                            avt_done = cp
                        ps1_free = ("SV", avt_done)
                        if h == 0 and j == 0:
                            _DBGREF["PT"] = PT
                            _DBGREF["AVN"] = AVN
                            _ckpt("j0")

                _DBGREF["AVT"] = AVT
                _ckpt("attnav")

                rot = 0
                vacc_hist = []
                for dtl in range(NCH):
                    for tt in range(NTT):
                        slot = rot % 4
                        rot += 1
                        psv = PS4[:, slot * 512:(slot + 1) * 512]
                        for kc in range(NCH):
                            w = []
                            if kc == 0:
                                w = [slot_free[slot], ("SV", avt_done), ("GQ", gq_wrf)]
                                if len(sc_hist) >= 2:
                                    w.append(sc_hist[-2])
                                if len(vacc_hist) >= 2:
                                    w.append(("SV", vacc_hist[-2]))
                            P.emit("tensor",
                                   lambda e, psv=psv, kc=kc, dtl=dtl, tt=tt: e.matmul(
                                       psv,
                                       WRF[:, kc * D + dtl * 128: kc * D + (dtl + 1) * 128],
                                       AVT[:, kc * T + tt * 512: kc * T + (tt + 1) * 512],
                                       start=(kc == 0), stop=False),
                                   waits=w)
                        twr = P.emit("tensor",
                               lambda e, psv=psv, dtl=dtl: e.matmul(
                                   psv, BRR[0:1, dtl * 128:(dtl + 1) * 128],
                                   ONES[0:1, 0:512], start=False, stop=True),
                               waits=[("GQ", gq_brr)], inc="ST")
                        vacc = P.emit("vector",
                               lambda e, psv=psv, dtl=dtl, tt=tt: e.tensor_add(
                                   xres(dtl)[:, tt * 512:(tt + 1) * 512],
                                   xres(dtl)[:, tt * 512:(tt + 1) * 512], psv),
                               waits=[("ST", twr)], inc="SV")
                        slot_free[slot] = ("SV", vacc)
                        vacc_hist.append(vacc)

            barrier()
            _ckpt("attn")

            with ExitStack() as fctx:
                FE = fctx.enter_context
                W1 = FE(nc.sbuf_tensor("W1", [128, NCH * DFF], BF16))
                W2 = FE(nc.sbuf_tensor("W2", [128, 16 * D], BF16))
                WB = FE(nc.sbuf_tensor("WB", [128, T], BF16))
                WROW = FE(nc.sbuf_tensor("WROW", [1, T], BF16))
                B2C = FE(nc.sbuf_tensor("B2C", [128, 16], F32))
                B1C = FE(nc.sbuf_tensor("B1C", [128, 32], F32))
                RWW = FE(nc.sbuf_tensor("RWW", [128, NCH * NE], BF16))
                RBR = FE(nc.sbuf_tensor("RBR", [1, NE], BF16))

                gq_rww = P.emit("gpsimd", lambda e: e.dma_start(out=RWW[:], in_=rww_rl),
                                inc="GQ", inc_n=16)
                gq_rbr = P.emit("gpsimd", lambda e: e.dma_start(out=RBR[:], in_=rwb_row),
                                inc="GQ", inc_n=16)
                gq_w1 = [P.emit("gpsimd", lambda e: e.dma_start(out=W1[:], in_=w1cat[0]),
                                inc="W1S", inc_n=16)]
                gq_w2 = [P.emit("gpsimd", lambda e: e.dma_start(out=W2[:], in_=w2cat[0]),
                                inc="W2S", inc_n=16)]
                gq_b1 = [P.emit("gpsimd",
                                lambda e: e.dma_start(out=B1C[:, 0:16], in_=b1col[0]),
                                inc="B1S", inc_n=16)]
                gq_b2 = [P.emit("gpsimd",
                                lambda e: e.dma_start(out=B2C[:, 0:8], in_=b2col_t[0]),
                                inc="B2S_", inc_n=16)]

                ln2_applied = emit_ln([[] for _ in range(NCH)])
                ln2_done = ln2_applied[-1]
                _ckpt("ln2")

                with ExitStack() as rctx:
                    RE_ = rctx.enter_context
                    RLOG = RE_(nc.sbuf_tensor("RLOG", [8, 512], F32))
                    RT = RE_(nc.sbuf_tensor("RT", [128, 128], F32))
                    REX = RE_(nc.sbuf_tensor("REX", [128, 128], F32))
                    RS4 = RE_(nc.sbuf_tensor("RS4", [128, 64], F32))
                    RS2 = RE_(nc.sbuf_tensor("RS2", [128, 32], F32))
                    RS1 = RE_(nc.sbuf_tensor("RS1", [128, 16], F32))
                    RCPS = RE_(nc.sbuf_tensor("RCPS", [128, 16], F32))
                    RCPR = RE_(nc.sbuf_tensor("RCPR", [128, 128], F32))
                    MRA = RE_(nc.sbuf_tensor("MRA", [128, 128], F32))
                    PMA = RE_(nc.sbuf_tensor("PMA", [128, 128], F32))
                    M2R = RE_(nc.sbuf_tensor("M2R", [128, 128], F32))
                    MSKA = RE_(nc.sbuf_tensor("MSKA", [128, 128], F32))
                    WTA = RE_(nc.sbuf_tensor("WTA", [128, 128], F32))
                    IDF = RE_(nc.sbuf_tensor("IDF", [128, 128], F32))
                    PR = RE_(nc.psum_tensor("PR", [8, 512], F32))
                    PTRK = RE_(nc.psum_tensor("PTRK", [128, 8], F32))
                    PTW = RE_(nc.psum_tensor("PTW", [8, 128], F32))

                    sq_idf = P.emit("sync", lambda e: e.dma_start(out=IDF[:], in_=c_idf),
                                    waits=[("SV", ln2_done), ("ST", C["ST"]),
                                           ("SS", C["SS"])],
                                    inc="SQ", inc_n=16)
                    rt_last = 0
                    pr_free = ("SV", 0)
                    ptrk_free = ("SV", 0)
                    for tt in range(NTT):
                        for kc in range(NCH):
                            w = []
                            if kc == 0:
                                w = [("GQ", gq_rww), ("SV", ln2_applied[kc]), pr_free]
                            elif tt == 0:
                                w = [("SV", ln2_applied[kc])]
                            P.emit("tensor",
                                   lambda e, kc=kc, tt=tt: e.matmul(
                                       PR[:], RWW[:, kc * NE:(kc + 1) * NE],
                                       xln(kc)[:, tt * 512:(tt + 1) * 512],
                                       start=(kc == 0), stop=False),
                                   waits=w)
                        tpr = P.emit("tensor",
                               lambda e: e.matmul(PR[:], RBR[0:1, 0:NE], ONES[0:1, 0:512],
                                                  start=False, stop=True),
                               waits=[("GQ", gq_rbr)], inc="ST")
                        vlog = P.emit("vector",
                               lambda e: e.tensor_copy(RLOG[:], PR[:]),
                               waits=[("ST", tpr)], inc="SV")
                        pr_free = ("SV", vlog)
                        for b in range(4):
                            tk = P.emit("tensor",
                                   lambda e, b=b: e.transpose(
                                       PTRK[:], RLOG[:, b * 128:(b + 1) * 128],
                                       IDF[0:8, 0:8]),
                                   waits=[("SV", vlog), ("SQ", sq_idf), ptrk_free],
                                   inc="ST")
                            g = tt * 4 + b
                            rt_last = P.emit("vector",
                                   lambda e, g=g: e.tensor_copy(
                                       RT[:, g * 8:(g + 1) * 8], PTRK[:]),
                                   waits=[("ST", tk)], inc="SV")
                            ptrk_free = ("SV", rt_last)

                    def r3(t_, n=8):
                        return t_.ap().rearrange("p (g n) -> p g n", n=n)

                    sx = P.emit("scalar",
                                lambda e: e.activation(REX[:], RT[:], AF.Exp),
                                waits=[("SV", rt_last)], inc="SS")
                    c0 = P.emit("vector", lambda e: e.tensor_add(
                            r3(RS4, 4), r3(REX)[:, :, 0:4], r3(REX)[:, :, 4:8]),
                            waits=[("SS", sx)], inc="SV")
                    c1 = P.emit("vector", lambda e: e.tensor_add(
                            r3(RS2, 2), r3(RS4, 4)[:, :, 0:2], r3(RS4, 4)[:, :, 2:4]),
                            waits=[("SV", c0)], inc="SV")
                    c2 = P.emit("vector", lambda e: e.tensor_add(
                            r3(RS1, 1), r3(RS2, 2)[:, :, 0:1], r3(RS2, 2)[:, :, 1:2]),
                            waits=[("SV", c1)], inc="SV")
                    c3 = P.emit("vector", lambda e: e.reciprocal(RCPS[:], RS1[:]),
                                waits=[("SV", c2)], inc="SV")
                    c4 = P.emit("vector", lambda e: e.tensor_copy(
                            r3(RCPR)[:, :, 0:1], r3(RCPS, 1)),
                            waits=[("SV", c3)], inc="SV")
                    c5 = P.emit("vector", lambda e: e.tensor_copy(
                            r3(RCPR)[:, :, 1:2], r3(RCPR)[:, :, 0:1]),
                            waits=[("SV", c4)], inc="SV")
                    c6 = P.emit("vector", lambda e: e.tensor_copy(
                            r3(RCPR)[:, :, 2:4], r3(RCPR)[:, :, 0:2]),
                            waits=[("SV", c5)], inc="SV")
                    c7 = P.emit("vector", lambda e: e.tensor_copy(
                            r3(RCPR)[:, :, 4:8], r3(RCPR)[:, :, 0:4]),
                            waits=[("SV", c6)], inc="SV")
                    m0 = P.emit("vector", lambda e: e.tensor_max(
                            r3(RS4, 4), r3(REX)[:, :, 0:4], r3(REX)[:, :, 4:8]),
                            waits=[("SV", c1)], inc="SV")
                    m1 = P.emit("vector", lambda e: e.tensor_max(
                            r3(RS2, 2), r3(RS4, 4)[:, :, 0:2], r3(RS4, 4)[:, :, 2:4]),
                            waits=[("SV", m0)], inc="SV")
                    m2 = P.emit("vector", lambda e: e.tensor_max(
                            r3(RS1, 1), r3(RS2, 2)[:, :, 0:1], r3(RS2, 2)[:, :, 1:2]),
                            waits=[("SV", m1), ("SV", c3)], inc="SV")
                    mfill = P.emit("vector", lambda e: e.memset(MRA[:], -1e30), inc="SV")
                    mput = P.emit("vector", lambda e: e.tensor_copy(
                            r3(MRA)[:, :, 0:1], r3(RS1, 1)),
                            waits=[("SV", m2), ("SV", mfill)], inc="SV")
                    mr_last = mput
                    for g in range(16):
                        mr_last = P.emit("vector",
                                lambda e, g=g: e.match_replace(
                                    PMA[:, g * 8:(g + 1) * 8], MRA[:, g * 8:(g + 1) * 8],
                                    REX[:, g * 8:(g + 1) * 8], -1e30),
                                waits=([("SV", mput)] if g == 0 else []), inc="SV")
                    n0 = P.emit("vector", lambda e: e.tensor_max(
                            r3(RS4, 4), r3(PMA)[:, :, 0:4], r3(PMA)[:, :, 4:8]),
                            waits=[("SV", mr_last)], inc="SV")
                    n1 = P.emit("vector", lambda e: e.tensor_max(
                            r3(RS2, 2), r3(RS4, 4)[:, :, 0:2], r3(RS4, 4)[:, :, 2:4]),
                            waits=[("SV", n0)], inc="SV")
                    n2 = P.emit("vector", lambda e: e.tensor_max(
                            r3(RS1, 1), r3(RS2, 2)[:, :, 0:1], r3(RS2, 2)[:, :, 1:2]),
                            waits=[("SV", n1)], inc="SV")
                    d0 = P.emit("vector", lambda e: e.tensor_copy(
                            r3(M2R)[:, :, 0:1], r3(RS1, 1)),
                            waits=[("SV", n2)], inc="SV")
                    d1 = P.emit("vector", lambda e: e.tensor_copy(
                            r3(M2R)[:, :, 1:2], r3(M2R)[:, :, 0:1]),
                            waits=[("SV", d0)], inc="SV")
                    d2 = P.emit("vector", lambda e: e.tensor_copy(
                            r3(M2R)[:, :, 2:4], r3(M2R)[:, :, 0:2]),
                            waits=[("SV", d1)], inc="SV")
                    d3 = P.emit("vector", lambda e: e.tensor_copy(
                            r3(M2R)[:, :, 4:8], r3(M2R)[:, :, 0:4]),
                            waits=[("SV", d2)], inc="SV")
                    e0 = P.emit("vector", lambda e: e.tensor_tensor(
                            MSKA[:], REX[:], M2R[:], ALU.is_ge),
                            waits=[("SV", d3)], inc="SV")
                    e1 = P.emit("vector", lambda e: e.tensor_mul(WTA[:], REX[:], MSKA[:]),
                            waits=[("SV", e0)], inc="SV")
                    e2 = P.emit("vector", lambda e: e.tensor_mul(WTA[:], WTA[:], RCPR[:]),
                            waits=[("SV", e1), ("SV", c7)], inc="SV")
                    ptw_free = ("SV", 0)
                    wsb_done = 0
                    for g in range(16):
                        tw = P.emit("tensor",
                                lambda e, g=g: e.transpose(
                                    PTW[:], WTA[:, g * 8:(g + 1) * 8], IDF[:]),
                                waits=[("SV", e2), ptw_free], inc="ST")
                        wsb_done = P.emit("vector",
                                lambda e, g=g: e.tensor_copy(
                                    WSB[:, g * 128:(g + 1) * 128], PTW[:]),
                                waits=[("ST", tw)], inc="SV")
                        ptw_free = ("SV", wsb_done)
                    router_done_sv = wsb_done
                    router_done_st = C["ST"]
                    _ckpt("router")

                H1G = FE(nc.sbuf_tensor("H1G", [128, 2 * 16 * 512], BF16))
                PM1 = FE(nc.psum_tensor("PM1", [128, 2048], F32))
                PM2 = FE(nc.psum_tensor("PM2", [128, 2048], F32))

                sg_hist = []
                vx_hist = []
                pm1_free = {r: ("SV", 0) for r in range(4)}
                pm2_free = {r: ("SV", 0) for r in range(4)}
                h1g_free = {0: ("ST", 0), 1: ("ST", 0)}
                wrow_free = ("ST", 0)
                wb_free = ("SV", 0)
                w1_free = ("ST", 0)
                w2_free = ("ST", 0)
                b1_free = {0: ("SS", 0), 1: ("SS", 0)}
                b2_free = ("ST", 0)
                state = {"rot1": 0, "rot2": 0, "xadd": 0}

                def emit_mm1(e_i, tt, wb_ready, mm1_first_w):
                    buf = tt % 2
                    last_marker = None
                    for ht in range(16):
                        slot = state["rot1"] % 4
                        state["rot1"] += 1
                        psv = PM1[:, slot * 512:(slot + 1) * 512]
                        tmm = 0
                        for kc in range(NCH):
                            w = []
                            if kc == 0:
                                w = [pm1_free[slot]]
                                if ht == 0:
                                    w = w + mm1_first_w
                            r = P.emit("tensor",
                                   lambda e, psv=psv, kc=kc, ht=ht, tt=tt: e.matmul(
                                       psv,
                                       W1[:, kc * DFF + ht * 128: kc * DFF + (ht + 1) * 128],
                                       xln(kc)[:, tt * 512:(tt + 1) * 512],
                                       start=(kc == 0), stop=(kc == NCH - 1)),
                                   waits=w,
                                   inc=("ST" if kc == NCH - 1 else None))
                            if r is not None:
                                tmm = r
                        h1sl = H1G[:, buf * 8192 + ht * 512: buf * 8192 + (ht + 1) * 512]
                        gw = [("ST", tmm), ("B1S", gq_b1[e_i])]
                        if ht == 0:
                            gw.append(h1g_free[buf])
                        if e_i == 0 and tt == 0 and ht == 0:
                            gw += [("SV", router_done_sv), ("ST", router_done_st)]
                        if DBG["evac"] == "gelu":
                            sg = P.emit("scalar",
                                   lambda e, h1sl=h1sl, psv=psv, e_i=e_i, ht=ht: e.activation(
                                       h1sl, psv, AF.Gelu,
                                       bias=B1C[:, (e_i % 2) * 16 + ht:
                                                (e_i % 2) * 16 + ht + 1]),
                                   waits=gw, inc="SS")
                        elif DBG["evac"] == "gelunb":
                            sg = P.emit("scalar",
                                   lambda e, h1sl=h1sl, psv=psv: e.activation(
                                       h1sl, psv, AF.Gelu),
                                   waits=gw, inc="SS")
                        elif DBG["evac"].startswith("gelun_"):
                            lim = int(DBG["evac"].split("_")[1])
                            if e_i == 0 and (tt * 16 + ht) < lim:
                                sg = P.emit("scalar",
                                       lambda e, h1sl=h1sl, psv=psv: e.activation(
                                           h1sl, psv, AF.Gelu),
                                       waits=gw, inc="SS")
                            else:
                                sg = P.emit("scalar",
                                       lambda e, h1sl=h1sl, psv=psv: e.copy(h1sl, psv),
                                       waits=gw, inc="SS")
                        elif DBG["evac"] == "gelu1":
                            if e_i == 0 and tt == 0 and ht == 0:
                                sg = P.emit("scalar",
                                       lambda e, h1sl=h1sl, psv=psv: e.activation(
                                           h1sl, psv, AF.Gelu),
                                       waits=gw, inc="SS")
                            else:
                                sg = P.emit("scalar",
                                       lambda e, h1sl=h1sl, psv=psv: e.copy(h1sl, psv),
                                       waits=gw, inc="SS")
                        elif DBG["evac"] == "tinygelu":
                            sg = P.emit("scalar",
                                   lambda e, h1sl=h1sl, psv=psv: e.copy(h1sl, psv),
                                   waits=gw, inc="SS")
                            if e_i == 0 and tt == 0 and ht == 0:
                                sg = P.emit("scalar",
                                       lambda e: e.activation(
                                           DUMMY[0:1, 0:2], DUMMY[0:1, 4:6], AF.Gelu),
                                       inc="SS")
                        else:
                            sg = P.emit("scalar",
                                   lambda e, h1sl=h1sl, psv=psv: e.copy(h1sl, psv),
                                   waits=gw, inc="SS")
                        pm1_free[slot] = ("SS", sg)
                        sg_hist.append(sg)
                        last_marker = ("SS", sg)
                        if e_i > 0:
                            vw = P.emit("vector",
                                   lambda e, h1sl=h1sl, tt=tt: e.tensor_mul(
                                       h1sl, h1sl, WB[:, tt * 512:(tt + 1) * 512]),
                                   waits=[("SS", sg), wb_ready], inc="SV")
                            last_marker = ("SV", vw)
                    return last_marker

                def emit_mm2(e_i, tt, h1_ready):
                    buf = tt % 2
                    last_st = 0
                    for dtl in range(NCH):
                        slot = state["rot2"] % 4
                        state["rot2"] += 1
                        psv = PM2[:, slot * 512:(slot + 1) * 512]
                        tr = 0
                        for kc in range(16):
                            w = []
                            if kc == 0:
                                w = [pm2_free[slot], h1_ready]
                                if dtl == 0 and tt == 0:
                                    w.append(("W2S", gq_w2[e_i]))
                            r = P.emit("tensor",
                                   lambda e, psv=psv, kc=kc, dtl=dtl, buf=buf: e.matmul(
                                       psv,
                                       W2[:, kc * D + dtl * 128: kc * D + (dtl + 1) * 128],
                                       H1G[:, buf * 8192 + kc * 512:
                                           buf * 8192 + (kc + 1) * 512],
                                       start=(kc == 0), stop=(kc == 15)),
                                   waits=w,
                                   inc=("ST" if kc == 15 else None))
                            if r is not None:
                                tr = r
                        last_st = tr
                        wsrc = (WB[:, tt * 512:(tt + 1) * 512] if e_i > 0
                                else ONES[:, 0:512])
                        b2ap = B2C[:, (e_i % 2) * 8 + dtl:(e_i % 2) * 8 + dtl + 1]
                        vx = P.emit("vector",
                               lambda e, psv=psv, dtl=dtl, tt=tt: e.tensor_add(
                                   xres(dtl)[:, tt * 512:(tt + 1) * 512],
                                   xres(dtl)[:, tt * 512:(tt + 1) * 512], psv),
                               waits=[("ST", tr)], inc="SV")
                        vb2 = P.emit("vector",
                               lambda e, wsrc=wsrc, b2ap=b2ap, dtl=dtl, tt=tt:
                                   e.scalar_tensor_tensor(
                                       xres(dtl)[:, tt * 512:(tt + 1) * 512],
                                       wsrc, b2ap,
                                       xres(dtl)[:, tt * 512:(tt + 1) * 512],
                                       ALU.mult, ALU.add),
                               waits=[("SV", vx), ("B2S_", gq_b2[e_i])], inc="SV")
                        pm2_free[slot] = ("SV", vx)
                        vx_hist.append(vx)
                        state["xadd"] = vb2
                    return ("ST", last_st)

                for e_i in range(min(NEXP, DBG["nexp"])):
                    if e_i > 0:
                        gq_w1.append(P.emit("gpsimd",
                                lambda e, e_i=e_i: e.dma_start(out=W1[:], in_=w1cat[e_i]),
                                waits=[w1_free], inc="W1S", inc_n=16))
                        gq_wrow = P.emit("gpsimd",
                                lambda e, e_i=e_i: e.dma_start(
                                    out=WROW[:], in_=WSB[e_i - 1:e_i, :]),
                                waits=[("SV", router_done_sv), wrow_free],
                                inc="WRS", inc_n=16)
                        gq_b1.append(P.emit("gpsimd",
                                lambda e, e_i=e_i: e.dma_start(
                                    out=B1C[:, (e_i % 2) * 16:(e_i % 2) * 16 + 16],
                                    in_=b1col[e_i]),
                                waits=[b1_free[e_i % 2]], inc="B1S", inc_n=16))
                        wb_evac = 0
                        for r in range(4):
                            tb_ = P.emit("tensor",
                                    lambda e, r=r: e.matmul(
                                        PM1[:, r * 512:(r + 1) * 512],
                                        ONES[0:1, 0:128],
                                        WROW[0:1, r * 512:(r + 1) * 512],
                                        start=True, stop=True),
                                    waits=[("WRS", gq_wrow), pm1_free[r]],
                                    inc="ST")
                            wb_evac = P.emit("vector",
                                    lambda e, r=r: e.tensor_copy(
                                        WB[:, r * 512:(r + 1) * 512],
                                        PM1[:, r * 512:(r + 1) * 512]),
                                    waits=[("ST", tb_), wb_free], inc="SV")
                            pm1_free[r] = ("SV", wb_evac)
                        wrow_free = ("ST", tb_)
                        wb_ready = ("SV", wb_evac)
                    else:
                        wb_ready = ("SV", 0)

                    mm1_first_w = [("W1S", gq_w1[e_i])]
                    if e_i == 0:
                        mm1_first_w += [("SV", ln2_done),
                                        ("SV", router_done_sv),
                                        ("ST", router_done_st),
                                        ("SS", C["SS"])]

                    h1_ready = {}
                    h1_ready[0] = emit_mm1(e_i, 0, wb_ready, mm1_first_w)
                    if e_i > 0:
                        gq_b2.append(P.emit("gpsimd",
                                lambda e, e_i=e_i: e.dma_start(
                                    out=B2C[:, (e_i % 2) * 8:(e_i % 2) * 8 + 8],
                                    in_=b2col_t[e_i]),
                                waits=[b2_free], inc="B2S_", inc_n=16))
                        gq_w2.append(P.emit("gpsimd",
                                lambda e, e_i=e_i: e.dma_start(out=W2[:], in_=w2cat[e_i]),
                                waits=[w2_free], inc="W2S", inc_n=16))
                    if DBG["stop"] == "mm1" and e_i == 0:
                        h1_ready[1] = emit_mm1(e_i, 1, wb_ready, mm1_first_w)
                        _DBGREF["H1G"] = H1G
                        _ckpt("mm1")
                    h1_ready[1] = emit_mm1(e_i, 1, wb_ready, mm1_first_w)
                    h1g_free[0] = emit_mm2(e_i, 0, h1_ready[0])
                    h1_ready[2] = emit_mm1(e_i, 2, wb_ready, mm1_first_w)
                    h1g_free[1] = emit_mm2(e_i, 1, h1_ready[1])
                    h1_ready[3] = emit_mm1(e_i, 3, wb_ready, mm1_first_w)
                    w1_free = ("ST", C["ST"])
                    h1g_free[0] = emit_mm2(e_i, 2, h1_ready[2])
                    h1g_free[1] = emit_mm2(e_i, 3, h1_ready[3])

                    w2_free = h1g_free[1]
                    wb_free = ("SV", state["xadd"])
                    b1_free[e_i % 2] = ("SS", C["SS"])
                    b2_free = ("SV", state["xadd"])


        except _PhaseDone:
            DBG["_aborted"] = True

        barrier()
        if DBG["dump"] == "h1g":
            H1G_ = _DBGREF["H1G"]
            gq_out = 0
            dw = [("SS", C["SS"]), ("SV", C["SV"]), ("ST", C["ST"])]
            for c in range(NCH):
                gq_out = P.emit("gpsimd",
                       lambda e, c=c: e.dma_start(
                           out=ot[c * 128:(c + 1) * 128, :],
                           in_=H1G_[:, c * 2048:(c + 1) * 2048]),
                       waits=(dw if c == 0 else []),
                       inc="GQ", inc_n=16)
            P.emit("gpsimd", lambda e: e.nop(), waits=[("GQ", gq_out)])
        elif DBG["dump"] == "qkv":
            QT_, KT_, VA_ = _DBGREF["QT"], _DBGREF["KT"], _DBGREF["VA"]
            gq_out = P.emit("gpsimd", lambda e: e.dma_start(
                out=ot[0:64, :], in_=QT_[0:64, :]), inc="GQ", inc_n=16)
            gq_out = P.emit("gpsimd", lambda e: e.dma_start(
                out=ot[64:128, :], in_=KT_[0:64, :]), inc="GQ", inc_n=16)
            gq_out = P.emit("gpsimd", lambda e: e.dma_start(
                out=ot[128:256, 0:1040], in_=VA_[:, :]), inc="GQ", inc_n=16)
            P.emit("gpsimd", lambda e: e.nop(), waits=[("GQ", gq_out)])
        elif DBG["dump"] == "j0":
            PT_, AVN_ = _DBGREF["PT"], _DBGREF["AVN"]
            gq_out = 0
            for b in range(4):
                gq_out = P.emit("gpsimd", lambda e, b=b: e.dma_start(
                    out=ot[b * 128:(b + 1) * 128, :],
                    in_=PT_[:, b * 2048:(b + 1) * 2048]), inc="GQ", inc_n=16)
            gq_out = P.emit("gpsimd", lambda e: e.dma_start(
                out=ot[512:640, 0:256], in_=AVN_[:, :]), inc="GQ", inc_n=16)
            P.emit("gpsimd", lambda e: e.nop(), waits=[("GQ", gq_out)])
        elif DBG["dump"] == "avt":
            AVT_ = _DBGREF["AVT"]
            gq_out = 0
            for c in range(NCH):
                gq_out = P.emit("gpsimd",
                       lambda e, c=c: e.dma_start(
                           out=ot[c * 128:(c + 1) * 128, :],
                           in_=AVT_[:, c * T:(c + 1) * T]),
                       inc="GQ", inc_n=16)
            P.emit("gpsimd", lambda e: e.nop(), waits=[("GQ", gq_out)])
        elif DBG["dump"] == "xln":
            gq_out = 0
            for c in range(NCH):
                gq_out = P.emit("gpsimd",
                       lambda e, c=c: e.dma_start(out=ot[c * 128:(c + 1) * 128, :],
                                                  in_=xln(c)),
                       inc="GQ", inc_n=16)
            P.emit("gpsimd", lambda e: e.nop(), waits=[("GQ", gq_out)])
        else:
            sq_out = 0
            for c in range(NCH):
                sq_out = P.emit("sync",
                       lambda e, c=c: e.dma_start(out=ot[c * 128:(c + 1) * 128, :],
                                                  in_=xres(c)),
                       inc="SQ", inc_n=16)
            P.emit("sync", lambda e: e.nop(), waits=[("SQ", sq_out)])

        def _preload_both(e):
            try:
                from concourse.hw_specs import get_activation_tables
                for idx, funcs in enumerate(
                        get_activation_tables(nc.m.arch).values()):
                    if AF.Exp in funcs and AF.Ln in funcs:
                        e.add_instruction(mybir.InstLoadActFuncSet(
                            name=nc.get_next_instruction_name(),
                            act_func_set_id=idx, ins=[], outs=[]))
                        return
            except Exception:
                pass

        with nc.Block() as block:
            @block.tensor
            def _(e):
                for f in P.ops["tensor"]:
                    f(e)

            @block.vector
            def _(e):
                for f in P.ops["vector"]:
                    f(e)

            @block.scalar
            def _(e):
                for f in P.ops["scalar"]:
                    f(e)

            @block.sync
            def _(e):
                for f in P.ops["sync"]:
                    f(e)

            @block.gpsimd
            def _(e):
                for f in P.ops["gpsimd"]:
                    f(e)

        if DBG.get("_aborted"):
            ctx.pop_all()
            DBG["_aborted"] = False

    return nc


_CACHE = {}


def _host_layout(inputs):
    f32 = np.float32
    g = lambda k: np.asarray(inputs[k], dtype=f32)
    x = g("L0_attn_x")
    wqkv = g("L0_attn_Wqkv")
    bqkv = g("L0_attn_bqkv")
    wr = g("L0_attn_Wr")
    br = g("L0_attn_br")
    rww = g("L0_ffn_router_w")
    rwb = g("L0_ffn_router_b")
    sw1 = g("L0_ffn_sw1")
    sb1 = g("L0_ffn_sb1")
    sw2 = g("L0_ffn_sw2")
    sb2 = g("L0_ffn_sb2")
    rw1 = g("L0_ffn_rw1")
    rb1 = g("L0_ffn_rb1")
    rw2 = g("L0_ffn_rw2")
    rb2 = g("L0_ffn_rb2")

    cc = np.ascontiguousarray
    com = {}
    com["wqkv2"] = cc(np.concatenate([wqkv, wqkv], axis=0))
    com["bqkv_row"] = cc(bqkv.reshape(1, 192))
    com["bqkv_col"] = cc(np.stack([bqkv[0:64], bqkv[64:128]], axis=1))
    com["wr_rl"] = cc(wr.reshape(8, 128, D).transpose(1, 0, 2).reshape(128, 8 * D))
    com["br_row"] = cc(br.reshape(1, D))
    com["rww_rl"] = cc(rww.reshape(8, 128, NE).transpose(1, 0, 2).reshape(128, 8 * NE))
    com["rwb_row"] = cc(rwb.reshape(1, NE))
    w1 = np.concatenate([sw1[None], rw1], axis=0)
    w2 = np.concatenate([sw2[None], rw2], axis=0)
    com["w1cat"] = cc(w1.reshape(NEXP, 8, 128, DFF).transpose(0, 2, 1, 3)
                      .reshape(NEXP, 128, 8 * DFF))
    com["w2cat"] = cc(w2.reshape(NEXP, 16, 128, D).transpose(0, 2, 1, 3)
                      .reshape(NEXP, 128, 16 * D))
    b1 = np.concatenate([sb1[None], rb1], axis=0)
    com["b1col"] = cc(b1.reshape(NEXP, 16, 128).transpose(0, 2, 1))
    com["b2row"] = cc(np.concatenate([sb2[None], rb2], axis=0)
                      .reshape(NEXP, 1, D))
    b2all = np.concatenate([sb2[None], rb2], axis=0)
    com["b2col_t"] = cc(b2all.reshape(NEXP, 8, 128).transpose(0, 2, 1))

    bf = ml_dtypes.bfloat16
    com["c_ones"] = np.ones((128, 512), bf)
    com["c_idb"] = np.eye(128, dtype=bf)
    com["c_idf"] = np.eye(128, dtype=f32)
    p = np.arange(128)[:, None]
    fidx = np.arange(512)[None, :]
    m = np.zeros((128, 4 * 512), f32)
    for d in range(4):
        m[:, d * 512:(d + 1) * 512] = ((p + 128 * d) < fidx).astype(f32)
    com["c_masks"] = m.astype(bf)
    com["c_eps"] = np.full((1, 1), 1e-5, f32)
    com["c_rcp1024"] = np.full((1, 1), 1.0 / 1024.0, f32)

    in_maps = []
    for c in range(8):
        mp = dict(com)
        mp["xt"] = cc(x[c].T)
        in_maps.append(mp)
    return in_maps


def kernel(**inputs):
    in_maps = _host_layout(inputs)
    if "nc" not in _CACHE:
        _CACHE["nc"] = build_program()
    nc = _CACHE["nc"]
    res = run_bass_kernel_spmd(nc, in_maps, core_ids=list(range(8)))
    out = np.stack([np.asarray(r["ot"]).T for r in res.results], axis=0)
    return out.astype(np.float32)
